# version 15
# baseline (speedup 1.0000x reference)
import numpy as np
import concourse.bass as bass
import concourse.mybir as mybir
from concourse import library_config as _lc
_LIB = {l.name: l for l in _lc.all_libraries}
from concourse.bass_utils import run_bass_kernel_spmd
from contextlib import ExitStack

F32 = mybir.dt.float32
F16 = mybir.dt.float16
BF16 = mybir.dt.bfloat16
I32 = mybir.dt.int32
AF = mybir.ActivationFunctionType
AX = mybir.AxisListType
OP = mybir.AluOpType

H, NH, HD, T, NSEQ = 1024, 16, 64, 2048, 2
DH = 4096
NT128, NT512 = T // 128, T // 512
OC, HC, KC = H // 128, DH // 128, H // 128
NPAIR, NKT = NH // 2, T // 128
EPS = 1e-5
RM = 1536.0


class Sem:
    def __init__(self, es, nc, name):
        self.h = es.enter_context(nc.semaphore(name))
        self.v = 0

    def inc(self, ins, n=1):
        ins.then_inc(self.h, n)
        self.v += n
        return self.v

    def wait(self, eng, v=None):
        eng.wait_ge(self.h, self.v if v is None else v)


def build_nc(dbg=()):
    nc = bass.Bass("TRN2", target_bir_lowering=False)
    _nctr = [0]

    def _un(n):
        _nctr[0] += 1
        return f"{n}_{_nctr[0]}"
    x_h = nc.dram_tensor("x", [NSEQ, T, H], F32, kind="ExternalInput").ap()
    wq_h = nc.dram_tensor("wq", [H, H], F32, kind="ExternalInput").ap()
    wk_h = nc.dram_tensor("wk", [H, H], F32, kind="ExternalInput").ap()
    wv_h = nc.dram_tensor("wv", [H, H], F32, kind="ExternalInput").ap()
    wo_h = nc.dram_tensor("wo", [H, H], F32, kind="ExternalInput").ap()
    bq_h = nc.dram_tensor("bq", [128, OC], F32, kind="ExternalInput").ap()
    bv_h = nc.dram_tensor("bv", [128, OC], F32, kind="ExternalInput").ap()
    bo_h = nc.dram_tensor("bo", [H], F32, kind="ExternalInput").ap()
    cos_h = nc.dram_tensor("cosT", [HD, 1024], F32, kind="ExternalInput").ap()
    sin_h = nc.dram_tensor("sinT", [HD, 1024], F32, kind="ExternalInput").ap()
    w1t_h = nc.dram_tensor("w1t", [H, DH], F32, kind="ExternalInput").ap()
    cw_h = nc.dram_tensor("cw", [128, 3 * HC], F32, kind="ExternalInput").ap()
    cb_h = nc.dram_tensor("cb", [128, HC], F32, kind="ExternalInput").ap()
    w2t_h = nc.dram_tensor("w2t", [DH, H], F32, kind="ExternalInput").ap()
    out_h = nc.dram_tensor("out", [NSEQ, T, H], F32, kind="ExternalOutput").ap()

    wo_sp = nc.dram_tensor("wo_sp", [KC, 128, H], BF16, kind="Internal").ap()
    w1qt_sp = nc.dram_tensor("w1qt_sp", [KC, 128, DH], BF16, kind="Internal").ap()
    w2qt_sp = nc.dram_tensor("w2qt_sp", [HC, 128, H], BF16, kind="Internal").ap()
    wqkv_sp = nc.dram_tensor("wqkv_sp", [OC, 128, 3 * KC * 128], BF16,
                             kind="Internal").ap()
    xqd_sp = nc.dram_tensor("xqd_sp", [NSEQ, KC, 128, T], BF16, kind="Internal").ap()
    x2_sp = nc.dram_tensor("x2_sp", [NSEQ, T, H], F32, kind="Internal").ap()
    hs_sp = nc.dram_tensor("hs_sp", [NSEQ, HC, 128, T], BF16, kind="Internal").ap()
    c1_sp = nc.dram_tensor("c1_sp", [NSEQ, T], F32, kind="Internal").ap()
    s2_sp = nc.dram_tensor("s2_sp", [NSEQ, T], F32, kind="Internal").ap()

    V, S, TE, G, SY = nc.vector, nc.scalar, nc.tensor, nc.gpsimd, nc.sync

    es = ExitStack()
    sbuf = lambda n, shape, d=F32: es.enter_context(nc.sbuf_tensor(n, shape, d))

    dbg_outs = {}
    for name, shape, d in dbg:
        dbg_outs[name] = nc.dram_tensor(name, shape, d, kind="ExternalOutput").ap()

    with es:
        ident = sbuf("ident", [128, 128], BF16)
        identf = sbuf("identf", [128, 128], F32)
        ones1f = sbuf("ones1f", [1, 128], F32)
        cosfm = sbuf("cosfm", [128, 1024], BF16)
        sinfm = sbuf("sinfm", [128, 1024], BF16)
        bq_pre = sbuf("bq_pre", [128, OC], F32)
        bv_pre = sbuf("bv_pre", [128, OC], F32)
        bo_row = sbuf("bo_row", [1, H], BF16)
        ones_row = sbuf("ones_row", [1, 128], BF16)
        ones2 = sbuf("ones2", [128, 2], BF16)
        convw_s = sbuf("convw_s", [128, 3 * HC], F32)
        convb_s = sbuf("convb_s", [128, HC], F32)
        negM = sbuf("negM", [128, NPAIR], F32)
        negRM = sbuf("negRM", [128, 1], F32)
        posRM = sbuf("posRM", [128, 1], F32)
        wsc = sbuf("wsc", [128, 4], F32)
        stat = sbuf("stat", [128, 64], F32)
        bnbuf = sbuf("bnbuf", [128, 12], F32)
        nrm = sbuf("nrm", [128, 24], F32)
        s2s = sbuf("s2s", [128, NT128], F32)
        c1sb = sbuf("c1sb", [128, NT128], F32)
        w3_es = ExitStack()
        wo3 = w3_es.enter_context(nc.sbuf_tensor(_un("wo3"), [128, KC * H], BF16))
        qres = w3_es.enter_context(nc.sbuf_tensor(_un("qres"), [128, OC * T], BF16))
        kres = w3_es.enter_context(nc.sbuf_tensor(_un("kres"), [128, OC * T], BF16))
        vres = w3_es.enter_context(
            nc.sbuf_tensor(_un("vres"), [128, NPAIR * NKT * 130], BF16))

        sems = {}
        for name in ["dc", "drl", "tv", "ts", "tt", "tg",
                     "dxr0", "dxr1", "dxr2", "dxr3", "dst0", "dst1", "dss0", "dss1",
                     "dsq0", "dsq1", "dsk0", "dsk1", "dsd0", "dsd1",
                     "datt0", "datt1", "dtb0", "dtb1", "dwo0", "dwo1",
                     "dx20", "dx21", "dqs0", "dqs1", "dcr", "dhs",
                     "dr20", "dr21", "dxa0", "dxa1", "dou0", "dou1", "dsw"]:
            sems[name] = Sem(es, nc, name)
        dc, drl = sems["dc"], sems["drl"]
        sv, ss, st, sg = (sems[k] for k in ["tv", "ts", "tt", "tg"])
        dxr = [sems["dxr0"], sems["dxr1"], sems["dxr2"], sems["dxr3"]]
        dstg = [sems["dst0"], sems["dst1"]]
        dss = [sems["dss0"], sems["dss1"]]
        dsq = [sems["dsq0"], sems["dsq1"]]
        dsk = [sems["dsk0"], sems["dsk1"]]
        dsd = [sems["dsd0"], sems["dsd1"]]
        datt = [sems["datt0"], sems["datt1"]]
        dtb = [sems["dtb0"], sems["dtb1"]]
        dwo = [sems["dwo0"], sems["dwo1"]]
        dx2 = [sems["dx20"], sems["dx21"]]
        dqs = [sems["dqs0"], sems["dqs1"]]
        dcr, dhs = sems["dcr"], sems["dhs"]
        dr2 = [sems["dr20"], sems["dr21"]]
        dxa = [sems["dxa0"], sems["dxa1"]]
        dou = [sems["dou0"], sems["dou1"]]
        dsw = sems["dsw"]
        for s_ in sems.values():
            SY.sem_clear(s_.h)
        nc.all_engine_barrier()

        def bar():
            nc.all_engine_barrier()

        def ln_stats(src_ap, so):
            V.bn_stats(bnbuf[:, 0:6], src_ap[:, 0:512])
            sv.inc(V.bn_stats(bnbuf[:, 6:12], src_ap[:, 512:1024]))
            sv.wait(V)
            sv.inc(V.bn_aggr(stat[:, so:so + 2], bnbuf[:, 0:12]))
            sv.wait(V)
            sv.inc(V.tensor_scalar(stat[:, so + 2:so + 3], stat[:, so + 1:so + 2],
                                   EPS, None, op0=OP.add))
            sv.wait(V)
            return sv.inc(V.reciprocal(stat[:, so + 3:so + 4],
                                       stat[:, so + 2:so + 3]))

        def ln_fin(so, v2):
            sv.wait(S, v2)
            s1 = ss.inc(S.activation(stat[:, so + 2:so + 3], stat[:, so + 3:so + 4],
                                     AF.Sqrt))
            ss.wait(V, s1)
            sv.wait(V)
            mbv = sv.inc(V.tensor_scalar(stat[:, so + 3:so + 4], stat[:, so:so + 1],
                                         stat[:, so + 2:so + 3], -1.0,
                                         op0=OP.mult, op1=OP.mult))
            return stat[:, so + 2:so + 3], stat[:, so + 3:so + 4], mbv

        def ln_chain(src_ap, so):
            return ln_fin(so, ln_stats(src_ap, so))

        _ph = ExitStack()
        it_f = _ph.enter_context(nc.sbuf_tensor(_un("it_f"), [128, 128], I32))
        it_p = _ph.enter_context(nc.sbuf_tensor(_un("it_p"), [128, 1], F32))
        trigA = _ph.enter_context(nc.sbuf_tensor(_un("trigA"), [64, 1024], F32))
        trigB = _ph.enter_context(nc.sbuf_tensor(_un("trigB"), [64, 1024], F32))
        borowf = _ph.enter_context(nc.sbuf_tensor(_un("borowf"), [1, H], F32))
        wstg0 = _ph.enter_context(nc.sbuf_tensor(_un("wstg0"), [128, DH], F32))
        wstg1 = _ph.enter_context(nc.sbuf_tensor(_un("wstg1"), [128, DH], F32))
        qh16 = _ph.enter_context(nc.sbuf_tensor(_un("qh16"), [128, DH], F16))
        wqa = _ph.enter_context(nc.sbuf_tensor(_un("wqa"), [128, DH], BF16))
        wqb = _ph.enter_context(nc.sbuf_tensor(_un("wqb"), [128, DH], BF16))
        pst = _ph.enter_context(nc.psum_tensor(_un("pst"), [128, 128], F32))
        with _ph:
            stg = [wstg0, wstg1]
            wqx = [wqa, wqb]
            G.iota(it_f[:], pattern=[[1, 128]], base=0, channel_multiplier=0)
            sg.inc(G.iota(it_p[:], pattern=[[0, 1]], base=0, channel_multiplier=1,
                   allow_small_or_imprecise_dtypes=True))
            sg.wait(V)
            V.tensor_scalar(ident[:], it_f[:], it_p[:], None, op0=OP.is_equal)
            V.tensor_scalar(identf[:], it_f[:], it_p[:], None, op0=OP.is_equal)
            V.memset(ones_row[:], 1.0)
            V.memset(ones1f[:], 1.0)
            V.memset(negRM[:], -RM)
            V.memset(posRM[:], RM)
            sv.inc(V.memset(ones2[:], 0.0))
            sv.wait(V)
            V.memset(ones2[0:64, 0:1], 1.0)
            sv.inc(V.memset(ones2[64:128, 1:2], 1.0))

            dc.inc(SY.dma_start(stat[:, 0:OC], bq_h), 16)
            dc.inc(SY.dma_start(bv_pre[:], bv_h), 16)
            dc.inc(SY.dma_start(borowf[:], bo_h[None, :]), 16)
            dc.inc(SY.dma_start(convw_s[:], cw_h), 16)
            dc.inc(SY.dma_start(convb_s[:], cb_h), 16)
            dc.inc(SY.dma_start(trigA[:], cos_h), 16)
            dc.inc(SY.dma_start(trigB[:], sin_h), 16)
            dc.wait(V)
            sv.inc(V.tensor_scalar(bq_pre[:], stat[:, 0:OC], 0.125, None, op0=OP.mult))
            sv.inc(V.tensor_copy(bo_row[:], borowf[:]))
            V.tensor_copy(cosfm[0:64, :], trigA[:])
            V.tensor_copy(cosfm[64:128, :], trigA[:])
            V.tensor_copy(sinfm[0:64, :], trigB[:])
            sv.inc(V.tensor_copy(sinfm[64:128, :], trigB[:]))

            wsrc = [wq_h, wk_h, wv_h]
            castv = []
            wspill = [0, 0]
            for i, (wh, b) in enumerate([(w, k) for w in range(3) for k in range(2)]):
                eng = SY if i % 2 == 0 else G
                if i >= 2:
                    sv.wait(eng, castv[i - 2])
                lv = dstg[i % 2].inc(eng.dma_start(
                    stg[i % 2][:, 0:4 * H].rearrange("p (c n) -> p c n", n=H),
                    wsrc[wh][b * 512:(b + 1) * 512, :].rearrange(
                        "(c p) n -> p c n", p=128)), 16)
                dstg[i % 2].wait(V, lv)
                if i >= 2:
                    dss[i % 2].wait(V, wspill[i % 2])
                castv.append(sv.inc(V.tensor_copy(
                    wqx[i % 2][:, 0:4 * H], stg[i % 2][:, 0:4 * H])))
                sv.wait(G, castv[-1])
                for k4 in range(4):
                    tix = wh * KC + b * 4 + k4
                    wspill[i % 2] = dss[i % 2].inc(G.dma_start(
                        wqkv_sp[:, :, tix * 128:(tix + 1) * 128].rearrange(
                            "o p n -> p o n"),
                        wqx[i % 2][:, k4 * H:(k4 + 1) * H].rearrange(
                            "p (o n) -> p o n", n=128)), 16)
            for b in range(2):
                sl = b % 2
                eng = SY if sl == 0 else G
                sv.wait(eng, castv[-2])
                lv = dstg[sl].inc(eng.dma_start(
                    stg[sl][:, 0:4 * H].rearrange("p (c n) -> p c n", n=H),
                    wo_h[b * 512:(b + 1) * 512, :].rearrange(
                        "(c p) n -> p c n", p=128)), 16)
                dstg[sl].wait(V, lv)
                castv.append(sv.inc(V.tensor_copy(
                    wo3[:, b * 4 * H:(b + 1) * 4 * H], stg[sl][:, 0:4 * H])))

            def quant_weight(src_h, nchunk, width, sw_col, dst_sp, mwcol=None):
                nb = DH // width
                nbatch = nchunk // nb
                bw = nb * width
                redv = []
                sv.wait(SY)
                sv.wait(G)
                for c in range(nbatch):
                    if c >= 2:
                        sv.wait(SY, redv[c - 2])
                        sv.wait(G, redv[c - 2])
                    eng = SY if c % 2 == 0 else G
                    lv = dstg[c % 2].inc(eng.dma_start(
                        stg[c % 2][:, 0:bw].rearrange("p (c n) -> p c n", n=width),
                        src_h[c * nb * 128:(c + 1) * nb * 128, :].rearrange(
                            "(c p) n -> p c n", p=128)), 16)
                    dstg[c % 2].wait(V, lv)
                    redv.append(sv.inc(V.tensor_reduce(
                        nrm[:, c % 16:c % 16 + 1], stg[c % 2][:, 0:bw],
                        axis=AX.X, op=OP.add, apply_absolute_value=True)))
                sv.wait(V)
                sv.inc(V.tensor_reduce(nrm[:, 20:21], nrm[:, 0:nbatch],
                                       axis=AX.X, op=OP.add))
                sv.wait(TE)
                tq1 = st.inc(TE.matmul(pst[0:1, 0:128], nrm[:, 20:21], identf[:],
                                       start=True, stop=True))
                st.wait(V, tq1)
                sv.inc(V.tensor_reduce(nrm[0:1, 21:22], pst[0:1, 0:128],
                                       axis=AX.X, op=OP.add))
                sv.wait(TE)
                tq2 = st.inc(TE.matmul(pst[:, 0:1], ones1f[:], nrm[0:1, 21:22],
                                       start=True, stop=True))
                st.wait(V, tq2)
                sv.inc(V.tensor_copy(nrm[:, 21:22], pst[:, 0:1]))
                sv.wait(V)
                sv.inc(V.reciprocal(nrm[:, 22:23], nrm[:, 21:22]))
                sv.wait(V)
                nel = float(nchunk * 128 * width)
                sv.inc(V.tensor_scalar(wsc[:, sw_col:sw_col + 1], nrm[:, 22:23], nel,
                                       None, op0=OP.mult))
                if mwcol is not None:
                    sv.inc(V.tensor_scalar(wsc[:, mwcol:mwcol + 1], nrm[:, 21:22],
                                           1.0 / (nel * 127.0), None, op0=OP.mult))
                sv.wait(V)
                qv = [0, 0]
                qst = [0, 0]
                sv.wait(SY)
                sv.wait(G)
                dss[0].wait(V)
                dss[1].wait(V)
                for c in range(nbatch):
                    sl = c % 2
                    if c >= 2:
                        sv.wait(SY, qv[sl])
                        sv.wait(G, qv[sl])
                        dss[sl].wait(V, qst[sl])
                    eng = SY if c % 2 == 0 else G
                    lv = dstg[sl].inc(eng.dma_start(
                        stg[sl][:, 0:bw].rearrange("p (c n) -> p c n", n=width),
                        src_h[c * nb * 128:(c + 1) * nb * 128, :].rearrange(
                            "(c p) n -> p c n", p=128)), 16)
                    dstg[sl].wait(V, lv)
                    sv.inc(V.tensor_scalar(qh16[:, 0:bw], stg[sl][:, 0:bw],
                                           wsc[:, sw_col:sw_col + 1], RM,
                                           op0=OP.mult, op1=OP.add))
                    sv.wait(V)
                    sv.inc(V.tensor_scalar(wqx[sl][:, 0:bw], qh16[:, 0:bw],
                                           -RM, -1.0, op0=OP.add, op1=OP.max))
                    sv.wait(V)
                    qv[sl] = sv.inc(V.tensor_scalar(wqx[sl][:, 0:bw], wqx[sl][:, 0:bw],
                                                    1.0, None, op0=OP.min))
                    sv.wait(G, qv[sl])
                    qst[sl] = dss[sl].inc(G.dma_start(
                        dst_sp[c * nb:(c + 1) * nb].rearrange("c p n -> p c n"),
                        wqx[sl][:, 0:bw].rearrange("p (c n) -> p c n", n=width)), 16)

            quant_weight(w1t_h, KC, DH, 0, w1qt_sp, mwcol=1)
            quant_weight(w2t_h, HC, H, 2, w2qt_sp)
            for s_ in (dstg[0], dstg[1], dss[0], dss[1], dc):
                s_.wait(SY)
        bar()

        _ph = ExitStack()
        attA = _ph.enter_context(nc.sbuf_tensor(_un("attA"), [128, OC * T], BF16))
        with _ph:
            A3 = attA[:, :].rearrange("p (c n) -> p c n", n=T)

            for s in range(NSEQ):
                _ph = ExitStack()
                xr = [_ph.enter_context(nc.sbuf_tensor(_un("xra"), [128, H], F32))
                      for _ in range(3)]
                ln1b0 = _ph.enter_context(nc.sbuf_tensor(_un("ln1b0"), [128, H], BF16))
                ln1b1 = _ph.enter_context(nc.sbuf_tensor(_un("ln1b1"), [128, H], BF16))
                ptrA = _ph.enter_context(nc.psum_tensor(_un("ptrA"), [128, 512], F32))
                ptrB = _ph.enter_context(nc.psum_tensor(_un("ptrB"), [128, 512], F32))
                with _ph:
                    ln1b = [ln1b0, ln1b1]
                    ptr = [ptrA, ptrB]
                    xload = [dxr[i].inc(SY.dma_start(
                        xr[i][:], x_h[s, i * 128:(i + 1) * 128, :]), 16) for i in range(3)]
                    tcv = [0, 0]
                    lnuse = [0, 0]
                    for i in range(NT128):
                        dxr[i % 3].wait(V, xload[i])
                        r_ap, mb_ap, mbv_ = ln_chain(xr[i % 3][:], (i % 4) * 4)
                        sv.wait(S, mbv_)
                        if i >= 2:
                            st.wait(S, lnuse[i % 2])
                        sev = ss.inc(S.activation(ln1b[i % 2][:], xr[i % 3][:], AF.Identity,
                                                  scale=r_ap, bias=mb_ap))
                        if i + 3 < NT128:
                            ss.wait(SY, sev)
                            xload.append(dxr[(i + 3) % 3].inc(SY.dma_start(
                                xr[(i + 3) % 3][:], x_h[s, (i + 3) * 128:(i + 4) * 128, :]), 16))
                        ss.wait(TE, sev)
                        for half in range(2):
                            sv.wait(TE, tcv[half])
                            for k4 in range(4):
                                kc = half * 4 + k4
                                _ins = TE.matmul(ptr[half][:, k4 * 128:(k4 + 1) * 128],
                                          ln1b[i % 2][:, kc * 128:(kc + 1) * 128],
                                          ident[:], start=True, stop=True)
                            tev = st.inc(_ins)
                            lnuse[i % 2] = tev
                            st.wait(V, tev)
                            tcv[half] = sv.inc(V.tensor_copy(
                                A3[:, half * 4:half * 4 + 4, i * 128:(i + 1) * 128],
                                ptr[half][:, :].rearrange("p (c n) -> p c n", n=128)))
                bar()

                _ph = ExitStack()
                wsb0 = _ph.enter_context(
                    nc.sbuf_tensor(_un("wsb0"), [128, 3 * KC * 128], BF16))
                wsb1 = _ph.enter_context(
                    nc.sbuf_tensor(_un("wsb1"), [128, 3 * KC * 128], BF16))
                vtmp = _ph.enter_context(nc.sbuf_tensor(_un("vtmp"), [128, 512], BF16))
                rot2 = [_ph.enter_context(nc.sbuf_tensor(_un("rot2"), [128, 512], BF16))
                        for _ in range(2)]
                rt1 = _ph.enter_context(nc.sbuf_tensor(_un("rt1"), [128, 512], BF16))
                rt2 = _ph.enter_context(nc.sbuf_tensor(_un("rt2"), [128, 512], BF16))
                sqb = _ph.enter_context(nc.sbuf_tensor(_un("sqb"), [128, 256], BF16))
                prodb = _ph.enter_context(nc.sbuf_tensor(_un("prodb"), [128, 8], BF16))
                pq = _ph.enter_context(nc.psum_tensor(_un("pq"), [128, 512], F32))
                pk = _ph.enter_context(nc.psum_tensor(_un("pk"), [128, 512], F32))
                pv = _ph.enter_context(nc.psum_tensor(_un("pv"), [128, 512], F32))
                ptrV = _ph.enter_context(nc.psum_tensor(_un("ptrV"), [128, 512], F32))
                pnb = _ph.enter_context(nc.psum_tensor(_un("pnb"), [2, 512], F32))
                pnb2 = _ph.enter_context(nc.psum_tensor(_un("pnb2"), [1, 8], F32))
                pbc = _ph.enter_context(nc.psum_tensor(_un("pbc"), [128, 8], F32))
                with _ph:
                    wsb = [wsb0, wsb1]
                    if s == 0:
                        VR = vres[:, :].rearrange("p (t c) -> p t c", c=130)
                        V.memset(VR[:, :, 64:65], 1.0)
                        onesv = sv.inc(V.memset(VR[:, :, 129:130], 1.0))
                    else:
                        onesv = 0
                    evq = evk = evv = 0
                    vtuse = 0
                    trv = 0
                    nbred = 0
                    rotuse = [0, 0]
                    wldv = [0, 0]
                    wsuse = [0, 0]
                    wldv[0] = dstg[0].inc(SY.dma_start(wsb[0][:], wqkv_sp[0]), 16)
                    for oc in range(OC):
                        par = oc % 2
                        qview = qres[:, oc * T:(oc + 1) * T]
                        kview = kres[:, oc * T:(oc + 1) * T]
                        VD4 = vres[:, oc * NKT * 130:(oc + 1) * NKT * 130].rearrange(
                            "p (t g c) -> p t g c", g=2, c=65)
                        if oc + 1 < OC:
                            sl2 = (oc + 1) % 2
                            eng = SY if sl2 == 0 else G
                            st.wait(eng, wsuse[sl2])
                            wldv[sl2] = dstg[sl2].inc(
                                eng.dma_start(wsb[sl2][:], wqkv_sp[oc + 1]), 16)
                        dstg[par].wait(TE, wldv[par])
                        for j in range(NT512):
                            ss.wait(TE, evq)
                            for kc in range(KC):
                                _ins = TE.matmul(pq[:],
                                          wsb[par][:, (0 * KC + kc) * 128:
                                                   (0 * KC + kc) * 128 + 128],
                                          A3[:, kc, j * 512:(j + 1) * 512],
                                          start=(kc == 0), stop=(kc == KC - 1))
                            mmq = st.inc(_ins)
                            ss.wait(TE, evk)
                            for kc in range(KC):
                                _ins = TE.matmul(pk[:],
                                          wsb[par][:, (1 * KC + kc) * 128:
                                                   (1 * KC + kc) * 128 + 128],
                                          A3[:, kc, j * 512:(j + 1) * 512],
                                          start=(kc == 0), stop=(kc == KC - 1))
                            mmk = st.inc(_ins)
                            ss.wait(TE, evv)
                            st.wait(TE, vtuse)
                            for kc in range(KC):
                                _ins = TE.matmul(pv[:],
                                          wsb[par][:, (2 * KC + kc) * 128:
                                                   (2 * KC + kc) * 128 + 128],
                                          A3[:, kc, j * 512:(j + 1) * 512],
                                          start=(kc == 0), stop=(kc == KC - 1))
                            mmv = st.inc(_ins)
                            wsuse[par] = mmv
                            st.wait(S, mmq)
                            evq = ss.inc(S.activation(qview[:, j * 512:(j + 1) * 512], pq[:],
                                                      AF.Identity, scale=0.125,
                                                      bias=bq_pre[:, oc:oc + 1]))
                            st.wait(S, mmk)
                            evk = ss.inc(S.activation(kview[:, j * 512:(j + 1) * 512], pk[:],
                                                      AF.Identity, scale=1.0, bias=0.0))
                            st.wait(S, mmv)
                            evv = ss.inc(S.activation(vtmp[:], pv[:], AF.Identity,
                                                      scale=1.0, bias=bv_pre[:, oc:oc + 1]))
                            ss.wait(TE, evv)
                            sv.wait(TE, trv)
                            if s == 0:
                                sv.wait(TE, onesv)
                            for i4 in range(4):
                                _ins = TE.matmul(ptrV[:, i4 * 128:(i4 + 1) * 128],
                                          vtmp[:, i4 * 128:(i4 + 1) * 128], ident[:],
                                          start=True, stop=True)
                            vtuse = st.inc(_ins)
                            st.wait(V, vtuse)
                            trv = sv.inc(V.tensor_copy(
                                VD4[:, j * 4:j * 4 + 4, :, 0:64],
                                ptrV[:, :].rearrange("p (t g c) -> p t g c", g=2, c=64)))
                        ss.wait(V)
                        ss.wait(S)
                        rolast = 0
                        rits = [(qview, 0, 2), (qview, 0, 3),
                                (kview, 2, 2), (kview, 2, 3)]
                        rotv = [0] * 4

                        def rot_S(it):
                            buf_, _, jj_ = rits[it]
                            sl_ = it % 2
                            src_ = buf_[:, jj_ * 512:(jj_ + 1) * 512]
                            sv.wait(S, rotuse[sl_])
                            for hh in range(2):
                                b0 = hh * 64
                                S.activation(rot2[sl_][b0:b0 + 32, :],
                                             src_[b0 + 32:b0 + 64, :],
                                             AF.Identity, scale=-1.0)
                                _i = S.activation(rot2[sl_][b0 + 32:b0 + 64, :],
                                                  src_[b0:b0 + 32, :],
                                                  AF.Identity, scale=1.0)
                            rotv[it] = ss.inc(_i)

                        rot_S(0)
                        for it in range(4):
                            buf, dstoff, jj = rits[it]
                            sl_ = it % 2
                            if it + 1 < 4:
                                rot_S(it + 1)
                            src = buf[:, jj * 512:(jj + 1) * 512]
                            ccols = cosfm[:, (jj - 2) * 512:(jj - 1) * 512]
                            scols = sinfm[:, (jj - 2) * 512:(jj - 1) * 512]
                            sv.wait(V)
                            V.tensor_tensor(rt1[:], src, ccols, op=OP.mult)
                            ss.wait(V, rotv[it])
                            rotuse[sl_] = sv.inc(V.tensor_tensor(
                                rt2[:], rot2[sl_][:], scols, op=OP.mult))
                            sv.wait(V)
                            ridx = dstoff + (jj - 2)
                            rolast = sv.inc(V.tensor_tensor(
                                src, rt1[:], rt2[:], op=OP.add))
                        sv.wait(S, rolast)
                        S.activation(sqb[:, 0:128], qview[:, 0:1024:8], AF.Square)
                        S.activation(sqb[:, 128:192], qview[:, 1024:1536:8], AF.Square)
                        sq_q = ss.inc(S.activation(sqb[:, 192:256],
                                                   qview[:, 1536:2048:8], AF.Square))
                        ss.wait(TE, sq_q)
                        sv.wait(TE, nbred)
                        tq = st.inc(TE.matmul(pnb[:, 0:256], ones2[:], sqb[:, 0:256],
                                              start=True, stop=True))
                        st.wait(V, tq)
                        nq = sv.inc(V.tensor_reduce(nrm[0:2, oc:oc + 1], pnb[:, 0:256],
                                                    axis=AX.X, op=OP.max))
                        sv.wait(S, nq)
                        S.activation(sqb[:, 0:128], kview[:, 0:1024:8], AF.Square)
                        S.activation(sqb[:, 128:192], kview[:, 1024:1536:8], AF.Square)
                        sq_k = ss.inc(S.activation(sqb[:, 192:256],
                                                   kview[:, 1536:2048:8], AF.Square))
                        ss.wait(TE, sq_k)
                        sv.wait(TE, nq)
                        tk = st.inc(TE.matmul(pnb[:, 256:512], ones2[:], sqb[:, 0:256],
                                              start=True, stop=True))
                        st.wait(V, tk)
                        nbred = sv.inc(V.tensor_reduce(nrm[0:2, 8 + oc:9 + oc], pnb[:, 256:512],
                                                       axis=AX.X, op=OP.max))
                    sv.wait(S)
                    ss.inc(S.activation(nrm[0:2, 16:24], nrm[0:2, 0:8], AF.Sqrt))
                    ss.wait(S)
                    ssq2 = ss.inc(S.activation(nrm[0:2, 0:8], nrm[0:2, 8:16], AF.Sqrt))
                    ss.wait(V, ssq2)
                    sv.inc(V.tensor_tensor(nrm[0:2, 8:16], nrm[0:2, 16:24], nrm[0:2, 0:8],
                                           op=OP.mult))
                    sv.wait(V)
                    pv_ = sv.inc(V.tensor_copy(prodb[0:2, 0:8], nrm[0:2, 8:16]))
                    sv.wait(TE, pv_)
                    tb = st.inc(TE.matmul(pnb2[:], ones2[0:2, 0:1], prodb[0:2, 0:8],
                                          start=True, stop=True))
                    st.wait(V, tb)
                    ngv = sv.inc(V.tensor_scalar(nrm[0:1, 0:8], pnb2[:], -1.0, None, op0=OP.mult))
                    sv.wait(TE, ngv)
                    tb2 = st.inc(TE.matmul(pbc[:], ones1f[:], nrm[0:1, 0:8],
                                           start=True, stop=True))
                    st.wait(V, tb2)
                    sv.inc(V.tensor_copy(negM[:, 0:8], pbc[:]))
                bar()

                _ph = ExitStack()
                ptil0 = _ph.enter_context(nc.sbuf_tensor(_un("ptil0"), [128, 1024], BF16))
                ptil1 = _ph.enter_context(nc.sbuf_tensor(_un("ptil1"), [128, 1024], BF16))
                ptil2 = _ph.enter_context(nc.sbuf_tensor(_un("ptil2"), [128, 1024], BF16))
                tmpB0 = _ph.enter_context(nc.sbuf_tensor(_un("tmpB0"), [64, T], BF16))
                tmpB1 = _ph.enter_context(nc.sbuf_tensor(_un("tmpB1"), [64, T], BF16))
                invrA = _ph.enter_context(nc.sbuf_tensor(_un("invrA"), [1, T], F32))
                invrB = _ph.enter_context(nc.sbuf_tensor(_un("invrB"), [1, T], F32))
                lbufA = _ph.enter_context(nc.sbuf_tensor(_un("lbufA"), [1, T], F32))
                lbufB = _ph.enter_context(nc.sbuf_tensor(_un("lbufB"), [1, T], F32))
                invbA2 = _ph.enter_context(nc.sbuf_tensor(_un("invbA2"), [64, T], BF16))
                invbB2 = _ph.enter_context(nc.sbuf_tensor(_un("invbB2"), [64, T], BF16))
                sc0 = _ph.enter_context(nc.psum_tensor(_un("sc0"), [128, 1024], F32))
                sc1 = _ph.enter_context(nc.psum_tensor(_un("sc1"), [128, 1024], F32))
                accA = _ph.enter_context(nc.psum_tensor(_un("accA"), [65, 512], F32))
                accB = _ph.enter_context(nc.psum_tensor(_un("accB"), [65, 512], F32))
                pib = _ph.enter_context(nc.psum_tensor(_un("pib"), [128, 512], F32))
                with _ph:
                    ptil = [ptil0, ptil1, ptil2]
                    tmpB = [tmpB0, tmpB1]
                    sc = [sc0, sc1]
                    expv = [0, 0, 0]
                    avv3 = [0, 0, 0]
                    epiv = 0
                    bcv = 0
                    ibcp = 0
                    tailv = 0
                    erv_ = 0
                    dmaBv = [0, 0]
                    smmv = {}
                    NSTEP = 4 * NKT
                    pend = [None]

                    def qkd(p):
                        return (qres[:, p * T:(p + 1) * T],
                                kres[:, p * T:(p + 1) * T],
                                vres[:, p * NKT * 130:(p + 1) * NKT * 130]
                                .rearrange("p (t n) -> p t n", n=130))

                    def issue_smm(g):
                        p, i = divmod(g, NSTEP)
                        qc, kt = divmod(i, NKT)
                        sl = g % 2
                        qc_, kc_, _ = qkd(p)
                        TE.matmul(sc[sl][:, 0:512], kc_[0:64, kt * 128:(kt + 1) * 128],
                                  qc_[0:64, qc * 512:(qc + 1) * 512],
                                  start=True, stop=True, tile_position=(0, 0))
                        smmv[g] = st.inc(TE.matmul(
                            sc[sl][:, 512:1024], kc_[64:128, kt * 128:(kt + 1) * 128],
                            qc_[64:128, qc * 512:(qc + 1) * 512],
                            start=True, stop=True, tile_position=(64, 0)))

                    def flush_tailA(r):
                        nonlocal erv_
                        if r == 0:
                            st.wait(V, bcv)
                            sv.wait(V)
                        V.reciprocal(invrA[0:1, r * 512:(r + 1) * 512],
                                     lbufA[0:1, r * 512:(r + 1) * 512])
                        erv_ = sv.inc(V.reciprocal(
                            invrB[0:1, r * 512:(r + 1) * 512],
                            lbufB[0:1, r * 512:(r + 1) * 512]))

                    def flush_tailB():
                        nonlocal bcv, ibcp, tailv
                        ep, erv = pend[0]
                        pend[0] = None
                        sv.wait(TE, erv_)
                        for r in range(4):
                            sv.wait(TE, ibcp)
                            TE.matmul(pib[0:64, :], ones1f[0:1, 0:64],
                                      invrA[0:1, r * 512:(r + 1) * 512],
                                      start=True, stop=True, tile_position=(0, 0))
                            bcv = st.inc(TE.matmul(
                                pib[64:128, :], ones1f[0:1, 0:64],
                                invrB[0:1, r * 512:(r + 1) * 512],
                                start=True, stop=True, tile_position=(0, 64)))
                            st.wait(V, bcv)
                            V.tensor_copy(invbA2[:, r * 512:(r + 1) * 512],
                                          pib[0:64, :])
                            ibcp = sv.inc(V.tensor_copy(
                                invbB2[:, r * 512:(r + 1) * 512], pib[64:128, :]))
                        sv.wait(V)
                        V.tensor_tensor(
                            attA[0:64, ep * T:(ep + 1) * T],
                            attA[0:64, ep * T:(ep + 1) * T],
                            invbA2[:, :], op=OP.mult)
                        tailv = sv.inc(V.tensor_tensor(
                            tmpB[ep % 2][:, :], tmpB[ep % 2][:, :],
                            invbB2[:, :], op=OP.mult))
                        sv.wait(SY, tailv)
                        dmaBv[ep % 2] = dtb[ep % 2].inc(SY.dma_start(
                            attA[64:128, ep * T:(ep + 1) * T], tmpB[ep % 2][:]), 16)

                    issue_smm(0)
                    issue_smm(1)
                    for g in range(NPAIR * NSTEP):
                        p, i = divmod(g, NSTEP)
                        qc, kt = divmod(i, NKT)
                        sl = g % 2
                        if pend[0] is not None and i in (2, 10, 18, 26):
                            flush_tailA((2, 10, 18, 26).index(i))
                        if i == 2 * NKT + 4 and pend[0] is not None:
                            flush_tailB()
                        _, _, dc4 = qkd(p)
                        pl = g % 3
                        st.wait(S, smmv[g])
                        st.wait(S, avv3[pl])
                        expv[pl] = ss.inc(S.activation(
                            ptil[pl][:], sc[sl][:], AF.Exp,
                            bias=negM[:, p:p + 1], scale=1.0))
                        ss.wait(TE, expv[pl])
                        if g + 2 < NPAIR * NSTEP:
                            issue_smm(g + 2)
                        if kt == 0:
                            sv.wait(TE, epiv)
                        TE.matmul(accA[:], dc4[:, kt, 0:65], ptil[pl][:, 0:512],
                                  start=(kt == 0), stop=(kt == NKT - 1))
                        avv3[pl] = st.inc(TE.matmul(
                            accB[:], dc4[:, kt, 65:130], ptil[pl][:, 512:1024],
                            start=(kt == 0), stop=(kt == NKT - 1)))
                        if kt == NKT - 1:
                            st.wait(V, avv3[pl])
                            sv.wait(V)
                            if qc == 0 and p >= 2:
                                dtb[p % 2].wait(V, dmaBv[p % 2])
                            V.tensor_copy(lbufA[0:1, qc * 512:(qc + 1) * 512],
                                          accA[64:65, :])
                            sv.inc(V.tensor_copy(lbufB[0:1, qc * 512:(qc + 1) * 512],
                                                 accB[64:65, :]))
                            V.tensor_copy(
                                attA[0:64, p * T + qc * 512:p * T + (qc + 1) * 512],
                                accA[0:64, :])
                            epiv = sv.inc(V.tensor_copy(
                                tmpB[p % 2][:, qc * 512:(qc + 1) * 512],
                                accB[0:64, :]))
                            if qc == 3:
                                pend[0] = (p, epiv)
                    for r_ in range(4):
                        flush_tailA(r_)
                    flush_tailB()
                    dtb[0].wait(SY)
                    dtb[1].wait(SY)
                bar()

                _ph = ExitStack()
                xr = [_ph.enter_context(nc.sbuf_tensor(_un("xrb"), [128, H], F32))
                      for _ in range(4)]
                ln3t = _ph.enter_context(nc.sbuf_tensor(_un("ln3t"), [128, H], F32))
                x2t = [_ph.enter_context(nc.sbuf_tensor(_un("x2t"), [128, H], F32))
                       for _ in range(4)]
                ln2t = _ph.enter_context(nc.sbuf_tensor(_un("ln2t"), [128, H], F32))
                qx16 = _ph.enter_context(nc.sbuf_tensor(_un("qx16"), [128, H], F16))
                xqtm = _ph.enter_context(nc.sbuf_tensor(_un("xqtm"), [128, H], BF16))
                xqs0 = _ph.enter_context(nc.sbuf_tensor(_un("xqs0"), [128, 512], BF16))
                xqs1 = _ph.enter_context(nc.sbuf_tensor(_un("xqs1"), [128, 512], BF16))
                rowA = _ph.enter_context(nc.sbuf_tensor(_un("rowA"), [1, T], F32))
                pw0 = _ph.enter_context(nc.psum_tensor(_un("pw0"), [128, 1024], F32))
                pw1 = _ph.enter_context(nc.psum_tensor(_un("pw1"), [128, 1024], F32))
                pw2 = _ph.enter_context(nc.psum_tensor(_un("pw2"), [128, 1024], F32))
                pw3 = _ph.enter_context(nc.psum_tensor(_un("pw3"), [128, 1024], F32))
                with _ph:
                    xqs = [xqs0, xqs1]
                    pw = [pw0, pw1, pw2, pw3]
                    wo_use = [0, 0]
                    evacv = [0] * 4
                    xq_dma = [0, 0]
                    ln3use_s = 0
                    ln2use_s = 0
                    xqtmuse_s = 0
                    x2dma = [0] * 4
                    x2rds = [0] * 4
                    xruse = [0] * 4
                    for g in range(4):
                        for kc in range(KC):
                            if kc == 0:
                                for i in range(4):
                                    ss.wait(TE, evacv[i])
                            for i in range(4):
                                ti = g * 4 + i
                                for hh in range(2):
                                    _ins = TE.matmul(pw[i][:, hh * 512:(hh + 1) * 512],
                                              A3[:, kc, ti * 128:(ti + 1) * 128],
                                              wo3[:, kc * H + hh * 512:
                                                  kc * H + (hh + 1) * 512],
                                              start=(kc == 0), stop=False)
                            st.inc(_ins)
                        for i in range(4):
                            for hh in range(2):
                                _ins = TE.matmul(pw[i][:, hh * 512:(hh + 1) * 512],
                                          ones_row[0:1, 0:128],
                                          bo_row[0:1, hh * 512:(hh + 1) * 512],
                                          start=False, stop=True)
                        womm = st.inc(_ins)
                        xvv = [0] * 4
                        v2s = [0] * 4
                        for i in range(4):
                            ti = g * 4 + i
                            sv.wait(SY, xruse[i])
                            xvv[i] = dxr[i].inc(SY.dma_start(
                                xr[i][:], x_h[s, ti * 128:(ti + 1) * 128, :]), 16)
                            st.wait(V, womm)
                            v2s[i] = ln_stats(pw[i][:], 16 + i * 4)
                        fin3 = [ln_fin(16 + i * 4, v2s[i]) for i in range(4)]
                        for i in range(4):
                            ti = g * 4 + i
                            r3, mb3, mbv3 = fin3[i]
                            sv.wait(S, mbv3)
                            sv.wait(S, ln3use_s)
                            l3v = ss.inc(S.activation(ln3t[:], pw[i][:],
                                                      AF.Identity,
                                                      scale=r3, bias=mb3))
                            dxr[i].wait(V, xvv[i])
                            ss.wait(V, l3v)
                            dx2[i % 2].wait(V)
                            ss.wait(V, x2rds[i])
                            addv = sv.inc(V.tensor_tensor(
                                x2t[i][:], xr[i][:], ln3t[:], op=OP.add))
                            ln3use_s = addv
                            xruse[i] = addv
                            sv.wait(SY, addv)
                            x2dma[i] = dx2[i % 2].inc(SY.dma_start(
                                x2_sp[s, ti * 128:(ti + 1) * 128, :], x2t[i][:]), 16)
                        sv.wait(V)
                        for i in range(4):
                            v2s[i] = ln_stats(x2t[i][:], 32 + i * 4)
                        fin2 = [ln_fin(32 + i * 4, v2s[i]) for i in range(4)]
                        for i in range(4):
                            ti = g * 4 + i
                            qo = 48 + i * 4
                            r2, mb2, mbv2 = fin2[i]
                            sv.wait(S, mbv2)
                            sv.wait(S, ln2use_s)
                            l2v = ss.inc(S.activation(ln2t[:], x2t[i][:],
                                                      AF.Identity,
                                                      scale=r2, bias=mb2))
                            x2rds[i] = l2v
                            ss.wait(V, l2v)
                            sv.inc(V.tensor_reduce(
                                stat[:, qo:qo + 1], ln2t[:], axis=AX.X,
                                op=OP.max, apply_absolute_value=True))
                            sv.wait(V)
                            sv.inc(V.tensor_scalar(stat[:, qo + 1:qo + 2],
                                                   stat[:, qo:qo + 1], 1e-5,
                                                   None, op0=OP.max))
                            sv.wait(V)
                            sv.inc(V.reciprocal(stat[:, qo + 2:qo + 3],
                                                stat[:, qo + 1:qo + 2]))
                            sv.wait(V)
                            sv.inc(V.tensor_scalar(
                                stat[:, qo + 3:qo + 4], stat[:, qo + 2:qo + 3],
                                127.0, None, op0=OP.mult))
                            sv.inc(V.tensor_scalar(c1sb[:, ti:ti + 1],
                                                   stat[:, qo + 1:qo + 2],
                                                   wsc[:, 1:2], None, op0=OP.mult))
                            sv.wait(V)
                            st.wait(V, xqtmuse_s)
                            sv.inc(V.tensor_scalar(qx16[:], ln2t[:],
                                                   stat[:, qo + 3:qo + 4], RM,
                                                   op0=OP.mult, op1=OP.add))
                            sv.wait(V)
                            xqv = sv.inc(V.tensor_scalar(
                                xqtm[:], qx16[:], -RM, None,
                                op0=OP.add))
                            ln2use_s = xqv
                            sv.wait(TE, xqv)
                            for half in range(2):
                                dqs[half].wait(TE, xq_dma[half])
                                for k4 in range(4):
                                    kc = half * 4 + k4
                                    _ins = TE.matmul(pw[i][:, half * 512 + k4 * 128:
                                                    half * 512 + (k4 + 1) * 128],
                                              xqtm[:, kc * 128:(kc + 1) * 128],
                                              ident[:], start=True, stop=True)
                                thv = xqtmuse_s = st.inc(_ins)
                                st.wait(S, thv)
                                cpv = ss.inc(S.copy(
                                    xqs[half][:], pw[i][:, half * 512:(half + 1) * 512]))
                                ss.wait(SY, cpv)
                                xq_dma[half] = dqs[half].inc(SY.dma_start(
                                    xqd_sp[s][half * 4:half * 4 + 4, :,
                                              ti * 128:(ti + 1) * 128].rearrange(
                                                  "c p n -> p c n"),
                                    xqs[half][:, :].rearrange("p (c n) -> p c n", n=128)), 16)
                            evacv[i] = ss.v
                    sv.wait(TE)
                    dqs[0].wait(TE, xq_dma[0])
                    dqs[1].wait(TE, xq_dma[1])
                    for ti in range(NT128):
                        dst = pw0 if ti < 8 else pw1
                        _ins = TE.matmul(dst[0:1, (ti % 8) * 128:(ti % 8 + 1) * 128],
                                  c1sb[:, ti:ti + 1], identf[:], start=True, stop=True)
                    tcv2 = st.inc(_ins)
                    st.wait(V, tcv2)
                    V.tensor_copy(rowA[0:1, 0:1024], pw0[0:1, :])
                    rv2 = sv.inc(V.tensor_copy(rowA[0:1, 1024:2048], pw1[0:1, :]))
                    sv.wait(SY, rv2)
                    dcr.inc(SY.dma_start(c1_sp[s][None, :], rowA[:]), 16)
                    for s_ in (dcr, dx2[0], dx2[1], dqs[0], dqs[1]):
                        s_.wait(SY)
                bar()

        w3_es.close()
        bar()
        _ph = ExitStack()
        w1q = _ph.enter_context(nc.sbuf_tensor(_un("w1q"), [128, KC * DH], BF16))
        xqd = _ph.enter_context(nc.sbuf_tensor(_un("xqd"), [128, KC * T], BF16))
        diag = _ph.enter_context(nc.sbuf_tensor(_un("diag"), [128, 3 * HC * 128], BF16))
        c1f = _ph.enter_context(nc.sbuf_tensor(_un("c1f"), [128, T], F32))
        crow = _ph.enter_context(nc.sbuf_tensor(_un("crow"), [1, T], F32))
        hsc2 = [_ph.enter_context(nc.sbuf_tensor(_un("hsc2"), [128, T + 2], BF16))
                for _ in range(2)]
        hsb = _ph.enter_context(nc.sbuf_tensor(_un("hsb"), [128, T], BF16))
        macc = _ph.enter_context(nc.sbuf_tensor(_un("macc"), [128, T], BF16))
        mincc = _ph.enter_context(nc.sbuf_tensor(_un("mincc"), [128, T], BF16))
        s2row = _ph.enter_context(nc.sbuf_tensor(_un("s2row"), [1, T], F32))
        pwA = _ph.enter_context(nc.psum_tensor(_un("pwA"), [128, 512], F32))
        pwB = _ph.enter_context(nc.psum_tensor(_un("pwB"), [128, 512], F32))
        pcA = _ph.enter_context(nc.psum_tensor(_un("pcA"), [128, 512], F32))
        pcB = _ph.enter_context(nc.psum_tensor(_un("pcB"), [128, 512], F32))
        pam = _ph.enter_context(nc.psum_tensor(_un("pam"), [128, 256], F32))
        prow = _ph.enter_context(nc.psum_tensor(_un("prow"), [1, 512], F32))
        pbc1 = _ph.enter_context(nc.psum_tensor(_un("pbc1"), [128, 512], F32))
        with _ph:
            pwx = [pwA, pwB]
            pcx = [pcA, pcB]
            w1ld = 0
            for kc in range(KC):
                w1ld = drl.inc(G.dma_start(w1q[:, kc * DH:(kc + 1) * DH], w1qt_sp[kc]), 16)
            dgv = 0
            for c in range(HC):
                for j3 in range(3):
                    dgv = sv.inc(V.tensor_scalar(
                        diag[:, (c * 3 + j3) * 128:(c * 3 + j3 + 1) * 128],
                        ident[:], convw_s[:, c * 3 + j3:c * 3 + j3 + 1], None, op0=OP.mult))
            V.memset(hsc2[0][:, 0:1], 0.0)
            V.memset(hsc2[0][:, T + 1:T + 2], 0.0)
            V.memset(hsc2[1][:, 0:1], 0.0)
            hscz = sv.inc(V.memset(hsc2[1][:, T + 1:T + 2], 0.0))
            spillv = 0
            mm_prev = 0
            hscuse = [0, 0]
            for s in range(NSEQ):
                st.wait(G)
                sv.wait(G)
                st.wait(V)
                for kc in range(KC):
                    drl.inc(G.dma_start(xqd[:, kc * T:(kc + 1) * T], xqd_sp[s, kc]), 16)
                drl.inc(G.dma_start(crow[:], c1_sp[s][None, :]), 16)
                drl.wait(TE)
                sv.wait(S)
                c1bv = 0
                for jbc in range(NT512):
                    if jbc > 0:
                        ss.wait(TE, c1bv)
                    tcb = st.inc(TE.matmul(pbc1[:], ones1f[:],
                                           crow[0:1, jbc * 512:(jbc + 1) * 512],
                                           start=True, stop=True))
                    st.wait(S, tcb)
                    c1bv = ss.inc(S.copy(c1f[:, jbc * 512:(jbc + 1) * 512], pbc1[:]))
                V.memset(macc[:], -3e38)
                mmi = sv.inc(V.memset(mincc[:], 3e38))
                drl.wait(TE)
                pwv = [0, 0]
                silv = [0, 0]
                hval = {}

                def w1_block(c):
                    nonlocal mmi
                    hs_ = hsc2[c % 2]
                    for j in range(NT512):
                        slw = j % 2
                        sv.wait(TE, pwv[slw])
                        for kc in range(KC):
                            _ins = TE.matmul(pwx[slw][:],
                                      w1q[:, kc * DH + c * 128:kc * DH + (c + 1) * 128],
                                      xqd[:, kc * T + j * 512:kc * T + (j + 1) * 512],
                                      start=(kc == 0), stop=(kc == KC - 1))
                        wmm = st.inc(_ins)
                        st.wait(V, wmm)
                        ss.wait(V, c1bv)
                        sv.wait(V, hscz)
                        st.wait(V, hscuse[c % 2])
                        pwv[slw] = sv.inc(V.tensor_tensor(
                            hs_[:, 1 + j * 512:1 + (j + 1) * 512], pwx[slw][:],
                            c1f[:, j * 512:(j + 1) * 512], op=OP.mult))
                    hval[c] = sv.v

                def conv_block(c):
                    nonlocal mmi, mm_prev, spillv
                    hs_ = hsc2[c % 2]
                    for j in range(NT512):
                        slc = j % 2
                        sv.wait(TE, hval[c])
                        if c == 0:
                            sv.wait(TE, dgv)
                        ss.wait(TE, silv[slc])
                        for tap in range(3):
                            _ins = TE.matmul(pcx[slc][:],
                                      diag[:, (c * 3 + tap) * 128:(c * 3 + tap + 1) * 128],
                                      hs_[:, tap + j * 512:tap + j * 512 + 512],
                                      start=(tap == 0), stop=(tap == 2))
                        cmm = st.inc(_ins)
                        hscuse[c % 2] = cmm
                        st.wait(S, cmm)
                        dhs.wait(S, spillv)
                        sv.wait(S, mm_prev)
                        silv[slc] = ss.inc(S.activation(hsb[:, j * 512:(j + 1) * 512],
                                                        pcx[slc][:], AF.Silu,
                                                        bias=convb_s[:, c:c + 1], scale=1.0))
                    ss.wait(V, silv[(NT512 - 1) % 2])
                    sv.wait(V, mmi)
                    V.tensor_tensor(macc[:], macc[:], hsb[:], op=OP.max)
                    mmi = mm_prev = sv.inc(V.tensor_tensor(mincc[:], mincc[:], hsb[:], op=OP.min))
                    ss.wait(SY, silv[(NT512 - 1) % 2])
                    spillv = dhs.inc(SY.dma_start(hs_sp[s, c], hsb[:]), 16)

                w1_block(0)
                for c in range(HC):
                    if c + 1 < HC:
                        w1_block(c + 1)
                    conv_block(c)
                sv.wait(TE, mmi)
                pamv = 0
                for ti in range(NT128):
                    sv.wait(TE, pamv)
                    TE.matmul(pam[:, 0:128], macc[:, ti * 128:(ti + 1) * 128], ident[:],
                              start=True, stop=True)
                    tav = st.inc(TE.matmul(pam[:, 128:256], mincc[:, ti * 128:(ti + 1) * 128],
                                           ident[:], start=True, stop=True))
                    st.wait(V, tav)
                    V.tensor_reduce(stat[:, 40:41], pam[:, 0:128], axis=AX.X, op=OP.max)
                    pamv = sv.inc(V.tensor_reduce(stat[:, 41:42], pam[:, 128:256], axis=AX.X,
                                                  op=OP.min, negate=True))
                    sv.wait(V)
                    sv.inc(V.tensor_tensor(stat[:, 42:43], stat[:, 40:41], stat[:, 41:42],
                                           op=OP.max))
                    sv.wait(V)
                    sv.inc(V.tensor_scalar(stat[:, 43:44], stat[:, 42:43], 1e-5, None,
                                           op0=OP.max))
                    sv.wait(V)
                    sv.inc(V.reciprocal(stat[:, 44:45], stat[:, 43:44]))
                    sv.wait(V)
                    sv.inc(V.tensor_scalar(s2s[:, ti:ti + 1], stat[:, 44:45], 127.0,
                                           None, op0=OP.mult))
                sv.wait(TE)
                dsw.wait(V)
                srv = 0
                for tg in range(4):
                    if tg > 0:
                        sv.wait(TE, srv)
                    for t4 in range(4):
                        ti = tg * 4 + t4
                        _ins = TE.matmul(prow[0:1, t4 * 128:(t4 + 1) * 128], s2s[:, ti:ti + 1],
                                  identf[:], start=True, stop=True)
                    trv2 = st.inc(_ins)
                    st.wait(V, trv2)
                    srv = sv.inc(V.tensor_copy(s2row[0:1, tg * 512:(tg + 1) * 512], prow[0:1, :]))
                sv.wait(SY, srv)
                dsw.inc(SY.dma_start(s2_sp[s][None, :], s2row[:]), 16)
            dhs.wait(SY)
            dsw.wait(SY)
            drl.wait(SY)
        bar()

        _ph = ExitStack()
        w2q = _ph.enter_context(nc.sbuf_tensor(_un("w2q"), [128, HC * H], BF16))
        ring0 = _ph.enter_context(nc.sbuf_tensor(_un("ring0"), [128, HC * 512], BF16))
        ring1 = _ph.enter_context(nc.sbuf_tensor(_un("ring1"), [128, HC * 512], BF16))
        srow0 = _ph.enter_context(nc.sbuf_tensor(_un("srow0"), [1, T], F32))
        srow1 = _ph.enter_context(nc.sbuf_tensor(_un("srow1"), [1, T], F32))
        s2b0 = _ph.enter_context(nc.sbuf_tensor(_un("s2b0"), [128, 512], BF16))
        s2b1 = _ph.enter_context(nc.sbuf_tensor(_un("s2b1"), [128, 512], BF16))
        m216a = _ph.enter_context(nc.sbuf_tensor(_un("m216a"), [128, 512], F16))
        m216b = _ph.enter_context(nc.sbuf_tensor(_un("m216b"), [128, 512], F16))
        q216a = _ph.enter_context(nc.sbuf_tensor(_un("q216a"), [128, 512], F16))
        q216b = _ph.enter_context(nc.sbuf_tensor(_un("q216b"), [128, 512], F16))
        ln4t = _ph.enter_context(nc.sbuf_tensor(_un("ln4t"), [128, H], F32))
        x2r0 = _ph.enter_context(nc.sbuf_tensor(_un("x2r0"), [128, H], F32))
        x2r1 = _ph.enter_context(nc.sbuf_tensor(_un("x2r1"), [128, H], F32))
        outt0 = _ph.enter_context(nc.sbuf_tensor(_un("outt0"), [128, H], F32))
        outt1 = _ph.enter_context(nc.sbuf_tensor(_un("outt1"), [128, H], F32))
        p2a = _ph.enter_context(nc.psum_tensor(_un("p2a"), [128, 1024], F32))
        p2b = _ph.enter_context(nc.psum_tensor(_un("p2b"), [128, 1024], F32))
        pbs = _ph.enter_context(nc.psum_tensor(_un("pbs"), [128, 512], F32))
        with _ph:
            ring = [ring0, ring1]
            s2b = [s2b0, s2b1]
            x2r = [x2r0, x2r1]
            outt = [outt0, outt1]
            p2 = [p2a, p2b]
            for c in range(HC):
                drl.inc(G.dma_start(w2q[:, c * H:(c + 1) * H], w2qt_sp[c]), 16)
            ringuse = [0, 0]
            s2use = [0, 0]
            addv2 = [0, 0]
            rsubv = [0, 0]
            outdma = [0, 0]
            x2v_use = [0, 0]
            evv_ = [0, 0]
            pbsuse = 0
            srow = [srow0, srow1]
            srldv = [0, 0]
            sj = [(s, j) for s in range(NSEQ) for j in range(NT512)]
            NSJ = len(sj)
            ringldv = {}
            rndall_v = {}

            def load_ring(idx):
                s_, j_ = sj[idx]
                slr_ = idx % 2
                if j_ == 0:
                    drl.wait(G)
                    srldv[s_] = drl.inc(
                        G.dma_start(srow[s_][:], s2_sp[s_][None, :]), 16)
                st.wait(SY, ringuse[slr_])
                ringldv[idx] = dr2[slr_].inc(SY.dma_start(
                    ring[slr_][:, :].rearrange("p (c n) -> p c n", n=512),
                    hs_sp[s_][:, :, j_ * 512:(j_ + 1) * 512].rearrange(
                        "c p n -> p c n")), 16)

            def quant(idx):
                nonlocal pbsuse
                s_, j_ = sj[idx]
                slr_ = idx % 2
                drl.wait(TE, srldv[s_])
                ss.wait(TE, pbsuse)
                tb4 = st.inc(TE.matmul(pbs[:], ones1f[:],
                                       srow[s_][0:1, j_ * 512:(j_ + 1) * 512],
                                       start=True, stop=True))
                st.wait(S, tb4)
                sv.wait(S, s2use[slr_])
                s2bv = pbsuse = ss.inc(S.copy(s2b[slr_][:], pbs[:]))
                ss.wait(V, s2bv)
                dr2[slr_].wait(V, ringldv[idx])
                q216 = [q216a, q216b]
                m216 = [m216a, m216b]
                mvs = {}

                def qmult(c_):
                    slt_ = c_ % 2
                    ss.wait(V, addv2[slt_])
                    mvs[c_] = sv.inc(V.tensor_tensor(
                        m216[slt_][:], ring[slr_][:, c_ * 512:(c_ + 1) * 512],
                        s2b[slr_][:], op=OP.mult))

                qmult(0)
                for c in range(HC):
                    slt = c % 2
                    if c + 1 < HC:
                        qmult(c + 1)
                    sv.wait(S, mvs[c])
                    sv.wait(S, rsubv[slt])
                    addv2[slt] = ss.inc(S.activation(
                        q216[slt][:], m216[slt][:], AF.Identity,
                        bias=posRM[:, 0:1], scale=1.0))
                    ss.wait(V, addv2[slt])
                    rsubv[slt] = sv.inc(V.tensor_scalar(
                        ring[slr_][:, c * 512:(c + 1) * 512], q216[slt][:],
                        -RM, None, op0=OP.add))
                s2use[slr_] = sv.v
                rndall_v[idx] = sv.v

            load_ring(0)
            quant(0)
            for idx in range(NSJ):
                s, j = sj[idx]
                slr = idx % 2
                if idx + 1 < NSJ:
                    load_ring(idx + 1)
                    quant(idx + 1)
                rndall = rndall_v[idx]
                for i2 in range(2):
                    for half in range(2):
                            ti = j * 4 + i2 * 2 + half
                            pp = p2[half]
                            sv.wait(TE, rndall)
                            sv.wait(TE, evv_[half])
                            for c in range(HC):
                                base = c * 512 + (i2 * 2 + half) * 128
                                for hh in range(2):
                                    _ins = TE.matmul(pp[:, hh * 512:(hh + 1) * 512],
                                              ring[slr][:, base:base + 128],
                                              w2q[:, c * H + hh * 512:c * H + (hh + 1) * 512],
                                              start=(c == 0), stop=(c == HC - 1))
                            mmv_ = st.inc(_ins)
                            ringuse[slr] = mmv_
                            sv.wait(SY, x2v_use[half])
                            xv = dxa[half].inc(SY.dma_start(
                                x2r[half][:], x2_sp[s, ti * 128:(ti + 1) * 128, :]), 16)
                            st.wait(V, mmv_)
                            r4, mb4, mbv4 = ln_chain(pp[:], 48 + half * 4)
                            sv.wait(S, mbv4)
                            l4v = ss.inc(S.activation(ln4t[:], pp[:], AF.Identity,
                                                      scale=r4, bias=mb4))
                            dxa[half].wait(V, xv)
                            ss.wait(V, l4v)
                            dou[half].wait(V, outdma[half])
                            ov = sv.inc(V.tensor_tensor(outt[half][:], x2r[half][:], ln4t[:],
                                                        op=OP.add))
                            evv_[half] = ov
                            x2v_use[half] = ov
                            sv.wait(SY, ov)
                            outdma[half] = dou[half].inc(SY.dma_start(
                                out_h[s, ti * 128:(ti + 1) * 128, :], outt[half][:]), 16)
            dou[0].wait(SY)
            dou[1].wait(SY)
        bar()

    return nc, dbg_outs


def host_inputs(inputs):
    x = np.ascontiguousarray(inputs["L0_attn_hidden_states"])
    common = dict(
        wq=np.ascontiguousarray(inputs["L0_attn_wq"]),
        wk=np.ascontiguousarray(inputs["L0_attn_wk"]),
        wv=np.ascontiguousarray(inputs["L0_attn_wv"]),
        wo=np.ascontiguousarray(inputs["L0_attn_wo"]),
        bq=np.ascontiguousarray(np.asarray(inputs["L0_attn_bq"]).reshape(OC, 128).T),
        bv=np.ascontiguousarray(np.asarray(inputs["L0_attn_bv"]).reshape(OC, 128).T),
        bo=np.ascontiguousarray(inputs["L0_attn_bo"]),
        cosT=np.ascontiguousarray(np.asarray(inputs["L0_attn_cos"]).T),
        sinT=np.ascontiguousarray(np.asarray(inputs["L0_attn_sin"]).T),
        w1t=np.ascontiguousarray(np.asarray(inputs["L0_ffn_w1"]).T),
        cw=np.ascontiguousarray(np.asarray(inputs["L0_ffn_conv_w"]).reshape(HC, 128, 3)
                                .transpose(1, 0, 2).reshape(128, HC * 3)),
        cb=np.ascontiguousarray(np.asarray(inputs["L0_ffn_conv_b"]).reshape(HC, 128).T),
        w2t=np.ascontiguousarray(np.asarray(inputs["L0_ffn_w2"]).T),
    )
    maps = []
    for c in range(8):
        m = dict(common)
        m["x"] = np.ascontiguousarray(x[c * NSEQ:(c + 1) * NSEQ])
        maps.append(m)
    return maps


def kernel(**inputs):
    maps = host_inputs(inputs)
    nc, _ = build_nc()
    res = run_bass_kernel_spmd(nc, maps, core_ids=list(range(8)))
    out = np.concatenate([np.asarray(r["out"]) for r in res.results], axis=0)
    return out.astype(np.float32)


# version 16
# speedup vs baseline: 1.0171x; 1.0171x over previous
import numpy as np
import concourse.bass as bass
import concourse.mybir as mybir
from concourse import library_config as _lc
_LIB = {l.name: l for l in _lc.all_libraries}
from concourse.bass_utils import run_bass_kernel_spmd
from contextlib import ExitStack

F32 = mybir.dt.float32
F16 = mybir.dt.float16
BF16 = mybir.dt.bfloat16
I32 = mybir.dt.int32
AF = mybir.ActivationFunctionType
AX = mybir.AxisListType
OP = mybir.AluOpType

H, NH, HD, T, NSEQ = 1024, 16, 64, 2048, 2
DH = 4096
NT128, NT512 = T // 128, T // 512
OC, HC, KC = H // 128, DH // 128, H // 128
NPAIR, NKT = NH // 2, T // 128
EPS = 1e-5
RM = 1536.0


class Sem:
    def __init__(self, es, nc, name):
        self.h = es.enter_context(nc.semaphore(name))
        self.v = 0

    def inc(self, ins, n=1):
        ins.then_inc(self.h, n)
        self.v += n
        return self.v

    def wait(self, eng, v=None):
        eng.wait_ge(self.h, self.v if v is None else v)


def build_nc(dbg=()):
    nc = bass.Bass("TRN2", target_bir_lowering=False)
    _nctr = [0]

    def _un(n):
        _nctr[0] += 1
        return f"{n}_{_nctr[0]}"
    x_h = nc.dram_tensor("x", [NSEQ, T, H], F32, kind="ExternalInput").ap()
    wq_h = nc.dram_tensor("wq", [H, H], F32, kind="ExternalInput").ap()
    wk_h = nc.dram_tensor("wk", [H, H], F32, kind="ExternalInput").ap()
    wv_h = nc.dram_tensor("wv", [H, H], F32, kind="ExternalInput").ap()
    wo_h = nc.dram_tensor("wo", [H, H], F32, kind="ExternalInput").ap()
    bq_h = nc.dram_tensor("bq", [128, OC], F32, kind="ExternalInput").ap()
    bv_h = nc.dram_tensor("bv", [128, OC], F32, kind="ExternalInput").ap()
    bo_h = nc.dram_tensor("bo", [H], F32, kind="ExternalInput").ap()
    cos_h = nc.dram_tensor("cosT", [HD, 1024], F32, kind="ExternalInput").ap()
    sin_h = nc.dram_tensor("sinT", [HD, 1024], F32, kind="ExternalInput").ap()
    w1t_h = nc.dram_tensor("w1t", [H, DH], F32, kind="ExternalInput").ap()
    cw_h = nc.dram_tensor("cw", [128, 3 * HC], F32, kind="ExternalInput").ap()
    cb_h = nc.dram_tensor("cb", [128, HC], F32, kind="ExternalInput").ap()
    w2t_h = nc.dram_tensor("w2t", [DH, H], F32, kind="ExternalInput").ap()
    out_h = nc.dram_tensor("out", [NSEQ, T, H], F32, kind="ExternalOutput").ap()

    wo_sp = nc.dram_tensor("wo_sp", [KC, 128, H], BF16, kind="Internal").ap()
    w1qt_sp = nc.dram_tensor("w1qt_sp", [KC, 128, DH], BF16, kind="Internal").ap()
    w2qt_sp = nc.dram_tensor("w2qt_sp", [HC, 128, H], BF16, kind="Internal").ap()
    wqkv_sp = nc.dram_tensor("wqkv_sp", [OC, 128, 3 * KC * 128], BF16,
                             kind="Internal").ap()
    xqd_sp = nc.dram_tensor("xqd_sp", [NSEQ, KC, 128, T], BF16, kind="Internal").ap()
    x2_sp = nc.dram_tensor("x2_sp", [NSEQ, T, H], F32, kind="Internal").ap()
    hs_sp = nc.dram_tensor("hs_sp", [NSEQ, HC, 128, T], BF16, kind="Internal").ap()
    c1_sp = nc.dram_tensor("c1_sp", [NSEQ, T], F32, kind="Internal").ap()
    s2_sp = nc.dram_tensor("s2_sp", [NSEQ, T], F32, kind="Internal").ap()

    V, S, TE, G, SY = nc.vector, nc.scalar, nc.tensor, nc.gpsimd, nc.sync

    es = ExitStack()
    sbuf = lambda n, shape, d=F32: es.enter_context(nc.sbuf_tensor(n, shape, d))

    dbg_outs = {}
    for name, shape, d in dbg:
        dbg_outs[name] = nc.dram_tensor(name, shape, d, kind="ExternalOutput").ap()

    with es:
        ident = sbuf("ident", [128, 128], BF16)
        identf = sbuf("identf", [128, 128], F32)
        ones1f = sbuf("ones1f", [1, 128], F32)
        cosfm = sbuf("cosfm", [128, 1024], BF16)
        sinfm = sbuf("sinfm", [128, 1024], BF16)
        bq_pre = sbuf("bq_pre", [128, OC], F32)
        bv_pre = sbuf("bv_pre", [128, OC], F32)
        bo_row = sbuf("bo_row", [1, H], BF16)
        ones_row = sbuf("ones_row", [1, 128], BF16)
        ones2 = sbuf("ones2", [128, 2], BF16)
        convw_s = sbuf("convw_s", [128, 3 * HC], F32)
        convb_s = sbuf("convb_s", [128, HC], F32)
        negM = sbuf("negM", [128, NPAIR], F32)
        negRM = sbuf("negRM", [128, 1], F32)
        posRM = sbuf("posRM", [128, 1], F32)
        wsc = sbuf("wsc", [128, 4], F32)
        stat = sbuf("stat", [128, 64], F32)
        bnbuf = sbuf("bnbuf", [128, 12], F32)
        nrm = sbuf("nrm", [128, 24], F32)
        s2s = sbuf("s2s", [128, NT128], F32)
        c1sb = sbuf("c1sb", [128, NT128], F32)
        w3_es = ExitStack()
        wo3 = w3_es.enter_context(nc.sbuf_tensor(_un("wo3"), [128, KC * H], BF16))
        qres = w3_es.enter_context(nc.sbuf_tensor(_un("qres"), [128, OC * T], BF16))
        kres = w3_es.enter_context(nc.sbuf_tensor(_un("kres"), [128, OC * T], BF16))
        vres = w3_es.enter_context(
            nc.sbuf_tensor(_un("vres"), [128, NPAIR * NKT * 130], BF16))

        sems = {}
        for name in ["dc", "drl", "tv", "ts", "tt", "tg",
                     "dxr0", "dxr1", "dxr2", "dxr3", "dst0", "dst1", "dss0", "dss1",
                     "dsq0", "dsq1", "dsk0", "dsk1", "dsd0", "dsd1",
                     "datt0", "datt1", "dtb0", "dtb1", "dwo0", "dwo1",
                     "dx20", "dx21", "dqs0", "dqs1", "dcr", "dhs",
                     "dr20", "dr21", "dxa0", "dxa1", "dou0", "dou1", "dsw"]:
            sems[name] = Sem(es, nc, name)
        dc, drl = sems["dc"], sems["drl"]
        sv, ss, st, sg = (sems[k] for k in ["tv", "ts", "tt", "tg"])
        dxr = [sems["dxr0"], sems["dxr1"], sems["dxr2"], sems["dxr3"]]
        dstg = [sems["dst0"], sems["dst1"]]
        dss = [sems["dss0"], sems["dss1"]]
        dsq = [sems["dsq0"], sems["dsq1"]]
        dsk = [sems["dsk0"], sems["dsk1"]]
        dsd = [sems["dsd0"], sems["dsd1"]]
        datt = [sems["datt0"], sems["datt1"]]
        dtb = [sems["dtb0"], sems["dtb1"]]
        dwo = [sems["dwo0"], sems["dwo1"]]
        dx2 = [sems["dx20"], sems["dx21"]]
        dqs = [sems["dqs0"], sems["dqs1"]]
        dcr, dhs = sems["dcr"], sems["dhs"]
        dr2 = [sems["dr20"], sems["dr21"]]
        dxa = [sems["dxa0"], sems["dxa1"]]
        dou = [sems["dou0"], sems["dou1"]]
        dsw = sems["dsw"]
        for s_ in sems.values():
            SY.sem_clear(s_.h)
        nc.all_engine_barrier()

        def bar():
            nc.all_engine_barrier()

        def ln_stats(src_ap, so):
            V.bn_stats(bnbuf[:, 0:6], src_ap[:, 0:512])
            sv.inc(V.bn_stats(bnbuf[:, 6:12], src_ap[:, 512:1024]))
            sv.wait(V)
            sv.inc(V.bn_aggr(stat[:, so:so + 2], bnbuf[:, 0:12]))
            sv.wait(V)
            sv.inc(V.tensor_scalar(stat[:, so + 2:so + 3], stat[:, so + 1:so + 2],
                                   EPS, None, op0=OP.add))
            sv.wait(V)
            return sv.inc(V.reciprocal(stat[:, so + 3:so + 4],
                                       stat[:, so + 2:so + 3]))

        def ln_fin(so, v2):
            sv.wait(S, v2)
            s1 = ss.inc(S.activation(stat[:, so + 2:so + 3], stat[:, so + 3:so + 4],
                                     AF.Sqrt))
            ss.wait(V, s1)
            sv.wait(V)
            mbv = sv.inc(V.tensor_scalar(stat[:, so + 3:so + 4], stat[:, so:so + 1],
                                         stat[:, so + 2:so + 3], -1.0,
                                         op0=OP.mult, op1=OP.mult))
            return stat[:, so + 2:so + 3], stat[:, so + 3:so + 4], mbv

        def ln_chain(src_ap, so):
            return ln_fin(so, ln_stats(src_ap, so))

        _ph = ExitStack()
        it_f = _ph.enter_context(nc.sbuf_tensor(_un("it_f"), [128, 128], I32))
        it_p = _ph.enter_context(nc.sbuf_tensor(_un("it_p"), [128, 1], F32))
        trigA = _ph.enter_context(nc.sbuf_tensor(_un("trigA"), [64, 1024], F32))
        trigB = _ph.enter_context(nc.sbuf_tensor(_un("trigB"), [64, 1024], F32))
        borowf = _ph.enter_context(nc.sbuf_tensor(_un("borowf"), [1, H], F32))
        wstg0 = _ph.enter_context(nc.sbuf_tensor(_un("wstg0"), [128, DH], F32))
        wstg1 = _ph.enter_context(nc.sbuf_tensor(_un("wstg1"), [128, DH], F32))
        qh16 = _ph.enter_context(nc.sbuf_tensor(_un("qh16"), [128, DH], F16))
        wqa = _ph.enter_context(nc.sbuf_tensor(_un("wqa"), [128, DH], BF16))
        wqb = _ph.enter_context(nc.sbuf_tensor(_un("wqb"), [128, DH], BF16))
        pst = _ph.enter_context(nc.psum_tensor(_un("pst"), [128, 128], F32))
        with _ph:
            stg = [wstg0, wstg1]
            wqx = [wqa, wqb]
            G.iota(it_f[:], pattern=[[1, 128]], base=0, channel_multiplier=0)
            sg.inc(G.iota(it_p[:], pattern=[[0, 1]], base=0, channel_multiplier=1,
                   allow_small_or_imprecise_dtypes=True))
            sg.wait(V)
            V.tensor_scalar(ident[:], it_f[:], it_p[:], None, op0=OP.is_equal)
            V.tensor_scalar(identf[:], it_f[:], it_p[:], None, op0=OP.is_equal)
            V.memset(ones_row[:], 1.0)
            V.memset(ones1f[:], 1.0)
            V.memset(negRM[:], -RM)
            V.memset(posRM[:], RM)
            sv.inc(V.memset(ones2[:], 0.0))
            sv.wait(V)
            V.memset(ones2[0:64, 0:1], 1.0)
            sv.inc(V.memset(ones2[64:128, 1:2], 1.0))

            dc.inc(SY.dma_start(stat[:, 0:OC], bq_h), 16)
            dc.inc(SY.dma_start(bv_pre[:], bv_h), 16)
            dc.inc(SY.dma_start(borowf[:], bo_h[None, :]), 16)
            dc.inc(SY.dma_start(convw_s[:], cw_h), 16)
            dc.inc(SY.dma_start(convb_s[:], cb_h), 16)
            dc.inc(SY.dma_start(trigA[:], cos_h), 16)
            dc.inc(SY.dma_start(trigB[:], sin_h), 16)
            dc.wait(V)
            sv.inc(V.tensor_scalar(bq_pre[:], stat[:, 0:OC], 0.125, None, op0=OP.mult))
            sv.inc(V.tensor_copy(bo_row[:], borowf[:]))
            V.tensor_copy(cosfm[0:64, :], trigA[:])
            V.tensor_copy(cosfm[64:128, :], trigA[:])
            V.tensor_copy(sinfm[0:64, :], trigB[:])
            sv.inc(V.tensor_copy(sinfm[64:128, :], trigB[:]))

            wsrc = [wq_h, wk_h, wv_h]
            castv = []
            wspill = [0, 0]
            for i, (wh, b) in enumerate([(w, k) for w in range(3) for k in range(2)]):
                eng = SY if i % 2 == 0 else G
                if i >= 2:
                    sv.wait(eng, castv[i - 2])
                lv = dstg[i % 2].inc(eng.dma_start(
                    stg[i % 2][:, 0:4 * H].rearrange("p (c n) -> p c n", n=H),
                    wsrc[wh][b * 512:(b + 1) * 512, :].rearrange(
                        "(c p) n -> p c n", p=128)), 16)
                dstg[i % 2].wait(V, lv)
                if i >= 2:
                    dss[i % 2].wait(V, wspill[i % 2])
                castv.append(sv.inc(V.tensor_copy(
                    wqx[i % 2][:, 0:4 * H], stg[i % 2][:, 0:4 * H])))
                sv.wait(G, castv[-1])
                for k4 in range(4):
                    tix = wh * KC + b * 4 + k4
                    wspill[i % 2] = dss[i % 2].inc(G.dma_start(
                        wqkv_sp[:, :, tix * 128:(tix + 1) * 128].rearrange(
                            "o p n -> p o n"),
                        wqx[i % 2][:, k4 * H:(k4 + 1) * H].rearrange(
                            "p (o n) -> p o n", n=128)), 16)
            for b in range(2):
                sl = b % 2
                eng = SY if sl == 0 else G
                sv.wait(eng, castv[-2])
                lv = dstg[sl].inc(eng.dma_start(
                    stg[sl][:, 0:4 * H].rearrange("p (c n) -> p c n", n=H),
                    wo_h[b * 512:(b + 1) * 512, :].rearrange(
                        "(c p) n -> p c n", p=128)), 16)
                dstg[sl].wait(V, lv)
                castv.append(sv.inc(V.tensor_copy(
                    wo3[:, b * 4 * H:(b + 1) * 4 * H], stg[sl][:, 0:4 * H])))

            def quant_weight(src_h, nchunk, width, sw_col, dst_sp, mwcol=None):
                nb = DH // width
                nbatch = nchunk // nb
                bw = nb * width
                redv = []
                sv.wait(SY)
                sv.wait(G)
                for c in range(nbatch):
                    if c >= 2:
                        sv.wait(SY, redv[c - 2])
                        sv.wait(G, redv[c - 2])
                    eng = SY if c % 2 == 0 else G
                    lv = dstg[c % 2].inc(eng.dma_start(
                        stg[c % 2][:, 0:bw].rearrange("p (c n) -> p c n", n=width),
                        src_h[c * nb * 128:(c + 1) * nb * 128, :].rearrange(
                            "(c p) n -> p c n", p=128)), 16)
                    dstg[c % 2].wait(V, lv)
                    redv.append(sv.inc(V.tensor_reduce(
                        nrm[:, c % 16:c % 16 + 1], stg[c % 2][:, 0:bw],
                        axis=AX.X, op=OP.add, apply_absolute_value=True)))
                sv.wait(V)
                sv.inc(V.tensor_reduce(nrm[:, 20:21], nrm[:, 0:nbatch],
                                       axis=AX.X, op=OP.add))
                sv.wait(TE)
                tq1 = st.inc(TE.matmul(pst[0:1, 0:128], nrm[:, 20:21], identf[:],
                                       start=True, stop=True))
                st.wait(V, tq1)
                sv.inc(V.tensor_reduce(nrm[0:1, 21:22], pst[0:1, 0:128],
                                       axis=AX.X, op=OP.add))
                sv.wait(TE)
                tq2 = st.inc(TE.matmul(pst[:, 0:1], ones1f[:], nrm[0:1, 21:22],
                                       start=True, stop=True))
                st.wait(V, tq2)
                sv.inc(V.tensor_copy(nrm[:, 21:22], pst[:, 0:1]))
                sv.wait(V)
                sv.inc(V.reciprocal(nrm[:, 22:23], nrm[:, 21:22]))
                sv.wait(V)
                nel = float(nchunk * 128 * width)
                sv.inc(V.tensor_scalar(wsc[:, sw_col:sw_col + 1], nrm[:, 22:23], nel,
                                       None, op0=OP.mult))
                if mwcol is not None:
                    sv.inc(V.tensor_scalar(wsc[:, mwcol:mwcol + 1], nrm[:, 21:22],
                                           1.0 / (nel * 127.0), None, op0=OP.mult))
                sv.wait(V)
                qv = [0, 0]
                qst = [0, 0]
                sv.wait(SY)
                sv.wait(G)
                dss[0].wait(V)
                dss[1].wait(V)
                for c in range(nbatch):
                    sl = c % 2
                    if c >= 2:
                        sv.wait(SY, qv[sl])
                        sv.wait(G, qv[sl])
                        dss[sl].wait(V, qst[sl])
                    eng = SY if c % 2 == 0 else G
                    lv = dstg[sl].inc(eng.dma_start(
                        stg[sl][:, 0:bw].rearrange("p (c n) -> p c n", n=width),
                        src_h[c * nb * 128:(c + 1) * nb * 128, :].rearrange(
                            "(c p) n -> p c n", p=128)), 16)
                    dstg[sl].wait(V, lv)
                    sv.inc(V.tensor_scalar(qh16[:, 0:bw], stg[sl][:, 0:bw],
                                           wsc[:, sw_col:sw_col + 1], RM,
                                           op0=OP.mult, op1=OP.add))
                    sv.wait(V)
                    sv.inc(V.tensor_scalar(wqx[sl][:, 0:bw], qh16[:, 0:bw],
                                           -RM, -1.0, op0=OP.add, op1=OP.max))
                    sv.wait(V)
                    qv[sl] = sv.inc(V.tensor_scalar(wqx[sl][:, 0:bw], wqx[sl][:, 0:bw],
                                                    1.0, None, op0=OP.min))
                    sv.wait(G, qv[sl])
                    qst[sl] = dss[sl].inc(G.dma_start(
                        dst_sp[c * nb:(c + 1) * nb].rearrange("c p n -> p c n"),
                        wqx[sl][:, 0:bw].rearrange("p (c n) -> p c n", n=width)), 16)

            quant_weight(w1t_h, KC, DH, 0, w1qt_sp, mwcol=1)
            quant_weight(w2t_h, HC, H, 2, w2qt_sp)
            for s_ in (dstg[0], dstg[1], dss[0], dss[1], dc):
                s_.wait(SY)
        bar()

        _ph = ExitStack()
        attA = _ph.enter_context(nc.sbuf_tensor(_un("attA"), [128, OC * T], BF16))
        with _ph:
            A3 = attA[:, :].rearrange("p (c n) -> p c n", n=T)

            for s in range(NSEQ):
                _ph = ExitStack()
                xr = [_ph.enter_context(nc.sbuf_tensor(_un("xra"), [128, H], F32))
                      for _ in range(3)]
                ln1b0 = _ph.enter_context(nc.sbuf_tensor(_un("ln1b0"), [128, H], BF16))
                ln1b1 = _ph.enter_context(nc.sbuf_tensor(_un("ln1b1"), [128, H], BF16))
                ptrA = _ph.enter_context(nc.psum_tensor(_un("ptrA"), [128, 512], F32))
                ptrB = _ph.enter_context(nc.psum_tensor(_un("ptrB"), [128, 512], F32))
                with _ph:
                    ln1b = [ln1b0, ln1b1]
                    ptr = [ptrA, ptrB]
                    xload = [dxr[i].inc(SY.dma_start(
                        xr[i][:], x_h[s, i * 128:(i + 1) * 128, :]), 16) for i in range(3)]
                    tcv = [0, 0]
                    lnuse = [0, 0]
                    for i in range(NT128):
                        dxr[i % 3].wait(V, xload[i])
                        r_ap, mb_ap, mbv_ = ln_chain(xr[i % 3][:], (i % 4) * 4)
                        sv.wait(S, mbv_)
                        if i >= 2:
                            st.wait(S, lnuse[i % 2])
                        sev = ss.inc(S.activation(ln1b[i % 2][:], xr[i % 3][:], AF.Identity,
                                                  scale=r_ap, bias=mb_ap))
                        if i + 3 < NT128:
                            ss.wait(SY, sev)
                            xload.append(dxr[(i + 3) % 3].inc(SY.dma_start(
                                xr[(i + 3) % 3][:], x_h[s, (i + 3) * 128:(i + 4) * 128, :]), 16))
                        ss.wait(TE, sev)
                        for half in range(2):
                            sv.wait(TE, tcv[half])
                            for k4 in range(4):
                                kc = half * 4 + k4
                                _ins = TE.matmul(ptr[half][:, k4 * 128:(k4 + 1) * 128],
                                          ln1b[i % 2][:, kc * 128:(kc + 1) * 128],
                                          ident[:], start=True, stop=True)
                            tev = st.inc(_ins)
                            lnuse[i % 2] = tev
                            st.wait(V, tev)
                            tcv[half] = sv.inc(V.tensor_copy(
                                A3[:, half * 4:half * 4 + 4, i * 128:(i + 1) * 128],
                                ptr[half][:, :].rearrange("p (c n) -> p c n", n=128)))
                bar()

                _ph = ExitStack()
                wsb0 = _ph.enter_context(
                    nc.sbuf_tensor(_un("wsb0"), [128, 3 * KC * 128], BF16))
                wsb1 = _ph.enter_context(
                    nc.sbuf_tensor(_un("wsb1"), [128, 3 * KC * 128], BF16))
                vtmp = _ph.enter_context(nc.sbuf_tensor(_un("vtmp"), [128, 512], BF16))
                rot2 = [_ph.enter_context(nc.sbuf_tensor(_un("rot2"), [128, 512], BF16))
                        for _ in range(2)]
                rt1 = _ph.enter_context(nc.sbuf_tensor(_un("rt1"), [128, 512], BF16))
                rt2 = _ph.enter_context(nc.sbuf_tensor(_un("rt2"), [128, 512], BF16))
                sqb = _ph.enter_context(nc.sbuf_tensor(_un("sqb"), [128, 512], BF16))
                prodb = _ph.enter_context(nc.sbuf_tensor(_un("prodb"), [128, 8], BF16))
                pq = _ph.enter_context(nc.psum_tensor(_un("pq"), [128, 512], F32))
                pk = _ph.enter_context(nc.psum_tensor(_un("pk"), [128, 512], F32))
                pv = _ph.enter_context(nc.psum_tensor(_un("pv"), [128, 512], F32))
                ptrV = _ph.enter_context(nc.psum_tensor(_un("ptrV"), [128, 512], F32))
                pnb = _ph.enter_context(nc.psum_tensor(_un("pnb"), [2, 512], F32))
                pnb2 = _ph.enter_context(nc.psum_tensor(_un("pnb2"), [1, 8], F32))
                pbc = _ph.enter_context(nc.psum_tensor(_un("pbc"), [128, 8], F32))
                with _ph:
                    wsb = [wsb0, wsb1]
                    if s == 0:
                        VR = vres[:, :].rearrange("p (t c) -> p t c", c=130)
                        V.memset(VR[:, :, 64:65], 1.0)
                        onesv = sv.inc(V.memset(VR[:, :, 129:130], 1.0))
                    else:
                        onesv = 0
                    evq = evk = evv = 0
                    vtuse = 0
                    trv = 0
                    nbred = 0
                    nbmm = 0
                    pend_nb = [None]

                    def flush_nb():
                        nonlocal nbred, nbmm
                        if pend_nb[0] is None:
                            return
                        oc_, sqk = pend_nb[0]
                        pend_nb[0] = None
                        ss.wait(TE, sqk)
                        sv.wait(TE, nbred)
                        TE.matmul(pnb[:, 0:256], ones2[:], sqb[:, 0:256],
                                  start=True, stop=True)
                        nbmm = st.inc(TE.matmul(pnb[:, 256:512], ones2[:],
                                                sqb[:, 256:512],
                                                start=True, stop=True))
                        st.wait(V, nbmm)
                        V.tensor_reduce(nrm[0:2, oc_:oc_ + 1], pnb[:, 0:256],
                                        axis=AX.X, op=OP.max)
                        nbred = sv.inc(V.tensor_reduce(
                            nrm[0:2, 8 + oc_:9 + oc_], pnb[:, 256:512],
                            axis=AX.X, op=OP.max))
                    wldv = [0, 0]
                    wsuse = [0, 0]
                    wldv[0] = dstg[0].inc(SY.dma_start(wsb[0][:], wqkv_sp[0]), 16)
                    for oc in range(OC):
                        par = oc % 2
                        qview = qres[:, oc * T:(oc + 1) * T]
                        kview = kres[:, oc * T:(oc + 1) * T]
                        VD4 = vres[:, oc * NKT * 130:(oc + 1) * NKT * 130].rearrange(
                            "p (t g c) -> p t g c", g=2, c=65)
                        if oc + 1 < OC:
                            sl2 = (oc + 1) % 2
                            eng = SY if sl2 == 0 else G
                            st.wait(eng, wsuse[sl2])
                            wldv[sl2] = dstg[sl2].inc(
                                eng.dma_start(wsb[sl2][:], wqkv_sp[oc + 1]), 16)
                        dstg[par].wait(TE, wldv[par])
                        for j in range(NT512):
                            ss.wait(TE, evq)
                            for kc in range(KC):
                                _ins = TE.matmul(pq[:],
                                          wsb[par][:, (0 * KC + kc) * 128:
                                                   (0 * KC + kc) * 128 + 128],
                                          A3[:, kc, j * 512:(j + 1) * 512],
                                          start=(kc == 0), stop=(kc == KC - 1))
                            mmq = st.inc(_ins)
                            ss.wait(TE, evk)
                            for kc in range(KC):
                                _ins = TE.matmul(pk[:],
                                          wsb[par][:, (1 * KC + kc) * 128:
                                                   (1 * KC + kc) * 128 + 128],
                                          A3[:, kc, j * 512:(j + 1) * 512],
                                          start=(kc == 0), stop=(kc == KC - 1))
                            mmk = st.inc(_ins)
                            ss.wait(TE, evv)
                            st.wait(TE, vtuse)
                            for kc in range(KC):
                                _ins = TE.matmul(pv[:],
                                          wsb[par][:, (2 * KC + kc) * 128:
                                                   (2 * KC + kc) * 128 + 128],
                                          A3[:, kc, j * 512:(j + 1) * 512],
                                          start=(kc == 0), stop=(kc == KC - 1))
                            mmv = st.inc(_ins)
                            wsuse[par] = mmv
                            st.wait(S, mmq)
                            evq = ss.inc(S.activation(qview[:, j * 512:(j + 1) * 512], pq[:],
                                                      AF.Identity, scale=0.125,
                                                      bias=bq_pre[:, oc:oc + 1]))
                            st.wait(S, mmk)
                            evk = ss.inc(S.activation(kview[:, j * 512:(j + 1) * 512], pk[:],
                                                      AF.Identity, scale=1.0, bias=0.0))
                            st.wait(S, mmv)
                            evv = ss.inc(S.activation(vtmp[:], pv[:], AF.Identity,
                                                      scale=1.0, bias=bv_pre[:, oc:oc + 1]))
                            ss.wait(TE, evv)
                            sv.wait(TE, trv)
                            if s == 0:
                                sv.wait(TE, onesv)
                            for i4 in range(4):
                                _ins = TE.matmul(ptrV[:, i4 * 128:(i4 + 1) * 128],
                                          vtmp[:, i4 * 128:(i4 + 1) * 128], ident[:],
                                          start=True, stop=True)
                            vtuse = st.inc(_ins)
                            st.wait(V, vtuse)
                            trv = sv.inc(V.tensor_copy(
                                VD4[:, j * 4:j * 4 + 4, :, 0:64],
                                ptrV[:, :].rearrange("p (t g c) -> p t g c", g=2, c=64)))
                        flush_nb()
                        ss.wait(V)
                        rolast = 0
                        rits = [(qview, 0, 2), (qview, 0, 3),
                                (kview, 2, 2), (kview, 2, 3)]
                        for it in range(4):
                            buf, dstoff, jj = rits[it]
                            src = buf[:, jj * 512:(jj + 1) * 512]
                            ccols = cosfm[:, (jj - 2) * 512:(jj - 1) * 512]
                            scols = sinfm[:, (jj - 2) * 512:(jj - 1) * 512]
                            sv.wait(V)
                            for hh in range(2):
                                b0 = hh * 64
                                V.tensor_scalar(rot2[0][b0:b0 + 32, :],
                                                src[b0 + 32:b0 + 64, :],
                                                -1.0, None, op0=OP.mult)
                                V.tensor_scalar(rot2[0][b0 + 32:b0 + 64, :],
                                                src[b0:b0 + 32, :],
                                                1.0, None, op0=OP.mult)
                            sv.inc(V.tensor_tensor(rt1[:], src, ccols, op=OP.mult))
                            sv.wait(V)
                            sv.inc(V.tensor_tensor(rt2[:], rot2[0][:], scols,
                                                   op=OP.mult))
                            sv.wait(V)
                            rolast = sv.inc(V.tensor_tensor(
                                src, rt1[:], rt2[:], op=OP.add))
                        st.wait(S, nbmm)
                        sv.wait(S, rolast)
                        S.activation(sqb[:, 0:128], qview[:, 0:1024:8], AF.Square)
                        S.activation(sqb[:, 128:192], qview[:, 1024:1536:8], AF.Square)
                        S.activation(sqb[:, 192:256], qview[:, 1536:2048:8], AF.Square)
                        S.activation(sqb[:, 256:384], kview[:, 0:1024:8], AF.Square)
                        S.activation(sqb[:, 384:448], kview[:, 1024:1536:8], AF.Square)
                        sq_k = ss.inc(S.activation(sqb[:, 448:512],
                                                   kview[:, 1536:2048:8], AF.Square))
                        pend_nb[0] = (oc, sq_k)
                    flush_nb()
                    sv.wait(S)
                    ss.inc(S.activation(nrm[0:2, 16:24], nrm[0:2, 0:8], AF.Sqrt))
                    ss.wait(S)
                    ssq2 = ss.inc(S.activation(nrm[0:2, 0:8], nrm[0:2, 8:16], AF.Sqrt))
                    ss.wait(V, ssq2)
                    sv.inc(V.tensor_tensor(nrm[0:2, 8:16], nrm[0:2, 16:24], nrm[0:2, 0:8],
                                           op=OP.mult))
                    sv.wait(V)
                    pv_ = sv.inc(V.tensor_copy(prodb[0:2, 0:8], nrm[0:2, 8:16]))
                    sv.wait(TE, pv_)
                    tb = st.inc(TE.matmul(pnb2[:], ones2[0:2, 0:1], prodb[0:2, 0:8],
                                          start=True, stop=True))
                    st.wait(V, tb)
                    ngv = sv.inc(V.tensor_scalar(nrm[0:1, 0:8], pnb2[:], -1.0, None, op0=OP.mult))
                    sv.wait(TE, ngv)
                    tb2 = st.inc(TE.matmul(pbc[:], ones1f[:], nrm[0:1, 0:8],
                                           start=True, stop=True))
                    st.wait(V, tb2)
                    sv.inc(V.tensor_copy(negM[:, 0:8], pbc[:]))
                bar()

                _ph = ExitStack()
                ptil0 = _ph.enter_context(nc.sbuf_tensor(_un("ptil0"), [128, 1024], BF16))
                ptil1 = _ph.enter_context(nc.sbuf_tensor(_un("ptil1"), [128, 1024], BF16))
                ptil2 = _ph.enter_context(nc.sbuf_tensor(_un("ptil2"), [128, 1024], BF16))
                tmpB0 = _ph.enter_context(nc.sbuf_tensor(_un("tmpB0"), [64, T], BF16))
                tmpB1 = _ph.enter_context(nc.sbuf_tensor(_un("tmpB1"), [64, T], BF16))
                invrA = _ph.enter_context(nc.sbuf_tensor(_un("invrA"), [1, T], F32))
                invrB = _ph.enter_context(nc.sbuf_tensor(_un("invrB"), [1, T], F32))
                lbufA = _ph.enter_context(nc.sbuf_tensor(_un("lbufA"), [1, T], F32))
                lbufB = _ph.enter_context(nc.sbuf_tensor(_un("lbufB"), [1, T], F32))
                invbA2 = _ph.enter_context(nc.sbuf_tensor(_un("invbA2"), [64, T], BF16))
                invbB2 = _ph.enter_context(nc.sbuf_tensor(_un("invbB2"), [64, T], BF16))
                sc0 = _ph.enter_context(nc.psum_tensor(_un("sc0"), [128, 1024], F32))
                sc1 = _ph.enter_context(nc.psum_tensor(_un("sc1"), [128, 1024], F32))
                accA = _ph.enter_context(nc.psum_tensor(_un("accA"), [65, 512], F32))
                accB = _ph.enter_context(nc.psum_tensor(_un("accB"), [65, 512], F32))
                pib = _ph.enter_context(nc.psum_tensor(_un("pib"), [128, 512], F32))
                with _ph:
                    ptil = [ptil0, ptil1, ptil2]
                    tmpB = [tmpB0, tmpB1]
                    sc = [sc0, sc1]
                    expv = [0, 0, 0]
                    avv3 = [0, 0, 0]
                    epiv = 0
                    bcv = 0
                    ibcp = 0
                    tailv = 0
                    erv_ = 0
                    dmaBv = [0, 0]
                    smmv = {}
                    NSTEP = 4 * NKT
                    pend = [None]

                    def qkd(p):
                        return (qres[:, p * T:(p + 1) * T],
                                kres[:, p * T:(p + 1) * T],
                                vres[:, p * NKT * 130:(p + 1) * NKT * 130]
                                .rearrange("p (t n) -> p t n", n=130))

                    def issue_smm(g):
                        p, i = divmod(g, NSTEP)
                        qc, kt = divmod(i, NKT)
                        sl = g % 2
                        qc_, kc_, _ = qkd(p)
                        TE.matmul(sc[sl][:, 0:512], kc_[0:64, kt * 128:(kt + 1) * 128],
                                  qc_[0:64, qc * 512:(qc + 1) * 512],
                                  start=True, stop=True, tile_position=(0, 0))
                        smmv[g] = st.inc(TE.matmul(
                            sc[sl][:, 512:1024], kc_[64:128, kt * 128:(kt + 1) * 128],
                            qc_[64:128, qc * 512:(qc + 1) * 512],
                            start=True, stop=True, tile_position=(64, 0)))

                    def flush_tailA(r):
                        nonlocal erv_
                        if r == 0:
                            st.wait(V, bcv)
                            sv.wait(V)
                        V.reciprocal(invrA[0:1, r * 512:(r + 1) * 512],
                                     lbufA[0:1, r * 512:(r + 1) * 512])
                        erv_ = sv.inc(V.reciprocal(
                            invrB[0:1, r * 512:(r + 1) * 512],
                            lbufB[0:1, r * 512:(r + 1) * 512]))

                    def flush_tailB():
                        nonlocal bcv, ibcp, tailv
                        ep, erv = pend[0]
                        pend[0] = None
                        sv.wait(TE, erv_)
                        for r in range(4):
                            sv.wait(TE, ibcp)
                            TE.matmul(pib[0:64, :], ones1f[0:1, 0:64],
                                      invrA[0:1, r * 512:(r + 1) * 512],
                                      start=True, stop=True, tile_position=(0, 0))
                            bcv = st.inc(TE.matmul(
                                pib[64:128, :], ones1f[0:1, 0:64],
                                invrB[0:1, r * 512:(r + 1) * 512],
                                start=True, stop=True, tile_position=(0, 64)))
                            st.wait(V, bcv)
                            V.tensor_copy(invbA2[:, r * 512:(r + 1) * 512],
                                          pib[0:64, :])
                            ibcp = sv.inc(V.tensor_copy(
                                invbB2[:, r * 512:(r + 1) * 512], pib[64:128, :]))
                        sv.wait(V)
                        V.tensor_tensor(
                            attA[0:64, ep * T:(ep + 1) * T],
                            attA[0:64, ep * T:(ep + 1) * T],
                            invbA2[:, :], op=OP.mult)
                        tailv = sv.inc(V.tensor_tensor(
                            tmpB[ep % 2][:, :], tmpB[ep % 2][:, :],
                            invbB2[:, :], op=OP.mult))
                        sv.wait(SY, tailv)
                        dmaBv[ep % 2] = dtb[ep % 2].inc(SY.dma_start(
                            attA[64:128, ep * T:(ep + 1) * T], tmpB[ep % 2][:]), 16)

                    issue_smm(0)
                    issue_smm(1)
                    for g in range(NPAIR * NSTEP):
                        p, i = divmod(g, NSTEP)
                        qc, kt = divmod(i, NKT)
                        sl = g % 2
                        if pend[0] is not None and i in (2, 10, 18, 26):
                            flush_tailA((2, 10, 18, 26).index(i))
                        if i == 2 * NKT + 4 and pend[0] is not None:
                            flush_tailB()
                        _, _, dc4 = qkd(p)
                        pl = g % 3
                        st.wait(S, smmv[g])
                        st.wait(S, avv3[pl])
                        expv[pl] = ss.inc(S.activation(
                            ptil[pl][:], sc[sl][:], AF.Exp,
                            bias=negM[:, p:p + 1], scale=1.0))
                        ss.wait(TE, expv[pl])
                        if g + 2 < NPAIR * NSTEP:
                            issue_smm(g + 2)
                        if kt == 0:
                            sv.wait(TE, epiv)
                        TE.matmul(accA[:], dc4[:, kt, 0:65], ptil[pl][:, 0:512],
                                  start=(kt == 0), stop=(kt == NKT - 1))
                        avv3[pl] = st.inc(TE.matmul(
                            accB[:], dc4[:, kt, 65:130], ptil[pl][:, 512:1024],
                            start=(kt == 0), stop=(kt == NKT - 1)))
                        if kt == NKT - 1:
                            st.wait(V, avv3[pl])
                            sv.wait(V)
                            if qc == 0 and p >= 2:
                                dtb[p % 2].wait(V, dmaBv[p % 2])
                            V.tensor_copy(lbufA[0:1, qc * 512:(qc + 1) * 512],
                                          accA[64:65, :])
                            sv.inc(V.tensor_copy(lbufB[0:1, qc * 512:(qc + 1) * 512],
                                                 accB[64:65, :]))
                            V.tensor_copy(
                                attA[0:64, p * T + qc * 512:p * T + (qc + 1) * 512],
                                accA[0:64, :])
                            epiv = sv.inc(V.tensor_copy(
                                tmpB[p % 2][:, qc * 512:(qc + 1) * 512],
                                accB[0:64, :]))
                            if qc == 3:
                                pend[0] = (p, epiv)
                    for r_ in range(4):
                        flush_tailA(r_)
                    flush_tailB()
                    dtb[0].wait(SY)
                    dtb[1].wait(SY)
                bar()

                _ph = ExitStack()
                xr = [_ph.enter_context(nc.sbuf_tensor(_un("xrb"), [128, H], F32))
                      for _ in range(4)]
                ln3t = _ph.enter_context(nc.sbuf_tensor(_un("ln3t"), [128, H], F32))
                x2t = [_ph.enter_context(nc.sbuf_tensor(_un("x2t"), [128, H], F32))
                       for _ in range(4)]
                ln2t = _ph.enter_context(nc.sbuf_tensor(_un("ln2t"), [128, H], F32))
                qx16 = _ph.enter_context(nc.sbuf_tensor(_un("qx16"), [128, H], F16))
                xqtm = _ph.enter_context(nc.sbuf_tensor(_un("xqtm"), [128, H], BF16))
                xqs0 = _ph.enter_context(nc.sbuf_tensor(_un("xqs0"), [128, 512], BF16))
                xqs1 = _ph.enter_context(nc.sbuf_tensor(_un("xqs1"), [128, 512], BF16))
                rowA = _ph.enter_context(nc.sbuf_tensor(_un("rowA"), [1, T], F32))
                pw0 = _ph.enter_context(nc.psum_tensor(_un("pw0"), [128, 1024], F32))
                pw1 = _ph.enter_context(nc.psum_tensor(_un("pw1"), [128, 1024], F32))
                pw2 = _ph.enter_context(nc.psum_tensor(_un("pw2"), [128, 1024], F32))
                pw3 = _ph.enter_context(nc.psum_tensor(_un("pw3"), [128, 1024], F32))
                with _ph:
                    xqs = [xqs0, xqs1]
                    pw = [pw0, pw1, pw2, pw3]
                    wo_use = [0, 0]
                    evacv = [0] * 4
                    xq_dma = [0, 0]
                    ln3use_s = 0
                    ln2use_s = 0
                    xqtmuse_s = 0
                    x2dma = [0] * 4
                    x2rds = [0] * 4
                    xruse = [0] * 4
                    for g in range(4):
                        for kc in range(KC):
                            if kc == 0:
                                for i in range(4):
                                    ss.wait(TE, evacv[i])
                            for i in range(4):
                                ti = g * 4 + i
                                for hh in range(2):
                                    _ins = TE.matmul(pw[i][:, hh * 512:(hh + 1) * 512],
                                              A3[:, kc, ti * 128:(ti + 1) * 128],
                                              wo3[:, kc * H + hh * 512:
                                                  kc * H + (hh + 1) * 512],
                                              start=(kc == 0), stop=False)
                            st.inc(_ins)
                        for i in range(4):
                            for hh in range(2):
                                _ins = TE.matmul(pw[i][:, hh * 512:(hh + 1) * 512],
                                          ones_row[0:1, 0:128],
                                          bo_row[0:1, hh * 512:(hh + 1) * 512],
                                          start=False, stop=True)
                        womm = st.inc(_ins)
                        xvv = [0] * 4
                        v2s = [0] * 4
                        for i in range(4):
                            ti = g * 4 + i
                            sv.wait(SY, xruse[i])
                            xvv[i] = dxr[i].inc(SY.dma_start(
                                xr[i][:], x_h[s, ti * 128:(ti + 1) * 128, :]), 16)
                            st.wait(V, womm)
                            v2s[i] = ln_stats(pw[i][:], 16 + i * 4)
                        fin3 = [ln_fin(16 + i * 4, v2s[i]) for i in range(4)]
                        for i in range(4):
                            ti = g * 4 + i
                            r3, mb3, mbv3 = fin3[i]
                            sv.wait(S, mbv3)
                            sv.wait(S, ln3use_s)
                            l3v = ss.inc(S.activation(ln3t[:], pw[i][:],
                                                      AF.Identity,
                                                      scale=r3, bias=mb3))
                            dxr[i].wait(V, xvv[i])
                            ss.wait(V, l3v)
                            dx2[i % 2].wait(V)
                            ss.wait(V, x2rds[i])
                            addv = sv.inc(V.tensor_tensor(
                                x2t[i][:], xr[i][:], ln3t[:], op=OP.add))
                            ln3use_s = addv
                            xruse[i] = addv
                            sv.wait(SY, addv)
                            x2dma[i] = dx2[i % 2].inc(SY.dma_start(
                                x2_sp[s, ti * 128:(ti + 1) * 128, :], x2t[i][:]), 16)
                        sv.wait(V)
                        for i in range(4):
                            v2s[i] = ln_stats(x2t[i][:], 32 + i * 4)
                        fin2 = [ln_fin(32 + i * 4, v2s[i]) for i in range(4)]
                        for i in range(4):
                            ti = g * 4 + i
                            qo = 48 + i * 4
                            r2, mb2, mbv2 = fin2[i]
                            sv.wait(S, mbv2)
                            sv.wait(S, ln2use_s)
                            l2v = ss.inc(S.activation(ln2t[:], x2t[i][:],
                                                      AF.Identity,
                                                      scale=r2, bias=mb2))
                            x2rds[i] = l2v
                            ss.wait(V, l2v)
                            sv.inc(V.tensor_reduce(
                                stat[:, qo:qo + 1], ln2t[:], axis=AX.X,
                                op=OP.max, apply_absolute_value=True))
                            sv.wait(V)
                            sv.inc(V.tensor_scalar(stat[:, qo + 1:qo + 2],
                                                   stat[:, qo:qo + 1], 1e-5,
                                                   None, op0=OP.max))
                            sv.wait(V)
                            sv.inc(V.reciprocal(stat[:, qo + 2:qo + 3],
                                                stat[:, qo + 1:qo + 2]))
                            sv.wait(V)
                            sv.inc(V.tensor_scalar(
                                stat[:, qo + 3:qo + 4], stat[:, qo + 2:qo + 3],
                                127.0, None, op0=OP.mult))
                            sv.inc(V.tensor_scalar(c1sb[:, ti:ti + 1],
                                                   stat[:, qo + 1:qo + 2],
                                                   wsc[:, 1:2], None, op0=OP.mult))
                            sv.wait(V)
                            st.wait(V, xqtmuse_s)
                            sv.inc(V.tensor_scalar(qx16[:], ln2t[:],
                                                   stat[:, qo + 3:qo + 4], RM,
                                                   op0=OP.mult, op1=OP.add))
                            sv.wait(V)
                            xqv = sv.inc(V.tensor_scalar(
                                xqtm[:], qx16[:], -RM, None,
                                op0=OP.add))
                            ln2use_s = xqv
                            sv.wait(TE, xqv)
                            for half in range(2):
                                dqs[half].wait(TE, xq_dma[half])
                                for k4 in range(4):
                                    kc = half * 4 + k4
                                    _ins = TE.matmul(pw[i][:, half * 512 + k4 * 128:
                                                    half * 512 + (k4 + 1) * 128],
                                              xqtm[:, kc * 128:(kc + 1) * 128],
                                              ident[:], start=True, stop=True)
                                thv = xqtmuse_s = st.inc(_ins)
                                st.wait(S, thv)
                                cpv = ss.inc(S.copy(
                                    xqs[half][:], pw[i][:, half * 512:(half + 1) * 512]))
                                ss.wait(SY, cpv)
                                xq_dma[half] = dqs[half].inc(SY.dma_start(
                                    xqd_sp[s][half * 4:half * 4 + 4, :,
                                              ti * 128:(ti + 1) * 128].rearrange(
                                                  "c p n -> p c n"),
                                    xqs[half][:, :].rearrange("p (c n) -> p c n", n=128)), 16)
                            evacv[i] = ss.v
                    sv.wait(TE)
                    dqs[0].wait(TE, xq_dma[0])
                    dqs[1].wait(TE, xq_dma[1])
                    for ti in range(NT128):
                        dst = pw0 if ti < 8 else pw1
                        _ins = TE.matmul(dst[0:1, (ti % 8) * 128:(ti % 8 + 1) * 128],
                                  c1sb[:, ti:ti + 1], identf[:], start=True, stop=True)
                    tcv2 = st.inc(_ins)
                    st.wait(V, tcv2)
                    V.tensor_copy(rowA[0:1, 0:1024], pw0[0:1, :])
                    rv2 = sv.inc(V.tensor_copy(rowA[0:1, 1024:2048], pw1[0:1, :]))
                    sv.wait(SY, rv2)
                    dcr.inc(SY.dma_start(c1_sp[s][None, :], rowA[:]), 16)
                    for s_ in (dcr, dx2[0], dx2[1], dqs[0], dqs[1]):
                        s_.wait(SY)
                bar()

        w3_es.close()
        bar()
        _ph = ExitStack()
        w1q = _ph.enter_context(nc.sbuf_tensor(_un("w1q"), [128, KC * DH], BF16))
        xqd = _ph.enter_context(nc.sbuf_tensor(_un("xqd"), [128, KC * T], BF16))
        diag = _ph.enter_context(nc.sbuf_tensor(_un("diag"), [128, 3 * HC * 128], BF16))
        c1f = _ph.enter_context(nc.sbuf_tensor(_un("c1f"), [128, T], F32))
        crow = _ph.enter_context(nc.sbuf_tensor(_un("crow"), [1, T], F32))
        hsc2 = [_ph.enter_context(nc.sbuf_tensor(_un("hsc2"), [128, T + 2], BF16))
                for _ in range(2)]
        hsb = _ph.enter_context(nc.sbuf_tensor(_un("hsb"), [128, T], BF16))
        macc = _ph.enter_context(nc.sbuf_tensor(_un("macc"), [128, T], BF16))
        mincc = _ph.enter_context(nc.sbuf_tensor(_un("mincc"), [128, T], BF16))
        s2row = _ph.enter_context(nc.sbuf_tensor(_un("s2row"), [1, T], F32))
        pwA = _ph.enter_context(nc.psum_tensor(_un("pwA"), [128, 512], F32))
        pwB = _ph.enter_context(nc.psum_tensor(_un("pwB"), [128, 512], F32))
        pcA = _ph.enter_context(nc.psum_tensor(_un("pcA"), [128, 512], F32))
        pcB = _ph.enter_context(nc.psum_tensor(_un("pcB"), [128, 512], F32))
        pam = _ph.enter_context(nc.psum_tensor(_un("pam"), [128, 256], F32))
        prow = _ph.enter_context(nc.psum_tensor(_un("prow"), [1, 512], F32))
        pbc1 = _ph.enter_context(nc.psum_tensor(_un("pbc1"), [128, 512], F32))
        with _ph:
            pwx = [pwA, pwB]
            pcx = [pcA, pcB]
            w1ld = 0
            for kc in range(KC):
                w1ld = drl.inc(G.dma_start(w1q[:, kc * DH:(kc + 1) * DH], w1qt_sp[kc]), 16)
            dgv = 0
            for c in range(HC):
                for j3 in range(3):
                    dgv = sv.inc(V.tensor_scalar(
                        diag[:, (c * 3 + j3) * 128:(c * 3 + j3 + 1) * 128],
                        ident[:], convw_s[:, c * 3 + j3:c * 3 + j3 + 1], None, op0=OP.mult))
            V.memset(hsc2[0][:, 0:1], 0.0)
            V.memset(hsc2[0][:, T + 1:T + 2], 0.0)
            V.memset(hsc2[1][:, 0:1], 0.0)
            hscz = sv.inc(V.memset(hsc2[1][:, T + 1:T + 2], 0.0))
            spillv = 0
            mm_prev = 0
            hscuse = [0, 0]
            for s in range(NSEQ):
                st.wait(G)
                sv.wait(G)
                st.wait(V)
                for kc in range(KC):
                    drl.inc(G.dma_start(xqd[:, kc * T:(kc + 1) * T], xqd_sp[s, kc]), 16)
                drl.inc(G.dma_start(crow[:], c1_sp[s][None, :]), 16)
                drl.wait(TE)
                sv.wait(S)
                c1bv = 0
                for jbc in range(NT512):
                    if jbc > 0:
                        ss.wait(TE, c1bv)
                    tcb = st.inc(TE.matmul(pbc1[:], ones1f[:],
                                           crow[0:1, jbc * 512:(jbc + 1) * 512],
                                           start=True, stop=True))
                    st.wait(S, tcb)
                    c1bv = ss.inc(S.copy(c1f[:, jbc * 512:(jbc + 1) * 512], pbc1[:]))
                V.memset(macc[:], -3e38)
                mmi = sv.inc(V.memset(mincc[:], 3e38))
                drl.wait(TE)
                pwv = [0, 0]
                silv = [0, 0]
                hval = {}

                def w1_block(c):
                    nonlocal mmi
                    hs_ = hsc2[c % 2]
                    for j in range(NT512):
                        slw = j % 2
                        sv.wait(TE, pwv[slw])
                        for kc in range(KC):
                            _ins = TE.matmul(pwx[slw][:],
                                      w1q[:, kc * DH + c * 128:kc * DH + (c + 1) * 128],
                                      xqd[:, kc * T + j * 512:kc * T + (j + 1) * 512],
                                      start=(kc == 0), stop=(kc == KC - 1))
                        wmm = st.inc(_ins)
                        st.wait(V, wmm)
                        ss.wait(V, c1bv)
                        sv.wait(V, hscz)
                        st.wait(V, hscuse[c % 2])
                        pwv[slw] = sv.inc(V.tensor_tensor(
                            hs_[:, 1 + j * 512:1 + (j + 1) * 512], pwx[slw][:],
                            c1f[:, j * 512:(j + 1) * 512], op=OP.mult))
                    hval[c] = sv.v

                def conv_block(c):
                    nonlocal mmi, mm_prev, spillv
                    hs_ = hsc2[c % 2]
                    for j in range(NT512):
                        slc = j % 2
                        sv.wait(TE, hval[c])
                        if c == 0:
                            sv.wait(TE, dgv)
                        ss.wait(TE, silv[slc])
                        for tap in range(3):
                            _ins = TE.matmul(pcx[slc][:],
                                      diag[:, (c * 3 + tap) * 128:(c * 3 + tap + 1) * 128],
                                      hs_[:, tap + j * 512:tap + j * 512 + 512],
                                      start=(tap == 0), stop=(tap == 2))
                        cmm = st.inc(_ins)
                        hscuse[c % 2] = cmm
                        st.wait(S, cmm)
                        dhs.wait(S, spillv)
                        sv.wait(S, mm_prev)
                        silv[slc] = ss.inc(S.activation(hsb[:, j * 512:(j + 1) * 512],
                                                        pcx[slc][:], AF.Silu,
                                                        bias=convb_s[:, c:c + 1], scale=1.0))
                    ss.wait(V, silv[(NT512 - 1) % 2])
                    sv.wait(V, mmi)
                    V.tensor_tensor(macc[:], macc[:], hsb[:], op=OP.max)
                    mmi = mm_prev = sv.inc(V.tensor_tensor(mincc[:], mincc[:], hsb[:], op=OP.min))
                    ss.wait(SY, silv[(NT512 - 1) % 2])
                    spillv = dhs.inc(SY.dma_start(hs_sp[s, c], hsb[:]), 16)

                w1_block(0)
                for c in range(HC):
                    if c + 1 < HC:
                        w1_block(c + 1)
                    conv_block(c)
                sv.wait(TE, mmi)
                pamv = 0
                for ti in range(NT128):
                    sv.wait(TE, pamv)
                    TE.matmul(pam[:, 0:128], macc[:, ti * 128:(ti + 1) * 128], ident[:],
                              start=True, stop=True)
                    tav = st.inc(TE.matmul(pam[:, 128:256], mincc[:, ti * 128:(ti + 1) * 128],
                                           ident[:], start=True, stop=True))
                    st.wait(V, tav)
                    V.tensor_reduce(stat[:, 40:41], pam[:, 0:128], axis=AX.X, op=OP.max)
                    pamv = sv.inc(V.tensor_reduce(stat[:, 41:42], pam[:, 128:256], axis=AX.X,
                                                  op=OP.min, negate=True))
                    sv.wait(V)
                    sv.inc(V.tensor_tensor(stat[:, 42:43], stat[:, 40:41], stat[:, 41:42],
                                           op=OP.max))
                    sv.wait(V)
                    sv.inc(V.tensor_scalar(stat[:, 43:44], stat[:, 42:43], 1e-5, None,
                                           op0=OP.max))
                    sv.wait(V)
                    sv.inc(V.reciprocal(stat[:, 44:45], stat[:, 43:44]))
                    sv.wait(V)
                    sv.inc(V.tensor_scalar(s2s[:, ti:ti + 1], stat[:, 44:45], 127.0,
                                           None, op0=OP.mult))
                sv.wait(TE)
                dsw.wait(V)
                srv = 0
                for tg in range(4):
                    if tg > 0:
                        sv.wait(TE, srv)
                    for t4 in range(4):
                        ti = tg * 4 + t4
                        _ins = TE.matmul(prow[0:1, t4 * 128:(t4 + 1) * 128], s2s[:, ti:ti + 1],
                                  identf[:], start=True, stop=True)
                    trv2 = st.inc(_ins)
                    st.wait(V, trv2)
                    srv = sv.inc(V.tensor_copy(s2row[0:1, tg * 512:(tg + 1) * 512], prow[0:1, :]))
                sv.wait(SY, srv)
                dsw.inc(SY.dma_start(s2_sp[s][None, :], s2row[:]), 16)
            dhs.wait(SY)
            dsw.wait(SY)
            drl.wait(SY)
        bar()

        _ph = ExitStack()
        w2q = _ph.enter_context(nc.sbuf_tensor(_un("w2q"), [128, HC * H], BF16))
        ring0 = _ph.enter_context(nc.sbuf_tensor(_un("ring0"), [128, HC * 512], BF16))
        ring1 = _ph.enter_context(nc.sbuf_tensor(_un("ring1"), [128, HC * 512], BF16))
        srow0 = _ph.enter_context(nc.sbuf_tensor(_un("srow0"), [1, T], F32))
        srow1 = _ph.enter_context(nc.sbuf_tensor(_un("srow1"), [1, T], F32))
        s2b0 = _ph.enter_context(nc.sbuf_tensor(_un("s2b0"), [128, 512], BF16))
        s2b1 = _ph.enter_context(nc.sbuf_tensor(_un("s2b1"), [128, 512], BF16))
        m216a = _ph.enter_context(nc.sbuf_tensor(_un("m216a"), [128, 512], F16))
        m216b = _ph.enter_context(nc.sbuf_tensor(_un("m216b"), [128, 512], F16))
        q216a = _ph.enter_context(nc.sbuf_tensor(_un("q216a"), [128, 512], F16))
        q216b = _ph.enter_context(nc.sbuf_tensor(_un("q216b"), [128, 512], F16))
        ln4t = _ph.enter_context(nc.sbuf_tensor(_un("ln4t"), [128, H], F32))
        x2r0 = _ph.enter_context(nc.sbuf_tensor(_un("x2r0"), [128, H], F32))
        x2r1 = _ph.enter_context(nc.sbuf_tensor(_un("x2r1"), [128, H], F32))
        outt0 = _ph.enter_context(nc.sbuf_tensor(_un("outt0"), [128, H], F32))
        outt1 = _ph.enter_context(nc.sbuf_tensor(_un("outt1"), [128, H], F32))
        p2a = _ph.enter_context(nc.psum_tensor(_un("p2a"), [128, 1024], F32))
        p2b = _ph.enter_context(nc.psum_tensor(_un("p2b"), [128, 1024], F32))
        pbs = _ph.enter_context(nc.psum_tensor(_un("pbs"), [128, 512], F32))
        with _ph:
            ring = [ring0, ring1]
            s2b = [s2b0, s2b1]
            x2r = [x2r0, x2r1]
            outt = [outt0, outt1]
            p2 = [p2a, p2b]
            for c in range(HC):
                drl.inc(G.dma_start(w2q[:, c * H:(c + 1) * H], w2qt_sp[c]), 16)
            ringuse = [0, 0]
            s2use = [0, 0]
            addv2 = [0, 0]
            rsubv = [0, 0]
            outdma = [0, 0]
            x2v_use = [0, 0]
            evv_ = [0, 0]
            pbsuse = 0
            srow = [srow0, srow1]
            srldv = [0, 0]
            sj = [(s, j) for s in range(NSEQ) for j in range(NT512)]
            NSJ = len(sj)
            ringldv = {}
            rndall_v = {}

            def load_ring(idx):
                s_, j_ = sj[idx]
                slr_ = idx % 2
                if j_ == 0:
                    drl.wait(G)
                    srldv[s_] = drl.inc(
                        G.dma_start(srow[s_][:], s2_sp[s_][None, :]), 16)
                st.wait(SY, ringuse[slr_])
                ringldv[idx] = dr2[slr_].inc(SY.dma_start(
                    ring[slr_][:, :].rearrange("p (c n) -> p c n", n=512),
                    hs_sp[s_][:, :, j_ * 512:(j_ + 1) * 512].rearrange(
                        "c p n -> p c n")), 16)

            def quant(idx):
                nonlocal pbsuse
                s_, j_ = sj[idx]
                slr_ = idx % 2
                drl.wait(TE, srldv[s_])
                ss.wait(TE, pbsuse)
                tb4 = st.inc(TE.matmul(pbs[:], ones1f[:],
                                       srow[s_][0:1, j_ * 512:(j_ + 1) * 512],
                                       start=True, stop=True))
                st.wait(S, tb4)
                sv.wait(S, s2use[slr_])
                s2bv = pbsuse = ss.inc(S.copy(s2b[slr_][:], pbs[:]))
                ss.wait(V, s2bv)
                dr2[slr_].wait(V, ringldv[idx])
                q216 = [q216a, q216b]
                m216 = [m216a, m216b]
                mvs = {}

                def qmult(c_):
                    slt_ = c_ % 2
                    ss.wait(V, addv2[slt_])
                    mvs[c_] = sv.inc(V.tensor_tensor(
                        m216[slt_][:], ring[slr_][:, c_ * 512:(c_ + 1) * 512],
                        s2b[slr_][:], op=OP.mult))

                qmult(0)
                for c in range(HC):
                    slt = c % 2
                    if c + 1 < HC:
                        qmult(c + 1)
                    sv.wait(S, mvs[c])
                    sv.wait(S, rsubv[slt])
                    addv2[slt] = ss.inc(S.activation(
                        q216[slt][:], m216[slt][:], AF.Identity,
                        bias=posRM[:, 0:1], scale=1.0))
                    ss.wait(V, addv2[slt])
                    rsubv[slt] = sv.inc(V.tensor_scalar(
                        ring[slr_][:, c * 512:(c + 1) * 512], q216[slt][:],
                        -RM, None, op0=OP.add))
                s2use[slr_] = sv.v
                rndall_v[idx] = sv.v

            load_ring(0)
            quant(0)
            for idx in range(NSJ):
                s, j = sj[idx]
                slr = idx % 2
                if idx + 1 < NSJ:
                    load_ring(idx + 1)
                    quant(idx + 1)
                rndall = rndall_v[idx]
                for i2 in range(2):
                    for half in range(2):
                            ti = j * 4 + i2 * 2 + half
                            pp = p2[half]
                            sv.wait(TE, rndall)
                            sv.wait(TE, evv_[half])
                            for c in range(HC):
                                base = c * 512 + (i2 * 2 + half) * 128
                                for hh in range(2):
                                    _ins = TE.matmul(pp[:, hh * 512:(hh + 1) * 512],
                                              ring[slr][:, base:base + 128],
                                              w2q[:, c * H + hh * 512:c * H + (hh + 1) * 512],
                                              start=(c == 0), stop=(c == HC - 1))
                            mmv_ = st.inc(_ins)
                            ringuse[slr] = mmv_
                            sv.wait(SY, x2v_use[half])
                            xv = dxa[half].inc(SY.dma_start(
                                x2r[half][:], x2_sp[s, ti * 128:(ti + 1) * 128, :]), 16)
                            st.wait(V, mmv_)
                            r4, mb4, mbv4 = ln_chain(pp[:], 48 + half * 4)
                            sv.wait(S, mbv4)
                            l4v = ss.inc(S.activation(ln4t[:], pp[:], AF.Identity,
                                                      scale=r4, bias=mb4))
                            dxa[half].wait(V, xv)
                            ss.wait(V, l4v)
                            dou[half].wait(V, outdma[half])
                            ov = sv.inc(V.tensor_tensor(outt[half][:], x2r[half][:], ln4t[:],
                                                        op=OP.add))
                            evv_[half] = ov
                            x2v_use[half] = ov
                            sv.wait(SY, ov)
                            outdma[half] = dou[half].inc(SY.dma_start(
                                out_h[s, ti * 128:(ti + 1) * 128, :], outt[half][:]), 16)
            dou[0].wait(SY)
            dou[1].wait(SY)
        bar()

    return nc, dbg_outs


def host_inputs(inputs):
    x = np.ascontiguousarray(inputs["L0_attn_hidden_states"])
    common = dict(
        wq=np.ascontiguousarray(inputs["L0_attn_wq"]),
        wk=np.ascontiguousarray(inputs["L0_attn_wk"]),
        wv=np.ascontiguousarray(inputs["L0_attn_wv"]),
        wo=np.ascontiguousarray(inputs["L0_attn_wo"]),
        bq=np.ascontiguousarray(np.asarray(inputs["L0_attn_bq"]).reshape(OC, 128).T),
        bv=np.ascontiguousarray(np.asarray(inputs["L0_attn_bv"]).reshape(OC, 128).T),
        bo=np.ascontiguousarray(inputs["L0_attn_bo"]),
        cosT=np.ascontiguousarray(np.asarray(inputs["L0_attn_cos"]).T),
        sinT=np.ascontiguousarray(np.asarray(inputs["L0_attn_sin"]).T),
        w1t=np.ascontiguousarray(np.asarray(inputs["L0_ffn_w1"]).T),
        cw=np.ascontiguousarray(np.asarray(inputs["L0_ffn_conv_w"]).reshape(HC, 128, 3)
                                .transpose(1, 0, 2).reshape(128, HC * 3)),
        cb=np.ascontiguousarray(np.asarray(inputs["L0_ffn_conv_b"]).reshape(HC, 128).T),
        w2t=np.ascontiguousarray(np.asarray(inputs["L0_ffn_w2"]).T),
    )
    maps = []
    for c in range(8):
        m = dict(common)
        m["x"] = np.ascontiguousarray(x[c * NSEQ:(c + 1) * NSEQ])
        maps.append(m)
    return maps


def kernel(**inputs):
    maps = host_inputs(inputs)
    nc, _ = build_nc()
    res = run_bass_kernel_spmd(nc, maps, core_ids=list(range(8)))
    out = np.concatenate([np.asarray(r["out"]) for r in res.results], axis=0)
    return out.astype(np.float32)


# version 18
# speedup vs baseline: 1.0440x; 1.0265x over previous
import numpy as np
import concourse.bass as bass
import concourse.mybir as mybir
from concourse import library_config as _lc
_LIB = {l.name: l for l in _lc.all_libraries}
from concourse.bass_utils import run_bass_kernel_spmd
from contextlib import ExitStack

F32 = mybir.dt.float32
F16 = mybir.dt.float16
BF16 = mybir.dt.bfloat16
I32 = mybir.dt.int32
AF = mybir.ActivationFunctionType
AX = mybir.AxisListType
OP = mybir.AluOpType

H, NH, HD, T, NSEQ = 1024, 16, 64, 2048, 2
DH = 4096
NT128, NT512 = T // 128, T // 512
OC, HC, KC = H // 128, DH // 128, H // 128
NPAIR, NKT = NH // 2, T // 128
EPS = 1e-5
RM = 1536.0


class Sem:
    def __init__(self, es, nc, name):
        self.h = es.enter_context(nc.semaphore(name))
        self.v = 0

    def inc(self, ins, n=1):
        ins.then_inc(self.h, n)
        self.v += n
        return self.v

    def wait(self, eng, v=None):
        eng.wait_ge(self.h, self.v if v is None else v)


def build_nc(dbg=()):
    nc = bass.Bass("TRN2", target_bir_lowering=False)
    _nctr = [0]

    def _un(n):
        _nctr[0] += 1
        return f"{n}_{_nctr[0]}"
    x_h = nc.dram_tensor("x", [NSEQ, T, H], F32, kind="ExternalInput").ap()
    wq_h = nc.dram_tensor("wq", [H, H], F32, kind="ExternalInput").ap()
    wk_h = nc.dram_tensor("wk", [H, H], F32, kind="ExternalInput").ap()
    wv_h = nc.dram_tensor("wv", [H, H], F32, kind="ExternalInput").ap()
    wo_h = nc.dram_tensor("wo", [H, H], F32, kind="ExternalInput").ap()
    bq_h = nc.dram_tensor("bq", [128, OC], F32, kind="ExternalInput").ap()
    bv_h = nc.dram_tensor("bv", [128, OC], F32, kind="ExternalInput").ap()
    bo_h = nc.dram_tensor("bo", [H], F32, kind="ExternalInput").ap()
    cos_h = nc.dram_tensor("cosT", [HD, 1024], F32, kind="ExternalInput").ap()
    sin_h = nc.dram_tensor("sinT", [HD, 1024], F32, kind="ExternalInput").ap()
    w1t_h = nc.dram_tensor("w1t", [H, DH], F32, kind="ExternalInput").ap()
    cw_h = nc.dram_tensor("cw", [128, 3 * HC], F32, kind="ExternalInput").ap()
    cb_h = nc.dram_tensor("cb", [128, HC], F32, kind="ExternalInput").ap()
    w2t_h = nc.dram_tensor("w2t", [DH, H], F32, kind="ExternalInput").ap()
    out_h = nc.dram_tensor("out", [NSEQ, T, H], F32, kind="ExternalOutput").ap()

    wo_sp = nc.dram_tensor("wo_sp", [KC, 128, H], BF16, kind="Internal").ap()
    w1qt_sp = nc.dram_tensor("w1qt_sp", [KC, 128, DH], BF16, kind="Internal").ap()
    w2qt_sp = nc.dram_tensor("w2qt_sp", [HC, 128, H], BF16, kind="Internal").ap()
    wqkv_sp = nc.dram_tensor("wqkv_sp", [OC, 128, 3 * KC * 128], BF16,
                             kind="Internal").ap()
    xqd_sp = nc.dram_tensor("xqd_sp", [NSEQ, KC, 128, T], BF16, kind="Internal").ap()
    x2_sp = nc.dram_tensor("x2_sp", [NSEQ, T, H], F32, kind="Internal").ap()
    hs_sp = nc.dram_tensor("hs_sp", [NSEQ, HC, 128, T], BF16, kind="Internal").ap()
    c1_sp = nc.dram_tensor("c1_sp", [NSEQ, T], F32, kind="Internal").ap()
    s2_sp = nc.dram_tensor("s2_sp", [NSEQ, T], F32, kind="Internal").ap()

    V, S, TE, G, SY = nc.vector, nc.scalar, nc.tensor, nc.gpsimd, nc.sync

    es = ExitStack()
    sbuf = lambda n, shape, d=F32: es.enter_context(nc.sbuf_tensor(n, shape, d))

    dbg_outs = {}
    for name, shape, d in dbg:
        dbg_outs[name] = nc.dram_tensor(name, shape, d, kind="ExternalOutput").ap()

    with es:
        ident = sbuf("ident", [128, 128], BF16)
        identf = sbuf("identf", [128, 128], F32)
        ones1f = sbuf("ones1f", [1, 128], F32)
        cosfm = sbuf("cosfm", [128, 1024], BF16)
        sinfm = sbuf("sinfm", [128, 1024], BF16)
        bq_pre = sbuf("bq_pre", [128, OC], F32)
        bv_pre = sbuf("bv_pre", [128, OC], F32)
        bo_row = sbuf("bo_row", [1, H], BF16)
        ones_row = sbuf("ones_row", [1, 128], BF16)
        ones2 = sbuf("ones2", [128, 2], BF16)
        convw_s = sbuf("convw_s", [128, 3 * HC], F32)
        convb_s = sbuf("convb_s", [128, HC], F32)
        negM = sbuf("negM", [128, NPAIR], F32)
        negRM = sbuf("negRM", [128, 1], F32)
        posRM = sbuf("posRM", [128, 1], F32)
        wsc = sbuf("wsc", [128, 4], F32)
        stat = sbuf("stat", [128, 64], F32)
        bnbuf = sbuf("bnbuf", [128, 12], F32)
        nrm = sbuf("nrm", [128, 24], F32)
        s2s = sbuf("s2s", [128, NT128], F32)
        c1sb = sbuf("c1sb", [128, NT128], F32)
        w3_es = ExitStack()
        wo3 = w3_es.enter_context(nc.sbuf_tensor(_un("wo3"), [128, KC * H], BF16))
        qres = w3_es.enter_context(nc.sbuf_tensor(_un("qres"), [128, OC * T], BF16))
        kres = w3_es.enter_context(nc.sbuf_tensor(_un("kres"), [128, OC * T], BF16))
        vres = w3_es.enter_context(
            nc.sbuf_tensor(_un("vres"), [128, NPAIR * NKT * 130], BF16))

        sems = {}
        for name in ["dc", "drl", "tv", "ts", "tt", "tg",
                     "dxr0", "dxr1", "dxr2", "dxr3", "dst0", "dst1", "dss0", "dss1",
                     "dsq0", "dsq1", "dsk0", "dsk1", "dsd0", "dsd1",
                     "datt0", "datt1", "dtb0", "dtb1", "dwo0", "dwo1",
                     "dx20", "dx21", "dqs0", "dqs1", "dcr", "dhs",
                     "dr20", "dr21", "dxa0", "dxa1", "dou0", "dou1", "dsw"]:
            sems[name] = Sem(es, nc, name)
        dc, drl = sems["dc"], sems["drl"]
        sv, ss, st, sg = (sems[k] for k in ["tv", "ts", "tt", "tg"])
        dxr = [sems["dxr0"], sems["dxr1"], sems["dxr2"], sems["dxr3"]]
        dstg = [sems["dst0"], sems["dst1"]]
        dss = [sems["dss0"], sems["dss1"]]
        dsq = [sems["dsq0"], sems["dsq1"]]
        dsk = [sems["dsk0"], sems["dsk1"]]
        dsd = [sems["dsd0"], sems["dsd1"]]
        datt = [sems["datt0"], sems["datt1"]]
        dtb = [sems["dtb0"], sems["dtb1"]]
        dwo = [sems["dwo0"], sems["dwo1"]]
        dx2 = [sems["dx20"], sems["dx21"]]
        dqs = [sems["dqs0"], sems["dqs1"]]
        dcr, dhs = sems["dcr"], sems["dhs"]
        dr2 = [sems["dr20"], sems["dr21"]]
        dxa = [sems["dxa0"], sems["dxa1"]]
        dou = [sems["dou0"], sems["dou1"]]
        dsw = sems["dsw"]
        for s_ in sems.values():
            SY.sem_clear(s_.h)
        nc.all_engine_barrier()

        def bar():
            nc.all_engine_barrier()

        def ln_stats(src_ap, so):
            V.bn_stats(bnbuf[:, 0:6], src_ap[:, 0:512])
            sv.inc(V.bn_stats(bnbuf[:, 6:12], src_ap[:, 512:1024]))
            sv.wait(V)
            sv.inc(V.bn_aggr(stat[:, so:so + 2], bnbuf[:, 0:12]))
            sv.wait(V)
            sv.inc(V.tensor_scalar(stat[:, so + 2:so + 3], stat[:, so + 1:so + 2],
                                   EPS, None, op0=OP.add))
            sv.wait(V)
            return sv.inc(V.reciprocal(stat[:, so + 3:so + 4],
                                       stat[:, so + 2:so + 3]))

        def ln_fin(so, v2):
            sv.wait(S, v2)
            s1 = ss.inc(S.activation(stat[:, so + 2:so + 3], stat[:, so + 3:so + 4],
                                     AF.Sqrt))
            ss.wait(V, s1)
            sv.wait(V)
            mbv = sv.inc(V.tensor_scalar(stat[:, so + 3:so + 4], stat[:, so:so + 1],
                                         stat[:, so + 2:so + 3], -1.0,
                                         op0=OP.mult, op1=OP.mult))
            return stat[:, so + 2:so + 3], stat[:, so + 3:so + 4], mbv

        def ln_chain(src_ap, so):
            return ln_fin(so, ln_stats(src_ap, so))

        _ph = ExitStack()
        it_f = _ph.enter_context(nc.sbuf_tensor(_un("it_f"), [128, 128], I32))
        it_p = _ph.enter_context(nc.sbuf_tensor(_un("it_p"), [128, 1], F32))
        trigA = _ph.enter_context(nc.sbuf_tensor(_un("trigA"), [64, 1024], F32))
        trigB = _ph.enter_context(nc.sbuf_tensor(_un("trigB"), [64, 1024], F32))
        borowf = _ph.enter_context(nc.sbuf_tensor(_un("borowf"), [1, H], F32))
        wstg0 = _ph.enter_context(nc.sbuf_tensor(_un("wstg0"), [128, DH], F32))
        wstg1 = _ph.enter_context(nc.sbuf_tensor(_un("wstg1"), [128, DH], F32))
        qh16 = _ph.enter_context(nc.sbuf_tensor(_un("qh16"), [128, DH], F16))
        wqa = _ph.enter_context(nc.sbuf_tensor(_un("wqa"), [128, DH], BF16))
        wqb = _ph.enter_context(nc.sbuf_tensor(_un("wqb"), [128, DH], BF16))
        pst = _ph.enter_context(nc.psum_tensor(_un("pst"), [128, 128], F32))
        with _ph:
            stg = [wstg0, wstg1]
            wqx = [wqa, wqb]
            G.iota(it_f[:], pattern=[[1, 128]], base=0, channel_multiplier=0)
            sg.inc(G.iota(it_p[:], pattern=[[0, 1]], base=0, channel_multiplier=1,
                   allow_small_or_imprecise_dtypes=True))
            sg.wait(V)
            V.tensor_scalar(ident[:], it_f[:], it_p[:], None, op0=OP.is_equal)
            V.tensor_scalar(identf[:], it_f[:], it_p[:], None, op0=OP.is_equal)
            V.memset(ones_row[:], 1.0)
            V.memset(ones1f[:], 1.0)
            V.memset(negRM[:], -RM)
            V.memset(posRM[:], RM)
            sv.inc(V.memset(ones2[:], 0.0))
            sv.wait(V)
            V.memset(ones2[0:64, 0:1], 1.0)
            sv.inc(V.memset(ones2[64:128, 1:2], 1.0))

            dc.inc(SY.dma_start(stat[:, 0:OC], bq_h), 16)
            dc.inc(SY.dma_start(bv_pre[:], bv_h), 16)
            dc.inc(SY.dma_start(borowf[:], bo_h[None, :]), 16)
            dc.inc(SY.dma_start(convw_s[:], cw_h), 16)
            dc.inc(SY.dma_start(convb_s[:], cb_h), 16)
            dc.inc(SY.dma_start(trigA[:], cos_h), 16)
            dc.inc(SY.dma_start(trigB[:], sin_h), 16)
            dc.wait(V)
            sv.inc(V.tensor_scalar(bq_pre[:], stat[:, 0:OC], 0.125, None, op0=OP.mult))
            sv.inc(V.tensor_copy(bo_row[:], borowf[:]))
            V.tensor_copy(cosfm[0:64, :], trigA[:])
            V.tensor_copy(cosfm[64:128, :], trigA[:])
            V.tensor_copy(sinfm[0:64, :], trigB[:])
            sv.inc(V.tensor_copy(sinfm[64:128, :], trigB[:]))

            wsrc = [wq_h, wk_h, wv_h]
            castv = []
            wspill = [0, 0]
            for i, (wh, b) in enumerate([(w, k) for w in range(3) for k in range(2)]):
                eng = SY if i % 2 == 0 else G
                if i >= 2:
                    sv.wait(eng, castv[i - 2])
                lv = dstg[i % 2].inc(eng.dma_start(
                    stg[i % 2][:, 0:4 * H].rearrange("p (c n) -> p c n", n=H),
                    wsrc[wh][b * 512:(b + 1) * 512, :].rearrange(
                        "(c p) n -> p c n", p=128)), 16)
                dstg[i % 2].wait(V, lv)
                if i >= 2:
                    dsq[i % 2].wait(V, wspill[i % 2])
                castv.append(sv.inc(V.tensor_copy(
                    wqx[i % 2][:, 0:4 * H], stg[i % 2][:, 0:4 * H])))
                sv.wait(SY, castv[-1])
                for k4 in range(4):
                    tix = wh * KC + b * 4 + k4
                    wspill[i % 2] = dsq[i % 2].inc(SY.dma_start(
                        wqkv_sp[:, :, tix * 128:(tix + 1) * 128].rearrange(
                            "o p n -> p o n"),
                        wqx[i % 2][:, k4 * H:(k4 + 1) * H].rearrange(
                            "p (o n) -> p o n", n=128)), 16)
            for b in range(2):
                sl = b % 2
                eng = SY if sl == 0 else G
                sv.wait(eng, castv[-2])
                lv = dstg[sl].inc(eng.dma_start(
                    stg[sl][:, 0:4 * H].rearrange("p (c n) -> p c n", n=H),
                    wo_h[b * 512:(b + 1) * 512, :].rearrange(
                        "(c p) n -> p c n", p=128)), 16)
                dstg[sl].wait(V, lv)
                castv.append(sv.inc(V.tensor_copy(
                    wo3[:, b * 4 * H:(b + 1) * 4 * H], stg[sl][:, 0:4 * H])))

            def quant_weight(src_h, nchunk, width, sw_col, dst_sp, mwcol=None):
                nb = DH // width
                nbatch = nchunk // nb
                bw = nb * width
                redv = []
                sv.wait(SY)
                sv.wait(G)
                for c in range(nbatch):
                    if c >= 2:
                        sv.wait(SY, redv[c - 2])
                        sv.wait(G, redv[c - 2])
                    eng = SY if c % 2 == 0 else G
                    lv = dstg[c % 2].inc(eng.dma_start(
                        stg[c % 2][:, 0:bw].rearrange("p (c n) -> p c n", n=width),
                        src_h[c * nb * 128:(c + 1) * nb * 128, :].rearrange(
                            "(c p) n -> p c n", p=128)), 16)
                    dstg[c % 2].wait(V, lv)
                    redv.append(sv.inc(V.tensor_reduce(
                        nrm[:, c % 16:c % 16 + 1], stg[c % 2][:, 0:bw],
                        axis=AX.X, op=OP.add, apply_absolute_value=True)))
                sv.wait(V)
                sv.inc(V.tensor_reduce(nrm[:, 20:21], nrm[:, 0:nbatch],
                                       axis=AX.X, op=OP.add))
                sv.wait(TE)
                tq1 = st.inc(TE.matmul(pst[0:1, 0:128], nrm[:, 20:21], identf[:],
                                       start=True, stop=True))
                st.wait(V, tq1)
                sv.inc(V.tensor_reduce(nrm[0:1, 21:22], pst[0:1, 0:128],
                                       axis=AX.X, op=OP.add))
                sv.wait(TE)
                tq2 = st.inc(TE.matmul(pst[:, 0:1], ones1f[:], nrm[0:1, 21:22],
                                       start=True, stop=True))
                st.wait(V, tq2)
                sv.inc(V.tensor_copy(nrm[:, 21:22], pst[:, 0:1]))
                sv.wait(V)
                sv.inc(V.reciprocal(nrm[:, 22:23], nrm[:, 21:22]))
                sv.wait(V)
                nel = float(nchunk * 128 * width)
                sv.inc(V.tensor_scalar(wsc[:, sw_col:sw_col + 1], nrm[:, 22:23], nel,
                                       None, op0=OP.mult))
                if mwcol is not None:
                    sv.inc(V.tensor_scalar(wsc[:, mwcol:mwcol + 1], nrm[:, 21:22],
                                           1.0 / (nel * 127.0), None, op0=OP.mult))
                sv.wait(V)
                qv = [0, 0]
                qst = [0, 0]
                sv.wait(SY)
                sv.wait(G)
                dss[0].wait(V)
                dss[1].wait(V)
                dsq[0].wait(V)
                dsq[1].wait(V)
                for c in range(nbatch):
                    sl = c % 2
                    if c >= 2:
                        sv.wait(SY, qv[sl])
                        sv.wait(G, qv[sl])
                        dss[sl].wait(V, qst[sl])
                    eng = SY if c % 2 == 0 else G
                    lv = dstg[sl].inc(eng.dma_start(
                        stg[sl][:, 0:bw].rearrange("p (c n) -> p c n", n=width),
                        src_h[c * nb * 128:(c + 1) * nb * 128, :].rearrange(
                            "(c p) n -> p c n", p=128)), 16)
                    dstg[sl].wait(V, lv)
                    sv.inc(V.tensor_scalar(qh16[:, 0:bw], stg[sl][:, 0:bw],
                                           wsc[:, sw_col:sw_col + 1], RM,
                                           op0=OP.mult, op1=OP.add))
                    sv.wait(V)
                    sv.inc(V.tensor_scalar(wqx[sl][:, 0:bw], qh16[:, 0:bw],
                                           -RM, -1.0, op0=OP.add, op1=OP.max))
                    sv.wait(V)
                    qv[sl] = sv.inc(V.tensor_scalar(wqx[sl][:, 0:bw], wqx[sl][:, 0:bw],
                                                    1.0, None, op0=OP.min))
                    sv.wait(G, qv[sl])
                    qst[sl] = dss[sl].inc(G.dma_start(
                        dst_sp[c * nb:(c + 1) * nb].rearrange("c p n -> p c n"),
                        wqx[sl][:, 0:bw].rearrange("p (c n) -> p c n", n=width)), 16)

            quant_weight(w1t_h, KC, DH, 0, w1qt_sp, mwcol=1)
            quant_weight(w2t_h, HC, H, 2, w2qt_sp)
            for s_ in (dstg[0], dstg[1], dss[0], dss[1], dsq[0], dsq[1], dc):
                s_.wait(SY)
        bar()

        _ph = ExitStack()
        attA = _ph.enter_context(nc.sbuf_tensor(_un("attA"), [128, OC * T], BF16))
        with _ph:
            A3 = attA[:, :].rearrange("p (c n) -> p c n", n=T)

            for s in range(NSEQ):
                _ph = ExitStack()
                xr = [_ph.enter_context(nc.sbuf_tensor(_un("xra"), [128, H], F32))
                      for _ in range(3)]
                ln1b0 = _ph.enter_context(nc.sbuf_tensor(_un("ln1b0"), [128, H], BF16))
                ln1b1 = _ph.enter_context(nc.sbuf_tensor(_un("ln1b1"), [128, H], BF16))
                ptrA = _ph.enter_context(nc.psum_tensor(_un("ptrA"), [128, 512], F32))
                ptrB = _ph.enter_context(nc.psum_tensor(_un("ptrB"), [128, 512], F32))
                with _ph:
                    ln1b = [ln1b0, ln1b1]
                    ptr = [ptrA, ptrB]
                    xload = [dxr[i].inc(SY.dma_start(
                        xr[i][:], x_h[s, i * 128:(i + 1) * 128, :]), 16) for i in range(3)]
                    tcv = [0, 0]
                    lnuse = [0, 0]
                    for i in range(NT128):
                        dxr[i % 3].wait(V, xload[i])
                        r_ap, mb_ap, mbv_ = ln_chain(xr[i % 3][:], (i % 4) * 4)
                        sv.wait(S, mbv_)
                        if i >= 2:
                            st.wait(S, lnuse[i % 2])
                        sev = ss.inc(S.activation(ln1b[i % 2][:], xr[i % 3][:], AF.Identity,
                                                  scale=r_ap, bias=mb_ap))
                        if i + 3 < NT128:
                            ss.wait(SY, sev)
                            xload.append(dxr[(i + 3) % 3].inc(SY.dma_start(
                                xr[(i + 3) % 3][:], x_h[s, (i + 3) * 128:(i + 4) * 128, :]), 16))
                        ss.wait(TE, sev)
                        for half in range(2):
                            sv.wait(TE, tcv[half])
                            for k4 in range(4):
                                kc = half * 4 + k4
                                _ins = TE.matmul(ptr[half][:, k4 * 128:(k4 + 1) * 128],
                                          ln1b[i % 2][:, kc * 128:(kc + 1) * 128],
                                          ident[:], start=True, stop=True)
                            tev = st.inc(_ins)
                            lnuse[i % 2] = tev
                            st.wait(V, tev)
                            tcv[half] = sv.inc(V.tensor_copy(
                                A3[:, half * 4:half * 4 + 4, i * 128:(i + 1) * 128],
                                ptr[half][:, :].rearrange("p (c n) -> p c n", n=128)))
                bar()

                _ph = ExitStack()
                wsb0 = _ph.enter_context(
                    nc.sbuf_tensor(_un("wsb0"), [128, 3 * KC * 128], BF16))
                wsb1 = _ph.enter_context(
                    nc.sbuf_tensor(_un("wsb1"), [128, 3 * KC * 128], BF16))
                vtmp = _ph.enter_context(nc.sbuf_tensor(_un("vtmp"), [128, 512], BF16))
                rot2 = [_ph.enter_context(nc.sbuf_tensor(_un("rot2"), [128, 512], BF16))
                        for _ in range(2)]
                rt1 = _ph.enter_context(nc.sbuf_tensor(_un("rt1"), [128, 512], BF16))
                rt2 = _ph.enter_context(nc.sbuf_tensor(_un("rt2"), [128, 512], BF16))
                sqb = _ph.enter_context(nc.sbuf_tensor(_un("sqb"), [128, 512], BF16))
                prodb = _ph.enter_context(nc.sbuf_tensor(_un("prodb"), [128, 8], BF16))
                pq = _ph.enter_context(nc.psum_tensor(_un("pq"), [128, 512], F32))
                pk = _ph.enter_context(nc.psum_tensor(_un("pk"), [128, 512], F32))
                pv = _ph.enter_context(nc.psum_tensor(_un("pv"), [128, 512], F32))
                ptrV = _ph.enter_context(nc.psum_tensor(_un("ptrV"), [128, 512], F32))
                pnb = _ph.enter_context(nc.psum_tensor(_un("pnb"), [2, 512], F32))
                pnb2 = _ph.enter_context(nc.psum_tensor(_un("pnb2"), [1, 8], F32))
                pbc = _ph.enter_context(nc.psum_tensor(_un("pbc"), [128, 8], F32))
                with _ph:
                    wsb = [wsb0, wsb1]
                    if s == 0:
                        VR = vres[:, :].rearrange("p (t c) -> p t c", c=130)
                        V.memset(VR[:, :, 64:65], 1.0)
                        onesv = sv.inc(V.memset(VR[:, :, 129:130], 1.0))
                    else:
                        onesv = 0
                    evq = evk = evv = 0
                    vtuse = 0
                    trv = 0
                    nbred = 0
                    nbmm = 0
                    pend_nb = [None]

                    def flush_nb():
                        nonlocal nbred, nbmm
                        if pend_nb[0] is None:
                            return
                        oc_, sqk = pend_nb[0]
                        pend_nb[0] = None
                        sv.wait(TE, sqk)
                        sv.wait(TE, nbred)
                        TE.matmul(pnb[:, 0:256], ones2[:], sqb[:, 0:256],
                                  start=True, stop=True)
                        nbmm = st.inc(TE.matmul(pnb[:, 256:512], ones2[:],
                                                sqb[:, 256:512],
                                                start=True, stop=True))
                        st.wait(V, nbmm)
                        V.tensor_reduce(nrm[0:2, oc_:oc_ + 1], pnb[:, 0:256],
                                        axis=AX.X, op=OP.max)
                        nbred = sv.inc(V.tensor_reduce(
                            nrm[0:2, 8 + oc_:9 + oc_], pnb[:, 256:512],
                            axis=AX.X, op=OP.max))
                    wldv = [0, 0]
                    wsuse = [0, 0]
                    wldv[0] = dstg[0].inc(SY.dma_start(wsb[0][:], wqkv_sp[0]), 16)
                    for oc in range(OC):
                        par = oc % 2
                        qview = qres[:, oc * T:(oc + 1) * T]
                        kview = kres[:, oc * T:(oc + 1) * T]
                        VD4 = vres[:, oc * NKT * 130:(oc + 1) * NKT * 130].rearrange(
                            "p (t g c) -> p t g c", g=2, c=65)
                        if oc + 1 < OC:
                            sl2 = (oc + 1) % 2
                            eng = SY if sl2 == 0 else G
                            st.wait(eng, wsuse[sl2])
                            wldv[sl2] = dstg[sl2].inc(
                                eng.dma_start(wsb[sl2][:], wqkv_sp[oc + 1]), 16)
                        dstg[par].wait(TE, wldv[par])
                        for j in range(NT512):
                            ss.wait(TE, evq)
                            for kc in range(KC):
                                _ins = TE.matmul(pq[:],
                                          wsb[par][:, (0 * KC + kc) * 128:
                                                   (0 * KC + kc) * 128 + 128],
                                          A3[:, kc, j * 512:(j + 1) * 512],
                                          start=(kc == 0), stop=(kc == KC - 1))
                            mmq = st.inc(_ins)
                            ss.wait(TE, evk)
                            for kc in range(KC):
                                _ins = TE.matmul(pk[:],
                                          wsb[par][:, (1 * KC + kc) * 128:
                                                   (1 * KC + kc) * 128 + 128],
                                          A3[:, kc, j * 512:(j + 1) * 512],
                                          start=(kc == 0), stop=(kc == KC - 1))
                            mmk = st.inc(_ins)
                            ss.wait(TE, evv)
                            st.wait(TE, vtuse)
                            for kc in range(KC):
                                _ins = TE.matmul(pv[:],
                                          wsb[par][:, (2 * KC + kc) * 128:
                                                   (2 * KC + kc) * 128 + 128],
                                          A3[:, kc, j * 512:(j + 1) * 512],
                                          start=(kc == 0), stop=(kc == KC - 1))
                            mmv = st.inc(_ins)
                            wsuse[par] = mmv
                            st.wait(S, mmq)
                            evq = ss.inc(S.activation(qview[:, j * 512:(j + 1) * 512], pq[:],
                                                      AF.Identity, scale=0.125,
                                                      bias=bq_pre[:, oc:oc + 1]))
                            st.wait(S, mmk)
                            evk = ss.inc(S.activation(kview[:, j * 512:(j + 1) * 512], pk[:],
                                                      AF.Identity, scale=1.0, bias=0.0))
                            st.wait(S, mmv)
                            evv = ss.inc(S.activation(vtmp[:], pv[:], AF.Identity,
                                                      scale=1.0, bias=bv_pre[:, oc:oc + 1]))
                            ss.wait(TE, evv)
                            ss.wait(TE, trv)
                            if s == 0:
                                sv.wait(TE, onesv)
                            for i4 in range(4):
                                _ins = TE.matmul(ptrV[:, i4 * 128:(i4 + 1) * 128],
                                          vtmp[:, i4 * 128:(i4 + 1) * 128], ident[:],
                                          start=True, stop=True)
                            vtuse = st.inc(_ins)
                            st.wait(S, vtuse)
                            trv = ss.inc(S.copy(
                                VD4[:, j * 4:j * 4 + 4, :, 0:64],
                                ptrV[:, :].rearrange("p (t g c) -> p t g c", g=2, c=64)))
                        flush_nb()
                        ss.wait(V)
                        rolast = 0
                        rits = [(qview, 0, 2), (qview, 0, 3),
                                (kview, 2, 2), (kview, 2, 3)]
                        for it in range(4):
                            buf, dstoff, jj = rits[it]
                            src = buf[:, jj * 512:(jj + 1) * 512]
                            ccols = cosfm[:, (jj - 2) * 512:(jj - 1) * 512]
                            scols = sinfm[:, (jj - 2) * 512:(jj - 1) * 512]
                            sv.wait(V)
                            for hh in range(2):
                                b0 = hh * 64
                                V.tensor_scalar(rot2[0][b0:b0 + 32, :],
                                                src[b0 + 32:b0 + 64, :],
                                                -1.0, None, op0=OP.mult)
                                V.tensor_scalar(rot2[0][b0 + 32:b0 + 64, :],
                                                src[b0:b0 + 32, :],
                                                1.0, None, op0=OP.mult)
                            sv.inc(V.tensor_tensor(rt1[:], src, ccols, op=OP.mult))
                            sv.wait(V)
                            sv.inc(V.tensor_tensor(rt2[:], rot2[0][:], scols,
                                                   op=OP.mult))
                            sv.wait(V)
                            rolast = sv.inc(V.tensor_tensor(
                                src, rt1[:], rt2[:], op=OP.add))
                        st.wait(V, nbmm)
                        sv.wait(V)
                        V.tensor_tensor(sqb[:, 0:128], qview[:, 0:1024:8],
                                        qview[:, 0:1024:8], op=OP.mult)
                        V.tensor_tensor(sqb[:, 128:192], qview[:, 1024:1536:8],
                                        qview[:, 1024:1536:8], op=OP.mult)
                        V.tensor_tensor(sqb[:, 192:256], qview[:, 1536:2048:8],
                                        qview[:, 1536:2048:8], op=OP.mult)
                        V.tensor_tensor(sqb[:, 256:384], kview[:, 0:1024:8],
                                        kview[:, 0:1024:8], op=OP.mult)
                        V.tensor_tensor(sqb[:, 384:448], kview[:, 1024:1536:8],
                                        kview[:, 1024:1536:8], op=OP.mult)
                        sq_k = sv.inc(V.tensor_tensor(
                            sqb[:, 448:512], kview[:, 1536:2048:8],
                            kview[:, 1536:2048:8], op=OP.mult))
                        pend_nb[0] = (oc, sq_k)
                    flush_nb()
                    sv.wait(S)
                    ss.inc(S.activation(nrm[0:2, 16:24], nrm[0:2, 0:8], AF.Sqrt))
                    ss.wait(S)
                    ssq2 = ss.inc(S.activation(nrm[0:2, 0:8], nrm[0:2, 8:16], AF.Sqrt))
                    ss.wait(V, ssq2)
                    sv.inc(V.tensor_tensor(nrm[0:2, 8:16], nrm[0:2, 16:24], nrm[0:2, 0:8],
                                           op=OP.mult))
                    sv.wait(V)
                    pv_ = sv.inc(V.tensor_copy(prodb[0:2, 0:8], nrm[0:2, 8:16]))
                    sv.wait(TE, pv_)
                    tb = st.inc(TE.matmul(pnb2[:], ones2[0:2, 0:1], prodb[0:2, 0:8],
                                          start=True, stop=True))
                    st.wait(V, tb)
                    ngv = sv.inc(V.tensor_scalar(nrm[0:1, 0:8], pnb2[:], -1.0, None, op0=OP.mult))
                    sv.wait(TE, ngv)
                    tb2 = st.inc(TE.matmul(pbc[:], ones1f[:], nrm[0:1, 0:8],
                                           start=True, stop=True))
                    st.wait(V, tb2)
                    sv.inc(V.tensor_copy(negM[:, 0:8], pbc[:]))
                bar()

                _ph = ExitStack()
                ptil0 = _ph.enter_context(nc.sbuf_tensor(_un("ptil0"), [128, 1024], BF16))
                ptil1 = _ph.enter_context(nc.sbuf_tensor(_un("ptil1"), [128, 1024], BF16))
                ptil2 = _ph.enter_context(nc.sbuf_tensor(_un("ptil2"), [128, 1024], BF16))
                tmpB0 = _ph.enter_context(nc.sbuf_tensor(_un("tmpB0"), [64, T], BF16))
                tmpB1 = _ph.enter_context(nc.sbuf_tensor(_un("tmpB1"), [64, T], BF16))
                invrA = _ph.enter_context(nc.sbuf_tensor(_un("invrA"), [1, T], F32))
                invrB = _ph.enter_context(nc.sbuf_tensor(_un("invrB"), [1, T], F32))
                lbufA = _ph.enter_context(nc.sbuf_tensor(_un("lbufA"), [1, T], F32))
                lbufB = _ph.enter_context(nc.sbuf_tensor(_un("lbufB"), [1, T], F32))
                invbA2 = _ph.enter_context(nc.sbuf_tensor(_un("invbA2"), [64, T], BF16))
                invbB2 = _ph.enter_context(nc.sbuf_tensor(_un("invbB2"), [64, T], BF16))
                sc0 = _ph.enter_context(nc.psum_tensor(_un("sc0"), [128, 1024], F32))
                sc1 = _ph.enter_context(nc.psum_tensor(_un("sc1"), [128, 1024], F32))
                accA = _ph.enter_context(nc.psum_tensor(_un("accA"), [65, 512], F32))
                accB = _ph.enter_context(nc.psum_tensor(_un("accB"), [65, 512], F32))
                pib = _ph.enter_context(nc.psum_tensor(_un("pib"), [128, 512], F32))
                with _ph:
                    ptil = [ptil0, ptil1, ptil2]
                    tmpB = [tmpB0, tmpB1]
                    sc = [sc0, sc1]
                    expv = [0, 0, 0]
                    avv3 = [0, 0, 0]
                    epiv = 0
                    bcv = 0
                    ibcp = 0
                    tailv = 0
                    erv_ = 0
                    dmaBv = [0, 0]
                    smmv = {}
                    NSTEP = 4 * NKT
                    pend = [None]

                    def qkd(p):
                        return (qres[:, p * T:(p + 1) * T],
                                kres[:, p * T:(p + 1) * T],
                                vres[:, p * NKT * 130:(p + 1) * NKT * 130]
                                .rearrange("p (t n) -> p t n", n=130))

                    def issue_smm(g):
                        p, i = divmod(g, NSTEP)
                        qc, kt = divmod(i, NKT)
                        sl = g % 2
                        qc_, kc_, _ = qkd(p)
                        TE.matmul(sc[sl][:, 0:512], kc_[0:64, kt * 128:(kt + 1) * 128],
                                  qc_[0:64, qc * 512:(qc + 1) * 512],
                                  start=True, stop=True, tile_position=(0, 0))
                        smmv[g] = st.inc(TE.matmul(
                            sc[sl][:, 512:1024], kc_[64:128, kt * 128:(kt + 1) * 128],
                            qc_[64:128, qc * 512:(qc + 1) * 512],
                            start=True, stop=True, tile_position=(64, 0)))

                    def flush_tailA(r):
                        nonlocal erv_
                        if r == 0:
                            st.wait(V, bcv)
                            sv.wait(V)
                        V.reciprocal(invrA[0:1, r * 512:(r + 1) * 512],
                                     lbufA[0:1, r * 512:(r + 1) * 512])
                        erv_ = sv.inc(V.reciprocal(
                            invrB[0:1, r * 512:(r + 1) * 512],
                            lbufB[0:1, r * 512:(r + 1) * 512]))

                    def flush_tailB():
                        nonlocal bcv, ibcp, tailv
                        ep, erv = pend[0]
                        pend[0] = None
                        sv.wait(TE, erv_)
                        for r in range(4):
                            sv.wait(TE, ibcp)
                            TE.matmul(pib[0:64, :], ones1f[0:1, 0:64],
                                      invrA[0:1, r * 512:(r + 1) * 512],
                                      start=True, stop=True, tile_position=(0, 0))
                            bcv = st.inc(TE.matmul(
                                pib[64:128, :], ones1f[0:1, 0:64],
                                invrB[0:1, r * 512:(r + 1) * 512],
                                start=True, stop=True, tile_position=(0, 64)))
                            st.wait(V, bcv)
                            V.tensor_copy(invbA2[:, r * 512:(r + 1) * 512],
                                          pib[0:64, :])
                            ibcp = sv.inc(V.tensor_copy(
                                invbB2[:, r * 512:(r + 1) * 512], pib[64:128, :]))
                        sv.wait(V)
                        V.tensor_tensor(
                            attA[0:64, ep * T:(ep + 1) * T],
                            attA[0:64, ep * T:(ep + 1) * T],
                            invbA2[:, :], op=OP.mult)
                        tailv = sv.inc(V.tensor_tensor(
                            tmpB[ep % 2][:, :], tmpB[ep % 2][:, :],
                            invbB2[:, :], op=OP.mult))
                        sv.wait(SY, tailv)
                        dmaBv[ep % 2] = dtb[ep % 2].inc(SY.dma_start(
                            attA[64:128, ep * T:(ep + 1) * T], tmpB[ep % 2][:]), 16)

                    issue_smm(0)
                    issue_smm(1)
                    for g in range(NPAIR * NSTEP):
                        p, i = divmod(g, NSTEP)
                        qc, kt = divmod(i, NKT)
                        sl = g % 2
                        if pend[0] is not None and i in (2, 10, 18, 26):
                            flush_tailA((2, 10, 18, 26).index(i))
                        if i == 2 * NKT + 4 and pend[0] is not None:
                            flush_tailB()
                        _, _, dc4 = qkd(p)
                        pl = g % 3
                        st.wait(S, smmv[g])
                        st.wait(S, avv3[pl])
                        expv[pl] = ss.inc(S.activation(
                            ptil[pl][:], sc[sl][:], AF.Exp,
                            bias=negM[:, p:p + 1], scale=1.0))
                        ss.wait(TE, expv[pl])
                        if g + 2 < NPAIR * NSTEP:
                            issue_smm(g + 2)
                        if kt == 0:
                            sv.wait(TE, epiv)
                        TE.matmul(accA[:], dc4[:, kt, 0:65], ptil[pl][:, 0:512],
                                  start=(kt == 0), stop=(kt == NKT - 1))
                        avv3[pl] = st.inc(TE.matmul(
                            accB[:], dc4[:, kt, 65:130], ptil[pl][:, 512:1024],
                            start=(kt == 0), stop=(kt == NKT - 1)))
                        if kt == NKT - 1:
                            st.wait(V, avv3[pl])
                            sv.wait(V)
                            if qc == 0 and p >= 2:
                                dtb[p % 2].wait(V, dmaBv[p % 2])
                            V.tensor_copy(lbufA[0:1, qc * 512:(qc + 1) * 512],
                                          accA[64:65, :])
                            sv.inc(V.tensor_copy(lbufB[0:1, qc * 512:(qc + 1) * 512],
                                                 accB[64:65, :]))
                            V.tensor_copy(
                                attA[0:64, p * T + qc * 512:p * T + (qc + 1) * 512],
                                accA[0:64, :])
                            epiv = sv.inc(V.tensor_copy(
                                tmpB[p % 2][:, qc * 512:(qc + 1) * 512],
                                accB[0:64, :]))
                            if qc == 3:
                                pend[0] = (p, epiv)
                    for r_ in range(4):
                        flush_tailA(r_)
                    flush_tailB()
                    dtb[0].wait(SY)
                    dtb[1].wait(SY)
                bar()

                _ph = ExitStack()
                xr = [_ph.enter_context(nc.sbuf_tensor(_un("xrb"), [128, H], F32))
                      for _ in range(4)]
                ln3t = _ph.enter_context(nc.sbuf_tensor(_un("ln3t"), [128, H], F32))
                x2t = [_ph.enter_context(nc.sbuf_tensor(_un("x2t"), [128, H], F32))
                       for _ in range(4)]
                ln2t = _ph.enter_context(nc.sbuf_tensor(_un("ln2t"), [128, H], F32))
                qx16 = _ph.enter_context(nc.sbuf_tensor(_un("qx16"), [128, H], F16))
                xqtm = _ph.enter_context(nc.sbuf_tensor(_un("xqtm"), [128, H], BF16))
                xqs0 = _ph.enter_context(nc.sbuf_tensor(_un("xqs0"), [128, 512], BF16))
                xqs1 = _ph.enter_context(nc.sbuf_tensor(_un("xqs1"), [128, 512], BF16))
                rowA = _ph.enter_context(nc.sbuf_tensor(_un("rowA"), [1, T], F32))
                pw0 = _ph.enter_context(nc.psum_tensor(_un("pw0"), [128, 1024], F32))
                pw1 = _ph.enter_context(nc.psum_tensor(_un("pw1"), [128, 1024], F32))
                pw2 = _ph.enter_context(nc.psum_tensor(_un("pw2"), [128, 1024], F32))
                pw3 = _ph.enter_context(nc.psum_tensor(_un("pw3"), [128, 1024], F32))
                with _ph:
                    xqs = [xqs0, xqs1]
                    pw = [pw0, pw1, pw2, pw3]
                    wo_use = [0, 0]
                    evacv = [0] * 4
                    xq_dma = [0, 0]
                    ln3use_s = 0
                    ln2use_s = 0
                    xqtmuse_s = 0
                    x2dma = [0] * 4
                    x2rds = [0] * 4
                    xruse = [0] * 4
                    for g in range(4):
                        for kc in range(KC):
                            if kc == 0:
                                for i in range(4):
                                    ss.wait(TE, evacv[i])
                            for i in range(4):
                                ti = g * 4 + i
                                for hh in range(2):
                                    _ins = TE.matmul(pw[i][:, hh * 512:(hh + 1) * 512],
                                              A3[:, kc, ti * 128:(ti + 1) * 128],
                                              wo3[:, kc * H + hh * 512:
                                                  kc * H + (hh + 1) * 512],
                                              start=(kc == 0), stop=False)
                            st.inc(_ins)
                        for i in range(4):
                            for hh in range(2):
                                _ins = TE.matmul(pw[i][:, hh * 512:(hh + 1) * 512],
                                          ones_row[0:1, 0:128],
                                          bo_row[0:1, hh * 512:(hh + 1) * 512],
                                          start=False, stop=True)
                        womm = st.inc(_ins)
                        xvv = [0] * 4
                        v2s = [0] * 4
                        for i in range(4):
                            ti = g * 4 + i
                            sv.wait(SY, xruse[i])
                            xvv[i] = dxr[i].inc(SY.dma_start(
                                xr[i][:], x_h[s, ti * 128:(ti + 1) * 128, :]), 16)
                            st.wait(V, womm)
                            v2s[i] = ln_stats(pw[i][:], 16 + i * 4)
                        fin3 = [ln_fin(16 + i * 4, v2s[i]) for i in range(4)]
                        for i in range(4):
                            ti = g * 4 + i
                            r3, mb3, mbv3 = fin3[i]
                            sv.wait(S, mbv3)
                            sv.wait(S, ln3use_s)
                            l3v = ss.inc(S.activation(ln3t[:], pw[i][:],
                                                      AF.Identity,
                                                      scale=r3, bias=mb3))
                            dxr[i].wait(V, xvv[i])
                            ss.wait(V, l3v)
                            dx2[i % 2].wait(V)
                            ss.wait(V, x2rds[i])
                            addv = sv.inc(V.tensor_tensor(
                                x2t[i][:], xr[i][:], ln3t[:], op=OP.add))
                            ln3use_s = addv
                            xruse[i] = addv
                            sv.wait(SY, addv)
                            x2dma[i] = dx2[i % 2].inc(SY.dma_start(
                                x2_sp[s, ti * 128:(ti + 1) * 128, :], x2t[i][:]), 16)
                        sv.wait(V)
                        for i in range(4):
                            v2s[i] = ln_stats(x2t[i][:], 32 + i * 4)
                        fin2 = [ln_fin(32 + i * 4, v2s[i]) for i in range(4)]
                        for i in range(4):
                            ti = g * 4 + i
                            qo = 48 + i * 4
                            r2, mb2, mbv2 = fin2[i]
                            sv.wait(S, mbv2)
                            sv.wait(S, ln2use_s)
                            l2v = ss.inc(S.activation(ln2t[:], x2t[i][:],
                                                      AF.Identity,
                                                      scale=r2, bias=mb2))
                            x2rds[i] = l2v
                            ss.wait(V, l2v)
                            sv.inc(V.tensor_reduce(
                                stat[:, qo:qo + 1], ln2t[:], axis=AX.X,
                                op=OP.max, apply_absolute_value=True))
                            sv.wait(V)
                            sv.inc(V.tensor_scalar(stat[:, qo + 1:qo + 2],
                                                   stat[:, qo:qo + 1], 1e-5,
                                                   None, op0=OP.max))
                            sv.wait(V)
                            sv.inc(V.reciprocal(stat[:, qo + 2:qo + 3],
                                                stat[:, qo + 1:qo + 2]))
                            sv.wait(V)
                            sv.inc(V.tensor_scalar(
                                stat[:, qo + 3:qo + 4], stat[:, qo + 2:qo + 3],
                                127.0, None, op0=OP.mult))
                            sv.inc(V.tensor_scalar(c1sb[:, ti:ti + 1],
                                                   stat[:, qo + 1:qo + 2],
                                                   wsc[:, 1:2], None, op0=OP.mult))
                            sv.wait(V)
                            st.wait(V, xqtmuse_s)
                            sv.inc(V.tensor_scalar(qx16[:], ln2t[:],
                                                   stat[:, qo + 3:qo + 4], RM,
                                                   op0=OP.mult, op1=OP.add))
                            sv.wait(V)
                            xqv = sv.inc(V.tensor_scalar(
                                xqtm[:], qx16[:], -RM, None,
                                op0=OP.add))
                            ln2use_s = xqv
                            sv.wait(TE, xqv)
                            for half in range(2):
                                dqs[half].wait(TE, xq_dma[half])
                                for k4 in range(4):
                                    kc = half * 4 + k4
                                    _ins = TE.matmul(pw[i][:, half * 512 + k4 * 128:
                                                    half * 512 + (k4 + 1) * 128],
                                              xqtm[:, kc * 128:(kc + 1) * 128],
                                              ident[:], start=True, stop=True)
                                thv = xqtmuse_s = st.inc(_ins)
                                st.wait(S, thv)
                                cpv = ss.inc(S.copy(
                                    xqs[half][:], pw[i][:, half * 512:(half + 1) * 512]))
                                ss.wait(SY, cpv)
                                xq_dma[half] = dqs[half].inc(SY.dma_start(
                                    xqd_sp[s][half * 4:half * 4 + 4, :,
                                              ti * 128:(ti + 1) * 128].rearrange(
                                                  "c p n -> p c n"),
                                    xqs[half][:, :].rearrange("p (c n) -> p c n", n=128)), 16)
                            evacv[i] = ss.v
                    sv.wait(TE)
                    dqs[0].wait(TE, xq_dma[0])
                    dqs[1].wait(TE, xq_dma[1])
                    for ti in range(NT128):
                        dst = pw0 if ti < 8 else pw1
                        _ins = TE.matmul(dst[0:1, (ti % 8) * 128:(ti % 8 + 1) * 128],
                                  c1sb[:, ti:ti + 1], identf[:], start=True, stop=True)
                    tcv2 = st.inc(_ins)
                    st.wait(V, tcv2)
                    V.tensor_copy(rowA[0:1, 0:1024], pw0[0:1, :])
                    rv2 = sv.inc(V.tensor_copy(rowA[0:1, 1024:2048], pw1[0:1, :]))
                    sv.wait(SY, rv2)
                    dcr.inc(SY.dma_start(c1_sp[s][None, :], rowA[:]), 16)
                    for s_ in (dcr, dx2[0], dx2[1], dqs[0], dqs[1]):
                        s_.wait(SY)
                bar()

        w3_es.close()
        bar()
        _ph = ExitStack()
        w1q = _ph.enter_context(nc.sbuf_tensor(_un("w1q"), [128, KC * DH], BF16))
        xqd = _ph.enter_context(nc.sbuf_tensor(_un("xqd"), [128, KC * T], BF16))
        diag = _ph.enter_context(nc.sbuf_tensor(_un("diag"), [128, 3 * HC * 128], BF16))
        c1f = _ph.enter_context(nc.sbuf_tensor(_un("c1f"), [128, T], F32))
        crow = _ph.enter_context(nc.sbuf_tensor(_un("crow"), [1, T], F32))
        hsc2 = [_ph.enter_context(nc.sbuf_tensor(_un("hsc2"), [128, T + 2], BF16))
                for _ in range(2)]
        hsb = _ph.enter_context(nc.sbuf_tensor(_un("hsb"), [128, T], BF16))
        macc = _ph.enter_context(nc.sbuf_tensor(_un("macc"), [128, T], BF16))
        mincc = _ph.enter_context(nc.sbuf_tensor(_un("mincc"), [128, T], BF16))
        s2row = _ph.enter_context(nc.sbuf_tensor(_un("s2row"), [1, T], F32))
        pwA = _ph.enter_context(nc.psum_tensor(_un("pwA"), [128, 512], F32))
        pwB = _ph.enter_context(nc.psum_tensor(_un("pwB"), [128, 512], F32))
        pcA = _ph.enter_context(nc.psum_tensor(_un("pcA"), [128, 512], F32))
        pcB = _ph.enter_context(nc.psum_tensor(_un("pcB"), [128, 512], F32))
        pam = _ph.enter_context(nc.psum_tensor(_un("pam"), [128, 256], F32))
        prow = _ph.enter_context(nc.psum_tensor(_un("prow"), [1, 512], F32))
        pbc1 = _ph.enter_context(nc.psum_tensor(_un("pbc1"), [128, 512], F32))
        with _ph:
            pwx = [pwA, pwB]
            pcx = [pcA, pcB]
            w1ld = 0
            for kc in range(KC):
                w1ld = drl.inc(G.dma_start(w1q[:, kc * DH:(kc + 1) * DH], w1qt_sp[kc]), 16)
            dgv = 0
            for c in range(HC):
                for j3 in range(3):
                    dgv = sv.inc(V.tensor_scalar(
                        diag[:, (c * 3 + j3) * 128:(c * 3 + j3 + 1) * 128],
                        ident[:], convw_s[:, c * 3 + j3:c * 3 + j3 + 1], None, op0=OP.mult))
            V.memset(hsc2[0][:, 0:1], 0.0)
            V.memset(hsc2[0][:, T + 1:T + 2], 0.0)
            V.memset(hsc2[1][:, 0:1], 0.0)
            hscz = sv.inc(V.memset(hsc2[1][:, T + 1:T + 2], 0.0))
            spillv = 0
            mm_prev = 0
            hscuse = [0, 0]
            for s in range(NSEQ):
                st.wait(G)
                sv.wait(G)
                st.wait(V)
                for kc in range(KC):
                    drl.inc(G.dma_start(xqd[:, kc * T:(kc + 1) * T], xqd_sp[s, kc]), 16)
                drl.inc(G.dma_start(crow[:], c1_sp[s][None, :]), 16)
                drl.wait(TE)
                sv.wait(S)
                c1bv = 0
                for jbc in range(NT512):
                    if jbc > 0:
                        ss.wait(TE, c1bv)
                    tcb = st.inc(TE.matmul(pbc1[:], ones1f[:],
                                           crow[0:1, jbc * 512:(jbc + 1) * 512],
                                           start=True, stop=True))
                    st.wait(S, tcb)
                    c1bv = ss.inc(S.copy(c1f[:, jbc * 512:(jbc + 1) * 512], pbc1[:]))
                V.memset(macc[:], -3e38)
                mmi = sv.inc(V.memset(mincc[:], 3e38))
                drl.wait(TE)
                pwv = [0, 0]
                silv = [0, 0]
                hval = {}

                def w1_block(c):
                    nonlocal mmi
                    hs_ = hsc2[c % 2]
                    for j in range(NT512):
                        slw = j % 2
                        sv.wait(TE, pwv[slw])
                        for kc in range(KC):
                            _ins = TE.matmul(pwx[slw][:],
                                      w1q[:, kc * DH + c * 128:kc * DH + (c + 1) * 128],
                                      xqd[:, kc * T + j * 512:kc * T + (j + 1) * 512],
                                      start=(kc == 0), stop=(kc == KC - 1))
                        wmm = st.inc(_ins)
                        st.wait(V, wmm)
                        ss.wait(V, c1bv)
                        sv.wait(V, hscz)
                        st.wait(V, hscuse[c % 2])
                        pwv[slw] = sv.inc(V.tensor_tensor(
                            hs_[:, 1 + j * 512:1 + (j + 1) * 512], pwx[slw][:],
                            c1f[:, j * 512:(j + 1) * 512], op=OP.mult))
                    hval[c] = sv.v

                def conv_block(c):
                    nonlocal mmi, mm_prev, spillv
                    hs_ = hsc2[c % 2]
                    for j in range(NT512):
                        slc = j % 2
                        sv.wait(TE, hval[c])
                        if c == 0:
                            sv.wait(TE, dgv)
                        ss.wait(TE, silv[slc])
                        for tap in range(3):
                            _ins = TE.matmul(pcx[slc][:],
                                      diag[:, (c * 3 + tap) * 128:(c * 3 + tap + 1) * 128],
                                      hs_[:, tap + j * 512:tap + j * 512 + 512],
                                      start=(tap == 0), stop=(tap == 2))
                        cmm = st.inc(_ins)
                        hscuse[c % 2] = cmm
                        st.wait(S, cmm)
                        dhs.wait(S, spillv)
                        sv.wait(S, mm_prev)
                        silv[slc] = ss.inc(S.activation(hsb[:, j * 512:(j + 1) * 512],
                                                        pcx[slc][:], AF.Silu,
                                                        bias=convb_s[:, c:c + 1], scale=1.0))
                    ss.wait(V, silv[(NT512 - 1) % 2])
                    sv.wait(V, mmi)
                    V.tensor_tensor(macc[:], macc[:], hsb[:], op=OP.max)
                    mmi = mm_prev = sv.inc(V.tensor_tensor(mincc[:], mincc[:], hsb[:], op=OP.min))
                    ss.wait(SY, silv[(NT512 - 1) % 2])
                    spillv = dhs.inc(SY.dma_start(hs_sp[s, c], hsb[:]), 16)

                w1_block(0)
                for c in range(HC):
                    if c + 1 < HC:
                        w1_block(c + 1)
                    conv_block(c)
                sv.wait(TE, mmi)
                pamv = 0
                for ti in range(NT128):
                    sv.wait(TE, pamv)
                    TE.matmul(pam[:, 0:128], macc[:, ti * 128:(ti + 1) * 128], ident[:],
                              start=True, stop=True)
                    tav = st.inc(TE.matmul(pam[:, 128:256], mincc[:, ti * 128:(ti + 1) * 128],
                                           ident[:], start=True, stop=True))
                    st.wait(V, tav)
                    V.tensor_reduce(stat[:, 40:41], pam[:, 0:128], axis=AX.X, op=OP.max)
                    pamv = sv.inc(V.tensor_reduce(stat[:, 41:42], pam[:, 128:256], axis=AX.X,
                                                  op=OP.min, negate=True))
                    sv.wait(V)
                    sv.inc(V.tensor_tensor(stat[:, 42:43], stat[:, 40:41], stat[:, 41:42],
                                           op=OP.max))
                    sv.wait(V)
                    sv.inc(V.tensor_scalar(stat[:, 43:44], stat[:, 42:43], 1e-5, None,
                                           op0=OP.max))
                    sv.wait(V)
                    sv.inc(V.reciprocal(stat[:, 44:45], stat[:, 43:44]))
                    sv.wait(V)
                    sv.inc(V.tensor_scalar(s2s[:, ti:ti + 1], stat[:, 44:45], 127.0,
                                           None, op0=OP.mult))
                sv.wait(TE)
                dsw.wait(V)
                srv = 0
                for tg in range(4):
                    if tg > 0:
                        sv.wait(TE, srv)
                    for t4 in range(4):
                        ti = tg * 4 + t4
                        _ins = TE.matmul(prow[0:1, t4 * 128:(t4 + 1) * 128], s2s[:, ti:ti + 1],
                                  identf[:], start=True, stop=True)
                    trv2 = st.inc(_ins)
                    st.wait(V, trv2)
                    srv = sv.inc(V.tensor_copy(s2row[0:1, tg * 512:(tg + 1) * 512], prow[0:1, :]))
                sv.wait(SY, srv)
                dsw.inc(SY.dma_start(s2_sp[s][None, :], s2row[:]), 16)
            dhs.wait(SY)
            dsw.wait(SY)
            drl.wait(SY)
        bar()

        _ph = ExitStack()
        w2q = _ph.enter_context(nc.sbuf_tensor(_un("w2q"), [128, HC * H], BF16))
        ring0 = _ph.enter_context(nc.sbuf_tensor(_un("ring0"), [128, HC * 512], BF16))
        ring1 = _ph.enter_context(nc.sbuf_tensor(_un("ring1"), [128, HC * 512], BF16))
        srow0 = _ph.enter_context(nc.sbuf_tensor(_un("srow0"), [1, T], F32))
        srow1 = _ph.enter_context(nc.sbuf_tensor(_un("srow1"), [1, T], F32))
        s2b0 = _ph.enter_context(nc.sbuf_tensor(_un("s2b0"), [128, 512], BF16))
        s2b1 = _ph.enter_context(nc.sbuf_tensor(_un("s2b1"), [128, 512], BF16))
        m216a = _ph.enter_context(nc.sbuf_tensor(_un("m216a"), [128, 512], F16))
        m216b = _ph.enter_context(nc.sbuf_tensor(_un("m216b"), [128, 512], F16))
        q216a = _ph.enter_context(nc.sbuf_tensor(_un("q216a"), [128, 512], F16))
        q216b = _ph.enter_context(nc.sbuf_tensor(_un("q216b"), [128, 512], F16))
        ln4t = _ph.enter_context(nc.sbuf_tensor(_un("ln4t"), [128, H], F32))
        x2r0 = _ph.enter_context(nc.sbuf_tensor(_un("x2r0"), [128, H], F32))
        x2r1 = _ph.enter_context(nc.sbuf_tensor(_un("x2r1"), [128, H], F32))
        outt0 = _ph.enter_context(nc.sbuf_tensor(_un("outt0"), [128, H], F32))
        outt1 = _ph.enter_context(nc.sbuf_tensor(_un("outt1"), [128, H], F32))
        p2a = _ph.enter_context(nc.psum_tensor(_un("p2a"), [128, 1024], F32))
        p2b = _ph.enter_context(nc.psum_tensor(_un("p2b"), [128, 1024], F32))
        pbs = _ph.enter_context(nc.psum_tensor(_un("pbs"), [128, 512], F32))
        with _ph:
            ring = [ring0, ring1]
            s2b = [s2b0, s2b1]
            x2r = [x2r0, x2r1]
            outt = [outt0, outt1]
            p2 = [p2a, p2b]
            for c in range(HC):
                drl.inc(G.dma_start(w2q[:, c * H:(c + 1) * H], w2qt_sp[c]), 16)
            ringuse = [0, 0]
            s2use = [0, 0]
            addv2 = [0, 0]
            rsubv = [0, 0]
            outdma = [0, 0]
            x2v_use = [0, 0]
            evv_ = [0, 0]
            pbsuse = 0
            srow = [srow0, srow1]
            srldv = [0, 0]
            sj = [(s, j) for s in range(NSEQ) for j in range(NT512)]
            NSJ = len(sj)
            ringldv = {}
            rndall_v = {}

            def load_ring(idx):
                s_, j_ = sj[idx]
                slr_ = idx % 2
                if j_ == 0:
                    drl.wait(G)
                    srldv[s_] = drl.inc(
                        G.dma_start(srow[s_][:], s2_sp[s_][None, :]), 16)
                st.wait(SY, ringuse[slr_])
                ringldv[idx] = dr2[slr_].inc(SY.dma_start(
                    ring[slr_][:, :].rearrange("p (c n) -> p c n", n=512),
                    hs_sp[s_][:, :, j_ * 512:(j_ + 1) * 512].rearrange(
                        "c p n -> p c n")), 16)

            def quant(idx):
                nonlocal pbsuse
                s_, j_ = sj[idx]
                slr_ = idx % 2
                drl.wait(TE, srldv[s_])
                ss.wait(TE, pbsuse)
                tb4 = st.inc(TE.matmul(pbs[:], ones1f[:],
                                       srow[s_][0:1, j_ * 512:(j_ + 1) * 512],
                                       start=True, stop=True))
                st.wait(S, tb4)
                sv.wait(S, s2use[slr_])
                s2bv = pbsuse = ss.inc(S.copy(s2b[slr_][:], pbs[:]))
                ss.wait(V, s2bv)
                dr2[slr_].wait(V, ringldv[idx])
                q216 = [q216a, q216b]
                m216 = [m216a, m216b]
                mvs = {}

                def qmult(c_):
                    slt_ = c_ % 2
                    ss.wait(V, addv2[slt_])
                    mvs[c_] = sv.inc(V.tensor_tensor(
                        m216[slt_][:], ring[slr_][:, c_ * 512:(c_ + 1) * 512],
                        s2b[slr_][:], op=OP.mult))

                qmult(0)
                for c in range(HC):
                    slt = c % 2
                    if c + 1 < HC:
                        qmult(c + 1)
                    sv.wait(S, mvs[c])
                    sv.wait(S, rsubv[slt])
                    addv2[slt] = ss.inc(S.activation(
                        q216[slt][:], m216[slt][:], AF.Identity,
                        bias=posRM[:, 0:1], scale=1.0))
                    ss.wait(V, addv2[slt])
                    rsubv[slt] = sv.inc(V.tensor_scalar(
                        ring[slr_][:, c * 512:(c + 1) * 512], q216[slt][:],
                        -RM, None, op0=OP.add))
                s2use[slr_] = sv.v
                rndall_v[idx] = sv.v

            load_ring(0)
            quant(0)
            for idx in range(NSJ):
                s, j = sj[idx]
                slr = idx % 2
                if idx + 1 < NSJ:
                    load_ring(idx + 1)
                    quant(idx + 1)
                rndall = rndall_v[idx]
                for i2 in range(2):
                    for half in range(2):
                            ti = j * 4 + i2 * 2 + half
                            pp = p2[half]
                            sv.wait(TE, rndall)
                            sv.wait(TE, evv_[half])
                            for c in range(HC):
                                base = c * 512 + (i2 * 2 + half) * 128
                                for hh in range(2):
                                    _ins = TE.matmul(pp[:, hh * 512:(hh + 1) * 512],
                                              ring[slr][:, base:base + 128],
                                              w2q[:, c * H + hh * 512:c * H + (hh + 1) * 512],
                                              start=(c == 0), stop=(c == HC - 1))
                            mmv_ = st.inc(_ins)
                            ringuse[slr] = mmv_
                            sv.wait(SY, x2v_use[half])
                            xv = dxa[half].inc(SY.dma_start(
                                x2r[half][:], x2_sp[s, ti * 128:(ti + 1) * 128, :]), 16)
                            st.wait(V, mmv_)
                            r4, mb4, mbv4 = ln_chain(pp[:], 48 + half * 4)
                            sv.wait(S, mbv4)
                            l4v = ss.inc(S.activation(ln4t[:], pp[:], AF.Identity,
                                                      scale=r4, bias=mb4))
                            dxa[half].wait(V, xv)
                            ss.wait(V, l4v)
                            dou[half].wait(V, outdma[half])
                            ov = sv.inc(V.tensor_tensor(outt[half][:], x2r[half][:], ln4t[:],
                                                        op=OP.add))
                            evv_[half] = ov
                            x2v_use[half] = ov
                            sv.wait(SY, ov)
                            outdma[half] = dou[half].inc(SY.dma_start(
                                out_h[s, ti * 128:(ti + 1) * 128, :], outt[half][:]), 16)
            dou[0].wait(SY)
            dou[1].wait(SY)
        bar()

    return nc, dbg_outs


def host_inputs(inputs):
    x = np.ascontiguousarray(inputs["L0_attn_hidden_states"])
    common = dict(
        wq=np.ascontiguousarray(inputs["L0_attn_wq"]),
        wk=np.ascontiguousarray(inputs["L0_attn_wk"]),
        wv=np.ascontiguousarray(inputs["L0_attn_wv"]),
        wo=np.ascontiguousarray(inputs["L0_attn_wo"]),
        bq=np.ascontiguousarray(np.asarray(inputs["L0_attn_bq"]).reshape(OC, 128).T),
        bv=np.ascontiguousarray(np.asarray(inputs["L0_attn_bv"]).reshape(OC, 128).T),
        bo=np.ascontiguousarray(inputs["L0_attn_bo"]),
        cosT=np.ascontiguousarray(np.asarray(inputs["L0_attn_cos"]).T),
        sinT=np.ascontiguousarray(np.asarray(inputs["L0_attn_sin"]).T),
        w1t=np.ascontiguousarray(np.asarray(inputs["L0_ffn_w1"]).T),
        cw=np.ascontiguousarray(np.asarray(inputs["L0_ffn_conv_w"]).reshape(HC, 128, 3)
                                .transpose(1, 0, 2).reshape(128, HC * 3)),
        cb=np.ascontiguousarray(np.asarray(inputs["L0_ffn_conv_b"]).reshape(HC, 128).T),
        w2t=np.ascontiguousarray(np.asarray(inputs["L0_ffn_w2"]).T),
    )
    maps = []
    for c in range(8):
        m = dict(common)
        m["x"] = np.ascontiguousarray(x[c * NSEQ:(c + 1) * NSEQ])
        maps.append(m)
    return maps


def kernel(**inputs):
    maps = host_inputs(inputs)
    nc, _ = build_nc()
    res = run_bass_kernel_spmd(nc, maps, core_ids=list(range(8)))
    out = np.concatenate([np.asarray(r["out"]) for r in res.results], axis=0)
    return out.astype(np.float32)


# version 19
# speedup vs baseline: 1.0939x; 1.0478x over previous
import numpy as np
import concourse.bass as bass
import concourse.mybir as mybir
from concourse import library_config as _lc
_LIB = {l.name: l for l in _lc.all_libraries}
from concourse.bass_utils import run_bass_kernel_spmd
from contextlib import ExitStack

F32 = mybir.dt.float32
F16 = mybir.dt.float16
BF16 = mybir.dt.bfloat16
I32 = mybir.dt.int32
AF = mybir.ActivationFunctionType
AX = mybir.AxisListType
OP = mybir.AluOpType

H, NH, HD, T, NSEQ = 1024, 16, 64, 2048, 2
DH = 4096
NT128, NT512 = T // 128, T // 512
OC, HC, KC = H // 128, DH // 128, H // 128
NPAIR, NKT = NH // 2, T // 128
EPS = 1e-5
RM = 1536.0


class Sem:
    def __init__(self, es, nc, name):
        self.h = es.enter_context(nc.semaphore(name))
        self.v = 0

    def inc(self, ins, n=1):
        ins.then_inc(self.h, n)
        self.v += n
        return self.v

    def wait(self, eng, v=None):
        eng.wait_ge(self.h, self.v if v is None else v)


def build_nc(dbg=()):
    nc = bass.Bass("TRN2", target_bir_lowering=False)
    _nctr = [0]

    def _un(n):
        _nctr[0] += 1
        return f"{n}_{_nctr[0]}"
    x_h = nc.dram_tensor("x", [NSEQ, T, H], F32, kind="ExternalInput").ap()
    wq_h = nc.dram_tensor("wq", [H, H], F32, kind="ExternalInput").ap()
    wk_h = nc.dram_tensor("wk", [H, H], F32, kind="ExternalInput").ap()
    wv_h = nc.dram_tensor("wv", [H, H], F32, kind="ExternalInput").ap()
    wo_h = nc.dram_tensor("wo", [H, H], F32, kind="ExternalInput").ap()
    bq_h = nc.dram_tensor("bq", [128, OC], F32, kind="ExternalInput").ap()
    bv_h = nc.dram_tensor("bv", [128, OC], F32, kind="ExternalInput").ap()
    bo_h = nc.dram_tensor("bo", [H], F32, kind="ExternalInput").ap()
    cos_h = nc.dram_tensor("cosT", [HD, 1024], F32, kind="ExternalInput").ap()
    sin_h = nc.dram_tensor("sinT", [HD, 1024], F32, kind="ExternalInput").ap()
    w1t_h = nc.dram_tensor("w1t", [H, DH], F32, kind="ExternalInput").ap()
    cw_h = nc.dram_tensor("cw", [128, 3 * HC], F32, kind="ExternalInput").ap()
    cb_h = nc.dram_tensor("cb", [128, HC], F32, kind="ExternalInput").ap()
    w2t_h = nc.dram_tensor("w2t", [DH, H], F32, kind="ExternalInput").ap()
    out_h = nc.dram_tensor("out", [NSEQ, T, H], F32, kind="ExternalOutput").ap()

    wo_sp = nc.dram_tensor("wo_sp", [KC, 128, H], BF16, kind="Internal").ap()
    w1qt_sp = nc.dram_tensor("w1qt_sp", [KC, 128, DH], BF16, kind="Internal").ap()
    w2qt_sp = nc.dram_tensor("w2qt_sp", [HC, 128, H], BF16, kind="Internal").ap()
    wqkv_sp = nc.dram_tensor("wqkv_sp", [OC, 128, 3 * KC * 128], BF16,
                             kind="Internal").ap()
    xqd_sp = nc.dram_tensor("xqd_sp", [NSEQ, KC, 128, T], BF16, kind="Internal").ap()
    x2_sp = nc.dram_tensor("x2_sp", [NSEQ, T, H], F32, kind="Internal").ap()
    hs_sp = nc.dram_tensor("hs_sp", [NSEQ, HC, 128, T], BF16, kind="Internal").ap()
    c1_sp = nc.dram_tensor("c1_sp", [NSEQ, T], F32, kind="Internal").ap()
    s2_sp = nc.dram_tensor("s2_sp", [NSEQ, T], F32, kind="Internal").ap()

    V, S, TE, G, SY = nc.vector, nc.scalar, nc.tensor, nc.gpsimd, nc.sync

    es = ExitStack()
    sbuf = lambda n, shape, d=F32: es.enter_context(nc.sbuf_tensor(n, shape, d))

    dbg_outs = {}
    for name, shape, d in dbg:
        dbg_outs[name] = nc.dram_tensor(name, shape, d, kind="ExternalOutput").ap()

    with es:
        ident = sbuf("ident", [128, 128], BF16)
        identf = sbuf("identf", [128, 128], F32)
        ones1f = sbuf("ones1f", [1, 128], F32)
        cosfm = sbuf("cosfm", [128, 1024], BF16)
        sinfm = sbuf("sinfm", [128, 1024], BF16)
        bq_pre = sbuf("bq_pre", [128, OC], F32)
        bv_pre = sbuf("bv_pre", [128, OC], F32)
        bo_row = sbuf("bo_row", [1, H], BF16)
        ones_row = sbuf("ones_row", [1, 128], BF16)
        ones2 = sbuf("ones2", [128, 2], BF16)
        convw_s = sbuf("convw_s", [128, 3 * HC], F32)
        convb_s = sbuf("convb_s", [128, HC], F32)
        negM = sbuf("negM", [128, NPAIR], F32)
        negRM = sbuf("negRM", [128, 1], F32)
        posRM = sbuf("posRM", [128, 1], F32)
        wsc = sbuf("wsc", [128, 4], F32)
        stat = sbuf("stat", [128, 64], F32)
        bnbuf = sbuf("bnbuf", [128, 12], F32)
        nrm = sbuf("nrm", [128, 24], F32)
        s2s = sbuf("s2s", [128, NT128], F32)
        c1sb = sbuf("c1sb", [128, NT128], F32)
        w3_es = ExitStack()
        wo3 = w3_es.enter_context(nc.sbuf_tensor(_un("wo3"), [128, KC * H], BF16))
        qres = w3_es.enter_context(nc.sbuf_tensor(_un("qres"), [128, OC * T], BF16))
        kres = w3_es.enter_context(nc.sbuf_tensor(_un("kres"), [128, OC * T], BF16))
        vres = w3_es.enter_context(
            nc.sbuf_tensor(_un("vres"), [128, NPAIR * NKT * 130], BF16))

        sems = {}
        for name in ["dc", "drl", "tv", "ts", "tt", "tg",
                     "dxr0", "dxr1", "dxr2", "dxr3", "dst0", "dst1", "dss0", "dss1",
                     "dsq0", "dsq1", "dsk0", "dsk1", "dsd0", "dsd1",
                     "datt0", "datt1", "dtb0", "dtb1", "dwo0", "dwo1",
                     "dx20", "dx21", "dqs0", "dqs1", "dcr", "dhs",
                     "dr20", "dr21", "dxa0", "dxa1", "dou0", "dou1", "dsw"]:
            sems[name] = Sem(es, nc, name)
        dc, drl = sems["dc"], sems["drl"]
        sv, ss, st, sg = (sems[k] for k in ["tv", "ts", "tt", "tg"])
        dxr = [sems["dxr0"], sems["dxr1"], sems["dxr2"], sems["dxr3"]]
        dstg = [sems["dst0"], sems["dst1"]]
        dss = [sems["dss0"], sems["dss1"]]
        dsq = [sems["dsq0"], sems["dsq1"]]
        dsk = [sems["dsk0"], sems["dsk1"]]
        dsd = [sems["dsd0"], sems["dsd1"]]
        datt = [sems["datt0"], sems["datt1"]]
        dtb = [sems["dtb0"], sems["dtb1"]]
        dwo = [sems["dwo0"], sems["dwo1"]]
        dx2 = [sems["dx20"], sems["dx21"]]
        dqs = [sems["dqs0"], sems["dqs1"]]
        dcr, dhs = sems["dcr"], sems["dhs"]
        dr2 = [sems["dr20"], sems["dr21"]]
        dxa = [sems["dxa0"], sems["dxa1"]]
        dou = [sems["dou0"], sems["dou1"]]
        dsw = sems["dsw"]
        for s_ in sems.values():
            SY.sem_clear(s_.h)
        nc.all_engine_barrier()

        def bar():
            nc.all_engine_barrier()

        def ln_stats(src_ap, so):
            V.bn_stats(bnbuf[:, 0:6], src_ap[:, 0:512])
            sv.inc(V.bn_stats(bnbuf[:, 6:12], src_ap[:, 512:1024]))
            sv.wait(V)
            sv.inc(V.bn_aggr(stat[:, so:so + 2], bnbuf[:, 0:12]))
            sv.wait(V)
            sv.inc(V.tensor_scalar(stat[:, so + 2:so + 3], stat[:, so + 1:so + 2],
                                   EPS, None, op0=OP.add))
            sv.wait(V)
            return sv.inc(V.reciprocal(stat[:, so + 3:so + 4],
                                       stat[:, so + 2:so + 3]))

        def ln_fin(so, v2):
            sv.wait(S, v2)
            s1 = ss.inc(S.activation(stat[:, so + 2:so + 3], stat[:, so + 3:so + 4],
                                     AF.Sqrt))
            ss.wait(V, s1)
            sv.wait(V)
            mbv = sv.inc(V.tensor_scalar(stat[:, so + 3:so + 4], stat[:, so:so + 1],
                                         stat[:, so + 2:so + 3], -1.0,
                                         op0=OP.mult, op1=OP.mult))
            return stat[:, so + 2:so + 3], stat[:, so + 3:so + 4], mbv

        def ln_chain(src_ap, so):
            return ln_fin(so, ln_stats(src_ap, so))

        _ph = ExitStack()
        it_f = _ph.enter_context(nc.sbuf_tensor(_un("it_f"), [128, 128], I32))
        it_p = _ph.enter_context(nc.sbuf_tensor(_un("it_p"), [128, 1], F32))
        trigA = _ph.enter_context(nc.sbuf_tensor(_un("trigA"), [64, 1024], F32))
        trigB = _ph.enter_context(nc.sbuf_tensor(_un("trigB"), [64, 1024], F32))
        borowf = _ph.enter_context(nc.sbuf_tensor(_un("borowf"), [1, H], F32))
        wstg0 = _ph.enter_context(nc.sbuf_tensor(_un("wstg0"), [128, DH], F32))
        wstg1 = _ph.enter_context(nc.sbuf_tensor(_un("wstg1"), [128, DH], F32))
        qh16 = _ph.enter_context(nc.sbuf_tensor(_un("qh16"), [128, DH], F16))
        wqa = _ph.enter_context(nc.sbuf_tensor(_un("wqa"), [128, DH], BF16))
        wqb = _ph.enter_context(nc.sbuf_tensor(_un("wqb"), [128, DH], BF16))
        pst = _ph.enter_context(nc.psum_tensor(_un("pst"), [128, 128], F32))
        with _ph:
            stg = [wstg0, wstg1]
            wqx = [wqa, wqb]
            G.iota(it_f[:], pattern=[[1, 128]], base=0, channel_multiplier=0)
            sg.inc(G.iota(it_p[:], pattern=[[0, 1]], base=0, channel_multiplier=1,
                   allow_small_or_imprecise_dtypes=True))
            sg.wait(V)
            V.tensor_scalar(ident[:], it_f[:], it_p[:], None, op0=OP.is_equal)
            V.tensor_scalar(identf[:], it_f[:], it_p[:], None, op0=OP.is_equal)
            V.memset(ones_row[:], 1.0)
            V.memset(ones1f[:], 1.0)
            V.memset(negRM[:], -RM)
            V.memset(posRM[:], RM)
            sv.inc(V.memset(ones2[:], 0.0))
            sv.wait(V)
            V.memset(ones2[0:64, 0:1], 1.0)
            sv.inc(V.memset(ones2[64:128, 1:2], 1.0))

            dc.inc(SY.dma_start(stat[:, 0:OC], bq_h), 16)
            dc.inc(SY.dma_start(bv_pre[:], bv_h), 16)
            dc.inc(SY.dma_start(borowf[:], bo_h[None, :]), 16)
            dc.inc(SY.dma_start(convw_s[:], cw_h), 16)
            dc.inc(SY.dma_start(convb_s[:], cb_h), 16)
            dc.inc(SY.dma_start(trigA[:], cos_h), 16)
            dc.inc(SY.dma_start(trigB[:], sin_h), 16)
            dc.wait(V)
            sv.inc(V.tensor_scalar(bq_pre[:], stat[:, 0:OC], 0.125, None, op0=OP.mult))
            sv.inc(V.tensor_copy(bo_row[:], borowf[:]))
            V.tensor_copy(cosfm[0:64, :], trigA[:])
            V.tensor_copy(cosfm[64:128, :], trigA[:])
            V.tensor_copy(sinfm[0:64, :], trigB[:])
            sv.inc(V.tensor_copy(sinfm[64:128, :], trigB[:]))

            wsrc = [wq_h, wk_h, wv_h]
            castv = []
            wspill = [0, 0]
            for i, (wh, b) in enumerate([(w, k) for w in range(3) for k in range(2)]):
                eng = SY if i % 2 == 0 else G
                if i >= 2:
                    sv.wait(eng, castv[i - 2])
                lv = dstg[i % 2].inc(eng.dma_start(
                    stg[i % 2][:, 0:4 * H].rearrange("p (c n) -> p c n", n=H),
                    wsrc[wh][b * 512:(b + 1) * 512, :].rearrange(
                        "(c p) n -> p c n", p=128)), 16)
                dstg[i % 2].wait(V, lv)
                if i >= 2:
                    dsq[i % 2].wait(V, wspill[i % 2])
                castv.append(sv.inc(V.tensor_copy(
                    wqx[i % 2][:, 0:4 * H], stg[i % 2][:, 0:4 * H])))
                sv.wait(SY, castv[-1])
                for k4 in range(4):
                    tix = wh * KC + b * 4 + k4
                    wspill[i % 2] = dsq[i % 2].inc(SY.dma_start(
                        wqkv_sp[:, :, tix * 128:(tix + 1) * 128].rearrange(
                            "o p n -> p o n"),
                        wqx[i % 2][:, k4 * H:(k4 + 1) * H].rearrange(
                            "p (o n) -> p o n", n=128)), 16)
            for b in range(2):
                sl = b % 2
                eng = SY if sl == 0 else G
                sv.wait(eng, castv[-2])
                lv = dstg[sl].inc(eng.dma_start(
                    stg[sl][:, 0:4 * H].rearrange("p (c n) -> p c n", n=H),
                    wo_h[b * 512:(b + 1) * 512, :].rearrange(
                        "(c p) n -> p c n", p=128)), 16)
                dstg[sl].wait(V, lv)
                castv.append(sv.inc(V.tensor_copy(
                    wo3[:, b * 4 * H:(b + 1) * 4 * H], stg[sl][:, 0:4 * H])))

            def quant_weight(src_h, nchunk, width, sw_col, dst_sp, mwcol=None):
                nb = DH // width
                nbatch = nchunk // nb
                bw = nb * width
                redv = []
                sv.wait(SY)
                sv.wait(G)
                for c in range(nbatch):
                    if c >= 2:
                        sv.wait(SY, redv[c - 2])
                        sv.wait(G, redv[c - 2])
                    eng = SY if c % 2 == 0 else G
                    lv = dstg[c % 2].inc(eng.dma_start(
                        stg[c % 2][:, 0:bw].rearrange("p (c n) -> p c n", n=width),
                        src_h[c * nb * 128:(c + 1) * nb * 128, :].rearrange(
                            "(c p) n -> p c n", p=128)), 16)
                    dstg[c % 2].wait(V, lv)
                    redv.append(sv.inc(V.tensor_reduce(
                        nrm[:, c % 16:c % 16 + 1], stg[c % 2][:, 0:bw],
                        axis=AX.X, op=OP.add, apply_absolute_value=True)))
                sv.wait(V)
                sv.inc(V.tensor_reduce(nrm[:, 20:21], nrm[:, 0:nbatch],
                                       axis=AX.X, op=OP.add))
                sv.wait(TE)
                tq1 = st.inc(TE.matmul(pst[0:1, 0:128], nrm[:, 20:21], identf[:],
                                       start=True, stop=True))
                st.wait(V, tq1)
                sv.inc(V.tensor_reduce(nrm[0:1, 21:22], pst[0:1, 0:128],
                                       axis=AX.X, op=OP.add))
                sv.wait(TE)
                tq2 = st.inc(TE.matmul(pst[:, 0:1], ones1f[:], nrm[0:1, 21:22],
                                       start=True, stop=True))
                st.wait(V, tq2)
                sv.inc(V.tensor_copy(nrm[:, 21:22], pst[:, 0:1]))
                sv.wait(V)
                sv.inc(V.reciprocal(nrm[:, 22:23], nrm[:, 21:22]))
                sv.wait(V)
                nel = float(nchunk * 128 * width)
                sv.inc(V.tensor_scalar(wsc[:, sw_col:sw_col + 1], nrm[:, 22:23], nel,
                                       None, op0=OP.mult))
                if mwcol is not None:
                    sv.inc(V.tensor_scalar(wsc[:, mwcol:mwcol + 1], nrm[:, 21:22],
                                           1.0 / (nel * 127.0), None, op0=OP.mult))
                sv.wait(V)
                qv = [0, 0]
                qst = [0, 0]
                sv.wait(SY)
                sv.wait(G)
                dss[0].wait(V)
                dss[1].wait(V)
                dsq[0].wait(V)
                dsq[1].wait(V)
                for c in range(nbatch):
                    sl = c % 2
                    if c >= 2:
                        sv.wait(SY, qv[sl])
                        sv.wait(G, qv[sl])
                        dss[sl].wait(V, qst[sl])
                    eng = SY if c % 2 == 0 else G
                    lv = dstg[sl].inc(eng.dma_start(
                        stg[sl][:, 0:bw].rearrange("p (c n) -> p c n", n=width),
                        src_h[c * nb * 128:(c + 1) * nb * 128, :].rearrange(
                            "(c p) n -> p c n", p=128)), 16)
                    dstg[sl].wait(V, lv)
                    sv.inc(V.tensor_scalar(qh16[:, 0:bw], stg[sl][:, 0:bw],
                                           wsc[:, sw_col:sw_col + 1], RM,
                                           op0=OP.mult, op1=OP.add))
                    sv.wait(V)
                    sv.inc(V.tensor_scalar(wqx[sl][:, 0:bw], qh16[:, 0:bw],
                                           -RM, -1.0, op0=OP.add, op1=OP.max))
                    sv.wait(V)
                    qv[sl] = sv.inc(V.tensor_scalar(wqx[sl][:, 0:bw], wqx[sl][:, 0:bw],
                                                    1.0, None, op0=OP.min))
                    sv.wait(G, qv[sl])
                    qst[sl] = dss[sl].inc(G.dma_start(
                        dst_sp[c * nb:(c + 1) * nb].rearrange("c p n -> p c n"),
                        wqx[sl][:, 0:bw].rearrange("p (c n) -> p c n", n=width)), 16)

            quant_weight(w1t_h, KC, DH, 0, w1qt_sp, mwcol=1)
            quant_weight(w2t_h, HC, H, 2, w2qt_sp)
            for s_ in (dstg[0], dstg[1], dss[0], dss[1], dsq[0], dsq[1], dc):
                s_.wait(SY)
        bar()

        _ph = ExitStack()
        attA = _ph.enter_context(nc.sbuf_tensor(_un("attA"), [128, OC * T], BF16))
        with _ph:
            A3 = attA[:, :].rearrange("p (c n) -> p c n", n=T)

            for s in range(NSEQ):
                _ph = ExitStack()
                xr = [_ph.enter_context(nc.sbuf_tensor(_un("xra"), [128, H], F32))
                      for _ in range(3)]
                ln1b0 = _ph.enter_context(nc.sbuf_tensor(_un("ln1b0"), [128, H], BF16))
                ln1b1 = _ph.enter_context(nc.sbuf_tensor(_un("ln1b1"), [128, H], BF16))
                ptrA = _ph.enter_context(nc.psum_tensor(_un("ptrA"), [128, 512], F32))
                ptrB = _ph.enter_context(nc.psum_tensor(_un("ptrB"), [128, 512], F32))
                with _ph:
                    ln1b = [ln1b0, ln1b1]
                    ptr = [ptrA, ptrB]
                    xload = [dxr[i].inc(SY.dma_start(
                        xr[i][:], x_h[s, i * 128:(i + 1) * 128, :]), 16) for i in range(3)]
                    tcv = [0, 0]
                    lnuse = [0, 0]
                    for i in range(NT128):
                        dxr[i % 3].wait(V, xload[i])
                        r_ap, mb_ap, mbv_ = ln_chain(xr[i % 3][:], (i % 4) * 4)
                        sv.wait(S, mbv_)
                        if i >= 2:
                            st.wait(S, lnuse[i % 2])
                        sev = ss.inc(S.activation(ln1b[i % 2][:], xr[i % 3][:], AF.Identity,
                                                  scale=r_ap, bias=mb_ap))
                        if i + 3 < NT128:
                            ss.wait(SY, sev)
                            xload.append(dxr[(i + 3) % 3].inc(SY.dma_start(
                                xr[(i + 3) % 3][:], x_h[s, (i + 3) * 128:(i + 4) * 128, :]), 16))
                        ss.wait(TE, sev)
                        for half in range(2):
                            sv.wait(TE, tcv[half])
                            for k4 in range(4):
                                kc = half * 4 + k4
                                _ins = TE.matmul(ptr[half][:, k4 * 128:(k4 + 1) * 128],
                                          ln1b[i % 2][:, kc * 128:(kc + 1) * 128],
                                          ident[:], start=True, stop=True)
                            tev = st.inc(_ins)
                            lnuse[i % 2] = tev
                            st.wait(V, tev)
                            tcv[half] = sv.inc(V.tensor_copy(
                                A3[:, half * 4:half * 4 + 4, i * 128:(i + 1) * 128],
                                ptr[half][:, :].rearrange("p (c n) -> p c n", n=128)))
                bar()

                _ph = ExitStack()
                wsb0 = _ph.enter_context(
                    nc.sbuf_tensor(_un("wsb0"), [128, 3 * KC * 128], BF16))
                wsb1 = _ph.enter_context(
                    nc.sbuf_tensor(_un("wsb1"), [128, 3 * KC * 128], BF16))
                vtmp = _ph.enter_context(nc.sbuf_tensor(_un("vtmp"), [128, 512], BF16))
                rot2 = [_ph.enter_context(nc.sbuf_tensor(_un("rot2"), [128, 512], BF16))
                        for _ in range(2)]
                rt1 = _ph.enter_context(nc.sbuf_tensor(_un("rt1"), [128, 512], BF16))
                rt2 = _ph.enter_context(nc.sbuf_tensor(_un("rt2"), [128, 512], BF16))
                sqb = _ph.enter_context(nc.sbuf_tensor(_un("sqb"), [128, 512], BF16))
                prodb = _ph.enter_context(nc.sbuf_tensor(_un("prodb"), [128, 8], BF16))
                pq = _ph.enter_context(nc.psum_tensor(_un("pq"), [128, 512], F32))
                pk = _ph.enter_context(nc.psum_tensor(_un("pk"), [128, 512], F32))
                pv = _ph.enter_context(nc.psum_tensor(_un("pv"), [128, 512], F32))
                ptrV = _ph.enter_context(nc.psum_tensor(_un("ptrV"), [128, 512], F32))
                pnb = _ph.enter_context(nc.psum_tensor(_un("pnb"), [2, 512], F32))
                pnb2 = _ph.enter_context(nc.psum_tensor(_un("pnb2"), [1, 8], F32))
                pbc = _ph.enter_context(nc.psum_tensor(_un("pbc"), [128, 8], F32))
                with _ph:
                    wsb = [wsb0, wsb1]
                    if s == 0:
                        VR = vres[:, :].rearrange("p (t c) -> p t c", c=130)
                        V.memset(VR[:, :, 64:65], 1.0)
                        onesv = sv.inc(V.memset(VR[:, :, 129:130], 1.0))
                    else:
                        onesv = 0
                    evq = evk = evv = 0
                    vtuse = 0
                    trv = 0
                    nbred = 0
                    nbmm = 0
                    pend_nb = [None]

                    def flush_nb():
                        nonlocal nbred, nbmm
                        if pend_nb[0] is None:
                            return
                        oc_, sqk = pend_nb[0]
                        pend_nb[0] = None
                        sv.wait(TE, sqk)
                        sv.wait(TE, nbred)
                        TE.matmul(pnb[:, 0:256], ones2[:], sqb[:, 0:256],
                                  start=True, stop=True)
                        nbmm = st.inc(TE.matmul(pnb[:, 256:512], ones2[:],
                                                sqb[:, 256:512],
                                                start=True, stop=True))
                        st.wait(V, nbmm)
                        V.tensor_reduce(nrm[0:2, oc_:oc_ + 1], pnb[:, 0:256],
                                        axis=AX.X, op=OP.max)
                        nbred = sv.inc(V.tensor_reduce(
                            nrm[0:2, 8 + oc_:9 + oc_], pnb[:, 256:512],
                            axis=AX.X, op=OP.max))
                    wldv = [0, 0]
                    wsuse = [0, 0]
                    wldv[0] = dstg[0].inc(SY.dma_start(wsb[0][:], wqkv_sp[0]), 16)
                    for oc in range(OC):
                        par = oc % 2
                        qview = qres[:, oc * T:(oc + 1) * T]
                        kview = kres[:, oc * T:(oc + 1) * T]
                        VD4 = vres[:, oc * NKT * 130:(oc + 1) * NKT * 130].rearrange(
                            "p (t g c) -> p t g c", g=2, c=65)
                        if oc + 1 < OC:
                            sl2 = (oc + 1) % 2
                            eng = SY if sl2 == 0 else G
                            st.wait(eng, wsuse[sl2])
                            wldv[sl2] = dstg[sl2].inc(
                                eng.dma_start(wsb[sl2][:], wqkv_sp[oc + 1]), 16)
                        dstg[par].wait(TE, wldv[par])
                        for j in range(NT512):
                            ss.wait(TE, evq)
                            for kc in range(KC):
                                _ins = TE.matmul(pq[:],
                                          wsb[par][:, (0 * KC + kc) * 128:
                                                   (0 * KC + kc) * 128 + 128],
                                          A3[:, kc, j * 512:(j + 1) * 512],
                                          start=(kc == 0), stop=(kc == KC - 1))
                            mmq = st.inc(_ins)
                            ss.wait(TE, evk)
                            for kc in range(KC):
                                _ins = TE.matmul(pk[:],
                                          wsb[par][:, (1 * KC + kc) * 128:
                                                   (1 * KC + kc) * 128 + 128],
                                          A3[:, kc, j * 512:(j + 1) * 512],
                                          start=(kc == 0), stop=(kc == KC - 1))
                            mmk = st.inc(_ins)
                            ss.wait(TE, evv)
                            st.wait(TE, vtuse)
                            for kc in range(KC):
                                _ins = TE.matmul(pv[:],
                                          wsb[par][:, (2 * KC + kc) * 128:
                                                   (2 * KC + kc) * 128 + 128],
                                          A3[:, kc, j * 512:(j + 1) * 512],
                                          start=(kc == 0), stop=(kc == KC - 1))
                            mmv = st.inc(_ins)
                            wsuse[par] = mmv
                            st.wait(S, mmq)
                            evq = ss.inc(S.activation(qview[:, j * 512:(j + 1) * 512], pq[:],
                                                      AF.Identity, scale=0.125,
                                                      bias=bq_pre[:, oc:oc + 1]))
                            st.wait(S, mmk)
                            evk = ss.inc(S.activation(kview[:, j * 512:(j + 1) * 512], pk[:],
                                                      AF.Identity, scale=1.0, bias=0.0))
                            st.wait(S, mmv)
                            evv = ss.inc(S.activation(vtmp[:], pv[:], AF.Identity,
                                                      scale=1.0, bias=bv_pre[:, oc:oc + 1]))
                            ss.wait(TE, evv)
                            ss.wait(TE, trv)
                            if s == 0:
                                sv.wait(TE, onesv)
                            for i4 in range(4):
                                _ins = TE.matmul(ptrV[:, i4 * 128:(i4 + 1) * 128],
                                          vtmp[:, i4 * 128:(i4 + 1) * 128], ident[:],
                                          start=True, stop=True)
                            vtuse = st.inc(_ins)
                            st.wait(S, vtuse)
                            trv = ss.inc(S.copy(
                                VD4[:, j * 4:j * 4 + 4, :, 0:64],
                                ptrV[:, :].rearrange("p (t g c) -> p t g c", g=2, c=64)))
                        flush_nb()
                        ss.wait(V)
                        rolast = 0
                        rits = [(qview, 0, 2), (qview, 0, 3),
                                (kview, 2, 2), (kview, 2, 3)]
                        for it in range(4):
                            buf, dstoff, jj = rits[it]
                            src = buf[:, jj * 512:(jj + 1) * 512]
                            ccols = cosfm[:, (jj - 2) * 512:(jj - 1) * 512]
                            scols = sinfm[:, (jj - 2) * 512:(jj - 1) * 512]
                            sv.wait(V)
                            for hh in range(2):
                                b0 = hh * 64
                                V.tensor_scalar(rot2[0][b0:b0 + 32, :],
                                                src[b0 + 32:b0 + 64, :],
                                                -1.0, None, op0=OP.mult)
                                V.tensor_scalar(rot2[0][b0 + 32:b0 + 64, :],
                                                src[b0:b0 + 32, :],
                                                1.0, None, op0=OP.mult)
                            sv.inc(V.tensor_tensor(rt1[:], src, ccols, op=OP.mult))
                            sv.wait(V)
                            sv.inc(V.tensor_tensor(rt2[:], rot2[0][:], scols,
                                                   op=OP.mult))
                            sv.wait(V)
                            rolast = sv.inc(V.tensor_tensor(
                                src, rt1[:], rt2[:], op=OP.add))
                        st.wait(V, nbmm)
                        sv.wait(V)
                        V.tensor_tensor(sqb[:, 0:128], qview[:, 0:1024:8],
                                        qview[:, 0:1024:8], op=OP.mult)
                        V.tensor_tensor(sqb[:, 128:192], qview[:, 1024:1536:8],
                                        qview[:, 1024:1536:8], op=OP.mult)
                        V.tensor_tensor(sqb[:, 192:256], qview[:, 1536:2048:8],
                                        qview[:, 1536:2048:8], op=OP.mult)
                        V.tensor_tensor(sqb[:, 256:384], kview[:, 0:1024:8],
                                        kview[:, 0:1024:8], op=OP.mult)
                        V.tensor_tensor(sqb[:, 384:448], kview[:, 1024:1536:8],
                                        kview[:, 1024:1536:8], op=OP.mult)
                        sq_k = sv.inc(V.tensor_tensor(
                            sqb[:, 448:512], kview[:, 1536:2048:8],
                            kview[:, 1536:2048:8], op=OP.mult))
                        pend_nb[0] = (oc, sq_k)
                    flush_nb()
                    sv.wait(S)
                    ss.inc(S.activation(nrm[0:2, 16:24], nrm[0:2, 0:8], AF.Sqrt))
                    ss.wait(S)
                    ssq2 = ss.inc(S.activation(nrm[0:2, 0:8], nrm[0:2, 8:16], AF.Sqrt))
                    ss.wait(V, ssq2)
                    sv.inc(V.tensor_tensor(nrm[0:2, 8:16], nrm[0:2, 16:24], nrm[0:2, 0:8],
                                           op=OP.mult))
                    sv.wait(V)
                    pv_ = sv.inc(V.tensor_copy(prodb[0:2, 0:8], nrm[0:2, 8:16]))
                    sv.wait(TE, pv_)
                    tb = st.inc(TE.matmul(pnb2[:], ones2[0:2, 0:1], prodb[0:2, 0:8],
                                          start=True, stop=True))
                    st.wait(V, tb)
                    ngv = sv.inc(V.tensor_scalar(nrm[0:1, 0:8], pnb2[:], -1.0, None, op0=OP.mult))
                    sv.wait(TE, ngv)
                    tb2 = st.inc(TE.matmul(pbc[:], ones1f[:], nrm[0:1, 0:8],
                                           start=True, stop=True))
                    st.wait(V, tb2)
                    sv.inc(V.tensor_copy(negM[:, 0:8], pbc[:]))
                bar()

                _ph = ExitStack()
                ptil0 = _ph.enter_context(nc.sbuf_tensor(_un("ptil0"), [128, 1024], BF16))
                ptil1 = _ph.enter_context(nc.sbuf_tensor(_un("ptil1"), [128, 1024], BF16))
                ptil2 = _ph.enter_context(nc.sbuf_tensor(_un("ptil2"), [128, 1024], BF16))
                tmpB0 = _ph.enter_context(nc.sbuf_tensor(_un("tmpB0"), [64, T], BF16))
                tmpB1 = _ph.enter_context(nc.sbuf_tensor(_un("tmpB1"), [64, T], BF16))
                invrA = _ph.enter_context(nc.sbuf_tensor(_un("invrA"), [1, T], F32))
                invrB = _ph.enter_context(nc.sbuf_tensor(_un("invrB"), [1, T], F32))
                lbufA = _ph.enter_context(nc.sbuf_tensor(_un("lbufA"), [1, T], F32))
                lbufB = _ph.enter_context(nc.sbuf_tensor(_un("lbufB"), [1, T], F32))
                invbA2 = _ph.enter_context(nc.sbuf_tensor(_un("invbA2"), [64, T], BF16))
                invbB2 = _ph.enter_context(nc.sbuf_tensor(_un("invbB2"), [64, T], BF16))
                sc0 = _ph.enter_context(nc.psum_tensor(_un("sc0"), [128, 1024], F32))
                sc1 = _ph.enter_context(nc.psum_tensor(_un("sc1"), [128, 1024], F32))
                accA = _ph.enter_context(nc.psum_tensor(_un("accA"), [65, 512], F32))
                accA2 = _ph.enter_context(nc.psum_tensor(_un("accA2"), [65, 512], F32))
                accB = _ph.enter_context(nc.psum_tensor(_un("accB"), [65, 512], F32))
                pib = _ph.enter_context(nc.psum_tensor(_un("pib"), [128, 512], F32))
                with _ph:
                    ptil = [ptil0, ptil1, ptil2]
                    tmpB = [tmpB0, tmpB1]
                    sc = [sc0, sc1]
                    expv = [0, 0, 0]
                    avv3 = [0, 0, 0]
                    epiv = 0
                    epivA = [0, 0]
                    epivB = 0
                    bcv = 0
                    ibcp = 0
                    tailv = 0
                    erv_ = 0
                    dmaBv = [0, 0]
                    smmv = {}
                    NSTEP = 4 * NKT
                    pend = [None]

                    def qkd(p):
                        return (qres[:, p * T:(p + 1) * T],
                                kres[:, p * T:(p + 1) * T],
                                vres[:, p * NKT * 130:(p + 1) * NKT * 130]
                                .rearrange("p (t n) -> p t n", n=130))

                    def issue_smm(g):
                        p, i = divmod(g, NSTEP)
                        qc, kt = divmod(i, NKT)
                        sl = g % 2
                        qc_, kc_, _ = qkd(p)
                        TE.matmul(sc[sl][:, 0:512], kc_[0:64, kt * 128:(kt + 1) * 128],
                                  qc_[0:64, qc * 512:(qc + 1) * 512],
                                  start=True, stop=True, tile_position=(0, 0))
                        smmv[g] = st.inc(TE.matmul(
                            sc[sl][:, 512:1024], kc_[64:128, kt * 128:(kt + 1) * 128],
                            qc_[64:128, qc * 512:(qc + 1) * 512],
                            start=True, stop=True, tile_position=(64, 0)))

                    def flush_tailA(r):
                        nonlocal erv_
                        if r == 0:
                            st.wait(V, bcv)
                            sv.wait(V)
                        V.reciprocal(invrA[0:1, r * 512:(r + 1) * 512],
                                     lbufA[0:1, r * 512:(r + 1) * 512])
                        erv_ = sv.inc(V.reciprocal(
                            invrB[0:1, r * 512:(r + 1) * 512],
                            lbufB[0:1, r * 512:(r + 1) * 512]))

                    def flush_tailB():
                        nonlocal bcv, ibcp, tailv
                        ep, erv = pend[0]
                        pend[0] = None
                        sv.wait(TE, erv_)
                        for r in range(4):
                            sv.wait(TE, ibcp)
                            TE.matmul(pib[0:64, :], ones1f[0:1, 0:64],
                                      invrA[0:1, r * 512:(r + 1) * 512],
                                      start=True, stop=True, tile_position=(0, 0))
                            bcv = st.inc(TE.matmul(
                                pib[64:128, :], ones1f[0:1, 0:64],
                                invrB[0:1, r * 512:(r + 1) * 512],
                                start=True, stop=True, tile_position=(0, 64)))
                            st.wait(V, bcv)
                            V.tensor_copy(invbA2[:, r * 512:(r + 1) * 512],
                                          pib[0:64, :])
                            ibcp = sv.inc(V.tensor_copy(
                                invbB2[:, r * 512:(r + 1) * 512], pib[64:128, :]))
                        sv.wait(V)
                        V.tensor_tensor(
                            attA[0:64, ep * T:(ep + 1) * T],
                            attA[0:64, ep * T:(ep + 1) * T],
                            invbA2[:, :], op=OP.mult)
                        tailv = sv.inc(V.tensor_tensor(
                            tmpB[ep % 2][:, :], tmpB[ep % 2][:, :],
                            invbB2[:, :], op=OP.mult))
                        sv.wait(SY, tailv)
                        dmaBv[ep % 2] = dtb[ep % 2].inc(SY.dma_start(
                            attA[64:128, ep * T:(ep + 1) * T], tmpB[ep % 2][:]), 16)

                    issue_smm(0)
                    issue_smm(1)
                    for g in range(NPAIR * NSTEP):
                        p, i = divmod(g, NSTEP)
                        qc, kt = divmod(i, NKT)
                        sl = g % 2
                        if pend[0] is not None and i in (2, 10, 18, 26):
                            flush_tailA((2, 10, 18, 26).index(i))
                        if i == 2 * NKT + 4 and pend[0] is not None:
                            flush_tailB()
                        _, _, dc4 = qkd(p)
                        pl = g % 3
                        st.wait(S, smmv[g])
                        st.wait(S, avv3[pl])
                        expv[pl] = ss.inc(S.activation(
                            ptil[pl][:], sc[sl][:], AF.Exp,
                            bias=negM[:, p:p + 1], scale=1.0))
                        ss.wait(TE, expv[pl])
                        if g + 2 < NPAIR * NSTEP:
                            issue_smm(g + 2)
                        accAd = accA if qc % 2 == 0 else accA2
                        if kt == 0:
                            sv.wait(TE, epivA[qc % 2])
                        TE.matmul(accAd[:], dc4[:, kt, 0:65], ptil[pl][:, 0:512],
                                  start=(kt == 0), stop=(kt == NKT - 1))
                        if kt == 0:
                            sv.wait(TE, epivB)
                        avv3[pl] = st.inc(TE.matmul(
                            accB[:], dc4[:, kt, 65:130], ptil[pl][:, 512:1024],
                            start=(kt == 0), stop=(kt == NKT - 1)))
                        if kt == NKT - 1:
                            st.wait(V, avv3[pl])
                            sv.wait(V)
                            if qc == 0 and p >= 2:
                                dtb[p % 2].wait(V, dmaBv[p % 2])
                            V.tensor_copy(lbufB[0:1, qc * 512:(qc + 1) * 512],
                                          accB[64:65, :])
                            epivB = sv.inc(V.tensor_copy(
                                tmpB[p % 2][:, qc * 512:(qc + 1) * 512],
                                accB[0:64, :]))
                            V.tensor_copy(lbufA[0:1, qc * 512:(qc + 1) * 512],
                                          accAd[64:65, :])
                            epivA[qc % 2] = epiv = sv.inc(V.tensor_copy(
                                attA[0:64, p * T + qc * 512:p * T + (qc + 1) * 512],
                                accAd[0:64, :]))
                            if qc == 3:
                                pend[0] = (p, epiv)
                    for r_ in range(4):
                        flush_tailA(r_)
                    flush_tailB()
                    dtb[0].wait(SY)
                    dtb[1].wait(SY)
                bar()

                _ph = ExitStack()
                xr = [_ph.enter_context(nc.sbuf_tensor(_un("xrb"), [128, H], F32))
                      for _ in range(4)]
                ln3t = _ph.enter_context(nc.sbuf_tensor(_un("ln3t"), [128, H], F32))
                x2t = [_ph.enter_context(nc.sbuf_tensor(_un("x2t"), [128, H], F32))
                       for _ in range(4)]
                ln2t = _ph.enter_context(nc.sbuf_tensor(_un("ln2t"), [128, H], F32))
                qx16 = _ph.enter_context(nc.sbuf_tensor(_un("qx16"), [128, H], F16))
                xqtm = _ph.enter_context(nc.sbuf_tensor(_un("xqtm"), [128, H], BF16))
                xqs0 = _ph.enter_context(nc.sbuf_tensor(_un("xqs0"), [128, 512], BF16))
                xqs1 = _ph.enter_context(nc.sbuf_tensor(_un("xqs1"), [128, 512], BF16))
                rowA = _ph.enter_context(nc.sbuf_tensor(_un("rowA"), [1, T], F32))
                pw0 = _ph.enter_context(nc.psum_tensor(_un("pw0"), [128, 1024], F32))
                pw1 = _ph.enter_context(nc.psum_tensor(_un("pw1"), [128, 1024], F32))
                pw2 = _ph.enter_context(nc.psum_tensor(_un("pw2"), [128, 1024], F32))
                pw3 = _ph.enter_context(nc.psum_tensor(_un("pw3"), [128, 1024], F32))
                with _ph:
                    xqs = [xqs0, xqs1]
                    pw = [pw0, pw1, pw2, pw3]
                    wo_use = [0, 0]
                    evacv = [0] * 4
                    xq_dma = [0, 0]
                    ln3use_s = 0
                    ln2use_s = 0
                    xqtmuse_s = 0
                    x2dma = [0] * 4
                    x2rds = [0] * 4
                    xruse = [0] * 4
                    for g in range(4):
                        for kc in range(KC):
                            if kc == 0:
                                for i in range(4):
                                    ss.wait(TE, evacv[i])
                            for i in range(4):
                                ti = g * 4 + i
                                for hh in range(2):
                                    _ins = TE.matmul(pw[i][:, hh * 512:(hh + 1) * 512],
                                              A3[:, kc, ti * 128:(ti + 1) * 128],
                                              wo3[:, kc * H + hh * 512:
                                                  kc * H + (hh + 1) * 512],
                                              start=(kc == 0), stop=False)
                            st.inc(_ins)
                        for i in range(4):
                            for hh in range(2):
                                _ins = TE.matmul(pw[i][:, hh * 512:(hh + 1) * 512],
                                          ones_row[0:1, 0:128],
                                          bo_row[0:1, hh * 512:(hh + 1) * 512],
                                          start=False, stop=True)
                        womm = st.inc(_ins)
                        xvv = [0] * 4
                        v2s = [0] * 4
                        for i in range(4):
                            ti = g * 4 + i
                            sv.wait(SY, xruse[i])
                            xvv[i] = dxr[i].inc(SY.dma_start(
                                xr[i][:], x_h[s, ti * 128:(ti + 1) * 128, :]), 16)
                            st.wait(V, womm)
                            v2s[i] = ln_stats(pw[i][:], 16 + i * 4)
                        fin3 = [ln_fin(16 + i * 4, v2s[i]) for i in range(4)]
                        for i in range(4):
                            ti = g * 4 + i
                            r3, mb3, mbv3 = fin3[i]
                            sv.wait(S, mbv3)
                            sv.wait(S, ln3use_s)
                            l3v = ss.inc(S.activation(ln3t[:], pw[i][:],
                                                      AF.Identity,
                                                      scale=r3, bias=mb3))
                            dxr[i].wait(V, xvv[i])
                            ss.wait(V, l3v)
                            dx2[i % 2].wait(V)
                            ss.wait(V, x2rds[i])
                            addv = sv.inc(V.tensor_tensor(
                                x2t[i][:], xr[i][:], ln3t[:], op=OP.add))
                            ln3use_s = addv
                            xruse[i] = addv
                            sv.wait(SY, addv)
                            x2dma[i] = dx2[i % 2].inc(SY.dma_start(
                                x2_sp[s, ti * 128:(ti + 1) * 128, :], x2t[i][:]), 16)
                        sv.wait(V)
                        for i in range(4):
                            v2s[i] = ln_stats(x2t[i][:], 32 + i * 4)
                        fin2 = [ln_fin(32 + i * 4, v2s[i]) for i in range(4)]
                        for i in range(4):
                            ti = g * 4 + i
                            qo = 48 + i * 4
                            r2, mb2, mbv2 = fin2[i]
                            sv.wait(S, mbv2)
                            sv.wait(S, ln2use_s)
                            l2v = ss.inc(S.activation(ln2t[:], x2t[i][:],
                                                      AF.Identity,
                                                      scale=r2, bias=mb2))
                            x2rds[i] = l2v
                            ss.wait(V, l2v)
                            sv.inc(V.tensor_reduce(
                                stat[:, qo:qo + 1], ln2t[:], axis=AX.X,
                                op=OP.max, apply_absolute_value=True))
                            sv.wait(V)
                            sv.inc(V.tensor_scalar(stat[:, qo + 1:qo + 2],
                                                   stat[:, qo:qo + 1], 1e-5,
                                                   None, op0=OP.max))
                            sv.wait(V)
                            sv.inc(V.reciprocal(stat[:, qo + 2:qo + 3],
                                                stat[:, qo + 1:qo + 2]))
                            sv.wait(V)
                            sv.inc(V.tensor_scalar(
                                stat[:, qo + 3:qo + 4], stat[:, qo + 2:qo + 3],
                                127.0, None, op0=OP.mult))
                            sv.inc(V.tensor_scalar(c1sb[:, ti:ti + 1],
                                                   stat[:, qo + 1:qo + 2],
                                                   wsc[:, 1:2], None, op0=OP.mult))
                            sv.wait(V)
                            st.wait(V, xqtmuse_s)
                            sv.inc(V.tensor_scalar(qx16[:], ln2t[:],
                                                   stat[:, qo + 3:qo + 4], RM,
                                                   op0=OP.mult, op1=OP.add))
                            sv.wait(V)
                            xqv = sv.inc(V.tensor_scalar(
                                xqtm[:], qx16[:], -RM, None,
                                op0=OP.add))
                            ln2use_s = xqv
                            sv.wait(TE, xqv)
                            for half in range(2):
                                dqs[half].wait(TE, xq_dma[half])
                                for k4 in range(4):
                                    kc = half * 4 + k4
                                    _ins = TE.matmul(pw[i][:, half * 512 + k4 * 128:
                                                    half * 512 + (k4 + 1) * 128],
                                              xqtm[:, kc * 128:(kc + 1) * 128],
                                              ident[:], start=True, stop=True)
                                thv = xqtmuse_s = st.inc(_ins)
                                st.wait(S, thv)
                                cpv = ss.inc(S.copy(
                                    xqs[half][:], pw[i][:, half * 512:(half + 1) * 512]))
                                ss.wait(SY, cpv)
                                xq_dma[half] = dqs[half].inc(SY.dma_start(
                                    xqd_sp[s][half * 4:half * 4 + 4, :,
                                              ti * 128:(ti + 1) * 128].rearrange(
                                                  "c p n -> p c n"),
                                    xqs[half][:, :].rearrange("p (c n) -> p c n", n=128)), 16)
                            evacv[i] = ss.v
                    sv.wait(TE)
                    dqs[0].wait(TE, xq_dma[0])
                    dqs[1].wait(TE, xq_dma[1])
                    for ti in range(NT128):
                        dst = pw0 if ti < 8 else pw1
                        _ins = TE.matmul(dst[0:1, (ti % 8) * 128:(ti % 8 + 1) * 128],
                                  c1sb[:, ti:ti + 1], identf[:], start=True, stop=True)
                    tcv2 = st.inc(_ins)
                    st.wait(V, tcv2)
                    V.tensor_copy(rowA[0:1, 0:1024], pw0[0:1, :])
                    rv2 = sv.inc(V.tensor_copy(rowA[0:1, 1024:2048], pw1[0:1, :]))
                    sv.wait(SY, rv2)
                    dcr.inc(SY.dma_start(c1_sp[s][None, :], rowA[:]), 16)
                    for s_ in (dcr, dx2[0], dx2[1], dqs[0], dqs[1]):
                        s_.wait(SY)
                bar()

        w3_es.close()
        bar()
        _ph = ExitStack()
        w1q = _ph.enter_context(nc.sbuf_tensor(_un("w1q"), [128, KC * DH], BF16))
        xqd = _ph.enter_context(nc.sbuf_tensor(_un("xqd"), [128, KC * T], BF16))
        diag = _ph.enter_context(nc.sbuf_tensor(_un("diag"), [128, 3 * HC * 128], BF16))
        c1f = _ph.enter_context(nc.sbuf_tensor(_un("c1f"), [128, T], F32))
        crow = _ph.enter_context(nc.sbuf_tensor(_un("crow"), [1, T], F32))
        hsc2 = [_ph.enter_context(nc.sbuf_tensor(_un("hsc2"), [128, T + 2], BF16))
                for _ in range(2)]
        hsb = _ph.enter_context(nc.sbuf_tensor(_un("hsb"), [128, T], BF16))
        macc = _ph.enter_context(nc.sbuf_tensor(_un("macc"), [128, T], BF16))
        mincc = _ph.enter_context(nc.sbuf_tensor(_un("mincc"), [128, T], BF16))
        s2row = _ph.enter_context(nc.sbuf_tensor(_un("s2row"), [1, T], F32))
        pwA = _ph.enter_context(nc.psum_tensor(_un("pwA"), [128, 512], F32))
        pwB = _ph.enter_context(nc.psum_tensor(_un("pwB"), [128, 512], F32))
        pcA = _ph.enter_context(nc.psum_tensor(_un("pcA"), [128, 512], F32))
        pcB = _ph.enter_context(nc.psum_tensor(_un("pcB"), [128, 512], F32))
        pam = _ph.enter_context(nc.psum_tensor(_un("pam"), [128, 256], F32))
        prow = _ph.enter_context(nc.psum_tensor(_un("prow"), [1, 512], F32))
        pbc1 = _ph.enter_context(nc.psum_tensor(_un("pbc1"), [128, 512], F32))
        with _ph:
            pwx = [pwA, pwB]
            pcx = [pcA, pcB]
            w1ld = 0
            for kc in range(KC):
                w1ld = drl.inc(G.dma_start(w1q[:, kc * DH:(kc + 1) * DH], w1qt_sp[kc]), 16)
            dgv = 0
            for c in range(HC):
                for j3 in range(3):
                    dgv = sv.inc(V.tensor_scalar(
                        diag[:, (c * 3 + j3) * 128:(c * 3 + j3 + 1) * 128],
                        ident[:], convw_s[:, c * 3 + j3:c * 3 + j3 + 1], None, op0=OP.mult))
            V.memset(hsc2[0][:, 0:1], 0.0)
            V.memset(hsc2[0][:, T + 1:T + 2], 0.0)
            V.memset(hsc2[1][:, 0:1], 0.0)
            hscz = sv.inc(V.memset(hsc2[1][:, T + 1:T + 2], 0.0))
            spillv = 0
            mm_prev = 0
            hscuse = [0, 0]
            for s in range(NSEQ):
                st.wait(G)
                sv.wait(G)
                st.wait(V)
                for kc in range(KC):
                    drl.inc(G.dma_start(xqd[:, kc * T:(kc + 1) * T], xqd_sp[s, kc]), 16)
                drl.inc(G.dma_start(crow[:], c1_sp[s][None, :]), 16)
                drl.wait(TE)
                sv.wait(S)
                c1bv = 0
                for jbc in range(NT512):
                    if jbc > 0:
                        ss.wait(TE, c1bv)
                    tcb = st.inc(TE.matmul(pbc1[:], ones1f[:],
                                           crow[0:1, jbc * 512:(jbc + 1) * 512],
                                           start=True, stop=True))
                    st.wait(S, tcb)
                    c1bv = ss.inc(S.copy(c1f[:, jbc * 512:(jbc + 1) * 512], pbc1[:]))
                V.memset(macc[:], -3e38)
                mmi = sv.inc(V.memset(mincc[:], 3e38))
                drl.wait(TE)
                pwv = [0, 0]
                silv = [0, 0]
                hval = {}

                def w1_block(c):
                    nonlocal mmi
                    hs_ = hsc2[c % 2]
                    for j in range(NT512):
                        slw = j % 2
                        sv.wait(TE, pwv[slw])
                        for kc in range(KC):
                            _ins = TE.matmul(pwx[slw][:],
                                      w1q[:, kc * DH + c * 128:kc * DH + (c + 1) * 128],
                                      xqd[:, kc * T + j * 512:kc * T + (j + 1) * 512],
                                      start=(kc == 0), stop=(kc == KC - 1))
                        wmm = st.inc(_ins)
                        st.wait(V, wmm)
                        ss.wait(V, c1bv)
                        sv.wait(V, hscz)
                        st.wait(V, hscuse[c % 2])
                        pwv[slw] = sv.inc(V.tensor_tensor(
                            hs_[:, 1 + j * 512:1 + (j + 1) * 512], pwx[slw][:],
                            c1f[:, j * 512:(j + 1) * 512], op=OP.mult))
                    hval[c] = sv.v

                def conv_block(c):
                    nonlocal mmi, mm_prev, spillv
                    hs_ = hsc2[c % 2]
                    for j in range(NT512):
                        slc = j % 2
                        sv.wait(TE, hval[c])
                        if c == 0:
                            sv.wait(TE, dgv)
                        ss.wait(TE, silv[slc])
                        for tap in range(3):
                            _ins = TE.matmul(pcx[slc][:],
                                      diag[:, (c * 3 + tap) * 128:(c * 3 + tap + 1) * 128],
                                      hs_[:, tap + j * 512:tap + j * 512 + 512],
                                      start=(tap == 0), stop=(tap == 2))
                        cmm = st.inc(_ins)
                        hscuse[c % 2] = cmm
                        st.wait(S, cmm)
                        dhs.wait(S, spillv)
                        sv.wait(S, mm_prev)
                        silv[slc] = ss.inc(S.activation(hsb[:, j * 512:(j + 1) * 512],
                                                        pcx[slc][:], AF.Silu,
                                                        bias=convb_s[:, c:c + 1], scale=1.0))
                    ss.wait(V, silv[(NT512 - 1) % 2])
                    sv.wait(V, mmi)
                    V.tensor_tensor(macc[:], macc[:], hsb[:], op=OP.max)
                    mmi = mm_prev = sv.inc(V.tensor_tensor(mincc[:], mincc[:], hsb[:], op=OP.min))
                    ss.wait(SY, silv[(NT512 - 1) % 2])
                    spillv = dhs.inc(SY.dma_start(hs_sp[s, c], hsb[:]), 16)

                w1_block(0)
                for c in range(HC):
                    if c + 1 < HC:
                        w1_block(c + 1)
                    conv_block(c)
                sv.wait(TE, mmi)
                pamv = 0
                for ti in range(NT128):
                    sv.wait(TE, pamv)
                    TE.matmul(pam[:, 0:128], macc[:, ti * 128:(ti + 1) * 128], ident[:],
                              start=True, stop=True)
                    tav = st.inc(TE.matmul(pam[:, 128:256], mincc[:, ti * 128:(ti + 1) * 128],
                                           ident[:], start=True, stop=True))
                    st.wait(V, tav)
                    V.tensor_reduce(stat[:, 40:41], pam[:, 0:128], axis=AX.X, op=OP.max)
                    pamv = sv.inc(V.tensor_reduce(stat[:, 41:42], pam[:, 128:256], axis=AX.X,
                                                  op=OP.min, negate=True))
                    sv.wait(V)
                    sv.inc(V.tensor_tensor(stat[:, 42:43], stat[:, 40:41], stat[:, 41:42],
                                           op=OP.max))
                    sv.wait(V)
                    sv.inc(V.tensor_scalar(stat[:, 43:44], stat[:, 42:43], 1e-5, None,
                                           op0=OP.max))
                    sv.wait(V)
                    sv.inc(V.reciprocal(stat[:, 44:45], stat[:, 43:44]))
                    sv.wait(V)
                    sv.inc(V.tensor_scalar(s2s[:, ti:ti + 1], stat[:, 44:45], 127.0,
                                           None, op0=OP.mult))
                sv.wait(TE)
                dsw.wait(V)
                srv = 0
                for tg in range(4):
                    if tg > 0:
                        sv.wait(TE, srv)
                    for t4 in range(4):
                        ti = tg * 4 + t4
                        _ins = TE.matmul(prow[0:1, t4 * 128:(t4 + 1) * 128], s2s[:, ti:ti + 1],
                                  identf[:], start=True, stop=True)
                    trv2 = st.inc(_ins)
                    st.wait(V, trv2)
                    srv = sv.inc(V.tensor_copy(s2row[0:1, tg * 512:(tg + 1) * 512], prow[0:1, :]))
                sv.wait(SY, srv)
                dsw.inc(SY.dma_start(s2_sp[s][None, :], s2row[:]), 16)
            dhs.wait(SY)
            dsw.wait(SY)
            drl.wait(SY)
        bar()

        _ph = ExitStack()
        w2q = _ph.enter_context(nc.sbuf_tensor(_un("w2q"), [128, HC * H], BF16))
        ring0 = _ph.enter_context(nc.sbuf_tensor(_un("ring0"), [128, HC * 512], BF16))
        ring1 = _ph.enter_context(nc.sbuf_tensor(_un("ring1"), [128, HC * 512], BF16))
        srow0 = _ph.enter_context(nc.sbuf_tensor(_un("srow0"), [1, T], F32))
        srow1 = _ph.enter_context(nc.sbuf_tensor(_un("srow1"), [1, T], F32))
        s2b0 = _ph.enter_context(nc.sbuf_tensor(_un("s2b0"), [128, 512], BF16))
        s2b1 = _ph.enter_context(nc.sbuf_tensor(_un("s2b1"), [128, 512], BF16))
        m216a = _ph.enter_context(nc.sbuf_tensor(_un("m216a"), [128, 512], F16))
        m216b = _ph.enter_context(nc.sbuf_tensor(_un("m216b"), [128, 512], F16))
        q216a = _ph.enter_context(nc.sbuf_tensor(_un("q216a"), [128, 512], F16))
        q216b = _ph.enter_context(nc.sbuf_tensor(_un("q216b"), [128, 512], F16))
        ln4t = _ph.enter_context(nc.sbuf_tensor(_un("ln4t"), [128, H], F32))
        x2r0 = _ph.enter_context(nc.sbuf_tensor(_un("x2r0"), [128, H], F32))
        x2r1 = _ph.enter_context(nc.sbuf_tensor(_un("x2r1"), [128, H], F32))
        outt0 = _ph.enter_context(nc.sbuf_tensor(_un("outt0"), [128, H], F32))
        outt1 = _ph.enter_context(nc.sbuf_tensor(_un("outt1"), [128, H], F32))
        p2a = _ph.enter_context(nc.psum_tensor(_un("p2a"), [128, 1024], F32))
        p2b = _ph.enter_context(nc.psum_tensor(_un("p2b"), [128, 1024], F32))
        pbs = _ph.enter_context(nc.psum_tensor(_un("pbs"), [128, 512], F32))
        with _ph:
            ring = [ring0, ring1]
            s2b = [s2b0, s2b1]
            x2r = [x2r0, x2r1]
            outt = [outt0, outt1]
            p2 = [p2a, p2b]
            for c in range(HC):
                drl.inc(G.dma_start(w2q[:, c * H:(c + 1) * H], w2qt_sp[c]), 16)
            ringuse = [0, 0]
            s2use = [0, 0]
            addv2 = [0, 0]
            rsubv = [0, 0]
            outdma = [0, 0]
            x2v_use = [0, 0]
            evv_ = [0, 0]
            pbsuse = 0
            srow = [srow0, srow1]
            srldv = [0, 0]
            sj = [(s, j) for s in range(NSEQ) for j in range(NT512)]
            NSJ = len(sj)
            ringldv = {}
            rndall_v = {}

            def load_ring(idx):
                s_, j_ = sj[idx]
                slr_ = idx % 2
                if j_ == 0:
                    drl.wait(G)
                    srldv[s_] = drl.inc(
                        G.dma_start(srow[s_][:], s2_sp[s_][None, :]), 16)
                st.wait(SY, ringuse[slr_])
                ringldv[idx] = dr2[slr_].inc(SY.dma_start(
                    ring[slr_][:, :].rearrange("p (c n) -> p c n", n=512),
                    hs_sp[s_][:, :, j_ * 512:(j_ + 1) * 512].rearrange(
                        "c p n -> p c n")), 16)

            def quant(idx):
                nonlocal pbsuse
                s_, j_ = sj[idx]
                slr_ = idx % 2
                drl.wait(TE, srldv[s_])
                ss.wait(TE, pbsuse)
                tb4 = st.inc(TE.matmul(pbs[:], ones1f[:],
                                       srow[s_][0:1, j_ * 512:(j_ + 1) * 512],
                                       start=True, stop=True))
                st.wait(S, tb4)
                sv.wait(S, s2use[slr_])
                s2bv = pbsuse = ss.inc(S.copy(s2b[slr_][:], pbs[:]))
                ss.wait(V, s2bv)
                dr2[slr_].wait(V, ringldv[idx])
                q216 = [q216a, q216b]
                m216 = [m216a, m216b]
                mvs = {}

                def qmult(c_):
                    slt_ = c_ % 2
                    ss.wait(V, addv2[slt_])
                    mvs[c_] = sv.inc(V.tensor_tensor(
                        m216[slt_][:], ring[slr_][:, c_ * 512:(c_ + 1) * 512],
                        s2b[slr_][:], op=OP.mult))

                qmult(0)
                for c in range(HC):
                    slt = c % 2
                    if c + 1 < HC:
                        qmult(c + 1)
                    sv.wait(S, mvs[c])
                    sv.wait(S, rsubv[slt])
                    addv2[slt] = ss.inc(S.activation(
                        q216[slt][:], m216[slt][:], AF.Identity,
                        bias=posRM[:, 0:1], scale=1.0))
                    ss.wait(V, addv2[slt])
                    rsubv[slt] = sv.inc(V.tensor_scalar(
                        ring[slr_][:, c * 512:(c + 1) * 512], q216[slt][:],
                        -RM, None, op0=OP.add))
                s2use[slr_] = sv.v
                rndall_v[idx] = sv.v

            load_ring(0)
            quant(0)
            for idx in range(NSJ):
                s, j = sj[idx]
                slr = idx % 2
                if idx + 1 < NSJ:
                    load_ring(idx + 1)
                    quant(idx + 1)
                rndall = rndall_v[idx]
                for i2 in range(2):
                    for half in range(2):
                            ti = j * 4 + i2 * 2 + half
                            pp = p2[half]
                            sv.wait(TE, rndall)
                            sv.wait(TE, evv_[half])
                            for c in range(HC):
                                base = c * 512 + (i2 * 2 + half) * 128
                                for hh in range(2):
                                    _ins = TE.matmul(pp[:, hh * 512:(hh + 1) * 512],
                                              ring[slr][:, base:base + 128],
                                              w2q[:, c * H + hh * 512:c * H + (hh + 1) * 512],
                                              start=(c == 0), stop=(c == HC - 1))
                            mmv_ = st.inc(_ins)
                            ringuse[slr] = mmv_
                            sv.wait(SY, x2v_use[half])
                            xv = dxa[half].inc(SY.dma_start(
                                x2r[half][:], x2_sp[s, ti * 128:(ti + 1) * 128, :]), 16)
                            st.wait(V, mmv_)
                            r4, mb4, mbv4 = ln_chain(pp[:], 48 + half * 4)
                            sv.wait(S, mbv4)
                            l4v = ss.inc(S.activation(ln4t[:], pp[:], AF.Identity,
                                                      scale=r4, bias=mb4))
                            dxa[half].wait(V, xv)
                            ss.wait(V, l4v)
                            dou[half].wait(V, outdma[half])
                            ov = sv.inc(V.tensor_tensor(outt[half][:], x2r[half][:], ln4t[:],
                                                        op=OP.add))
                            evv_[half] = ov
                            x2v_use[half] = ov
                            sv.wait(SY, ov)
                            outdma[half] = dou[half].inc(SY.dma_start(
                                out_h[s, ti * 128:(ti + 1) * 128, :], outt[half][:]), 16)
            dou[0].wait(SY)
            dou[1].wait(SY)
        bar()

    return nc, dbg_outs


def host_inputs(inputs):
    x = np.ascontiguousarray(inputs["L0_attn_hidden_states"])
    common = dict(
        wq=np.ascontiguousarray(inputs["L0_attn_wq"]),
        wk=np.ascontiguousarray(inputs["L0_attn_wk"]),
        wv=np.ascontiguousarray(inputs["L0_attn_wv"]),
        wo=np.ascontiguousarray(inputs["L0_attn_wo"]),
        bq=np.ascontiguousarray(np.asarray(inputs["L0_attn_bq"]).reshape(OC, 128).T),
        bv=np.ascontiguousarray(np.asarray(inputs["L0_attn_bv"]).reshape(OC, 128).T),
        bo=np.ascontiguousarray(inputs["L0_attn_bo"]),
        cosT=np.ascontiguousarray(np.asarray(inputs["L0_attn_cos"]).T),
        sinT=np.ascontiguousarray(np.asarray(inputs["L0_attn_sin"]).T),
        w1t=np.ascontiguousarray(np.asarray(inputs["L0_ffn_w1"]).T),
        cw=np.ascontiguousarray(np.asarray(inputs["L0_ffn_conv_w"]).reshape(HC, 128, 3)
                                .transpose(1, 0, 2).reshape(128, HC * 3)),
        cb=np.ascontiguousarray(np.asarray(inputs["L0_ffn_conv_b"]).reshape(HC, 128).T),
        w2t=np.ascontiguousarray(np.asarray(inputs["L0_ffn_w2"]).T),
    )
    maps = []
    for c in range(8):
        m = dict(common)
        m["x"] = np.ascontiguousarray(x[c * NSEQ:(c + 1) * NSEQ])
        maps.append(m)
    return maps


def kernel(**inputs):
    maps = host_inputs(inputs)
    nc, _ = build_nc()
    res = run_bass_kernel_spmd(nc, maps, core_ids=list(range(8)))
    out = np.concatenate([np.asarray(r["out"]) for r in res.results], axis=0)
    return out.astype(np.float32)


# version 20
# speedup vs baseline: 1.0942x; 1.0003x over previous
import numpy as np
import concourse.bass as bass
import concourse.mybir as mybir
from concourse import library_config as _lc
_LIB = {l.name: l for l in _lc.all_libraries}
from concourse.bass_utils import run_bass_kernel_spmd
from contextlib import ExitStack

F32 = mybir.dt.float32
F16 = mybir.dt.float16
BF16 = mybir.dt.bfloat16
I32 = mybir.dt.int32
AF = mybir.ActivationFunctionType
AX = mybir.AxisListType
OP = mybir.AluOpType

H, NH, HD, T, NSEQ = 1024, 16, 64, 2048, 2
DH = 4096
NT128, NT512 = T // 128, T // 512
OC, HC, KC = H // 128, DH // 128, H // 128
NPAIR, NKT = NH // 2, T // 128
EPS = 1e-5
RM = 1536.0


class Sem:
    def __init__(self, es, nc, name):
        self.h = es.enter_context(nc.semaphore(name))
        self.v = 0

    def inc(self, ins, n=1):
        ins.then_inc(self.h, n)
        self.v += n
        return self.v

    def wait(self, eng, v=None):
        eng.wait_ge(self.h, self.v if v is None else v)


def build_nc(dbg=()):
    nc = bass.Bass("TRN2", target_bir_lowering=False)
    _nctr = [0]

    def _un(n):
        _nctr[0] += 1
        return f"{n}_{_nctr[0]}"
    x_h = nc.dram_tensor("x", [NSEQ, T, H], F32, kind="ExternalInput").ap()
    wq_h = nc.dram_tensor("wq", [H, H], F32, kind="ExternalInput").ap()
    wk_h = nc.dram_tensor("wk", [H, H], F32, kind="ExternalInput").ap()
    wv_h = nc.dram_tensor("wv", [H, H], F32, kind="ExternalInput").ap()
    wo_h = nc.dram_tensor("wo", [H, H], F32, kind="ExternalInput").ap()
    bq_h = nc.dram_tensor("bq", [128, OC], F32, kind="ExternalInput").ap()
    bv_h = nc.dram_tensor("bv", [128, OC], F32, kind="ExternalInput").ap()
    bo_h = nc.dram_tensor("bo", [H], F32, kind="ExternalInput").ap()
    cos_h = nc.dram_tensor("cosT", [HD, 1024], F32, kind="ExternalInput").ap()
    sin_h = nc.dram_tensor("sinT", [HD, 1024], F32, kind="ExternalInput").ap()
    w1t_h = nc.dram_tensor("w1t", [H, DH], F32, kind="ExternalInput").ap()
    cw_h = nc.dram_tensor("cw", [128, 3 * HC], F32, kind="ExternalInput").ap()
    cb_h = nc.dram_tensor("cb", [128, HC], F32, kind="ExternalInput").ap()
    w2t_h = nc.dram_tensor("w2t", [DH, H], F32, kind="ExternalInput").ap()
    out_h = nc.dram_tensor("out", [NSEQ, T, H], F32, kind="ExternalOutput").ap()

    wo_sp = nc.dram_tensor("wo_sp", [KC, 128, H], BF16, kind="Internal").ap()
    w1qt_sp = nc.dram_tensor("w1qt_sp", [KC, 128, DH], BF16, kind="Internal").ap()
    w2qt_sp = nc.dram_tensor("w2qt_sp", [HC, 128, H], BF16, kind="Internal").ap()
    wqkv_sp = nc.dram_tensor("wqkv_sp", [OC, 128, 3 * KC * 128], BF16,
                             kind="Internal").ap()
    xqd_sp = nc.dram_tensor("xqd_sp", [NSEQ, KC, 128, T], BF16, kind="Internal").ap()
    x2_sp = nc.dram_tensor("x2_sp", [NSEQ, T, H], F32, kind="Internal").ap()
    hs_sp = nc.dram_tensor("hs_sp", [NSEQ, HC, 128, T], BF16, kind="Internal").ap()
    c1_sp = nc.dram_tensor("c1_sp", [NSEQ, T], F32, kind="Internal").ap()
    s2_sp = nc.dram_tensor("s2_sp", [NSEQ, T], F32, kind="Internal").ap()

    V, S, TE, G, SY = nc.vector, nc.scalar, nc.tensor, nc.gpsimd, nc.sync

    es = ExitStack()
    sbuf = lambda n, shape, d=F32: es.enter_context(nc.sbuf_tensor(n, shape, d))

    dbg_outs = {}
    for name, shape, d in dbg:
        dbg_outs[name] = nc.dram_tensor(name, shape, d, kind="ExternalOutput").ap()

    with es:
        ident = sbuf("ident", [128, 128], BF16)
        identf = sbuf("identf", [128, 128], F32)
        ones1f = sbuf("ones1f", [1, 128], F32)
        cosfm = sbuf("cosfm", [128, 1024], BF16)
        sinfm = sbuf("sinfm", [128, 1024], BF16)
        bq_pre = sbuf("bq_pre", [128, OC], F32)
        bv_pre = sbuf("bv_pre", [128, OC], F32)
        bo_row = sbuf("bo_row", [1, H], BF16)
        ones_row = sbuf("ones_row", [1, 128], BF16)
        ones2 = sbuf("ones2", [128, 2], BF16)
        convw_s = sbuf("convw_s", [128, 3 * HC], F32)
        convb_s = sbuf("convb_s", [128, HC], F32)
        negM = sbuf("negM", [128, NPAIR], F32)
        negRM = sbuf("negRM", [128, 1], F32)
        posRM = sbuf("posRM", [128, 1], F32)
        wsc = sbuf("wsc", [128, 4], F32)
        stat = sbuf("stat", [128, 64], F32)
        bnbuf = sbuf("bnbuf", [128, 12], F32)
        nrm = sbuf("nrm", [128, 24], F32)
        s2s = sbuf("s2s", [128, NT128], F32)
        c1sb = sbuf("c1sb", [128, NT128], F32)
        w3_es = ExitStack()
        wo3 = w3_es.enter_context(nc.sbuf_tensor(_un("wo3"), [128, KC * H], BF16))
        qres = w3_es.enter_context(nc.sbuf_tensor(_un("qres"), [128, OC * T], BF16))
        kres = w3_es.enter_context(nc.sbuf_tensor(_un("kres"), [128, OC * T], BF16))
        vres = w3_es.enter_context(
            nc.sbuf_tensor(_un("vres"), [128, NPAIR * NKT * 130], BF16))

        sems = {}
        for name in ["dc", "drl", "tv", "ts", "tt", "tg",
                     "dxr0", "dxr1", "dxr2", "dxr3", "dst0", "dst1", "dss0", "dss1",
                     "dsq0", "dsq1", "dsk0", "dsk1", "dsd0", "dsd1",
                     "datt0", "datt1", "dtb0", "dtb1", "dwo0", "dwo1",
                     "dx20", "dx21", "dqs0", "dqs1", "dcr", "dhs",
                     "dr20", "dr21", "dxa0", "dxa1", "dou0", "dou1", "dsw"]:
            sems[name] = Sem(es, nc, name)
        dc, drl = sems["dc"], sems["drl"]
        sv, ss, st, sg = (sems[k] for k in ["tv", "ts", "tt", "tg"])
        dxr = [sems["dxr0"], sems["dxr1"], sems["dxr2"], sems["dxr3"]]
        dstg = [sems["dst0"], sems["dst1"]]
        dss = [sems["dss0"], sems["dss1"]]
        dsq = [sems["dsq0"], sems["dsq1"]]
        dsk = [sems["dsk0"], sems["dsk1"]]
        dsd = [sems["dsd0"], sems["dsd1"]]
        datt = [sems["datt0"], sems["datt1"]]
        dtb = [sems["dtb0"], sems["dtb1"]]
        dwo = [sems["dwo0"], sems["dwo1"]]
        dx2 = [sems["dx20"], sems["dx21"]]
        dqs = [sems["dqs0"], sems["dqs1"]]
        dcr, dhs = sems["dcr"], sems["dhs"]
        dr2 = [sems["dr20"], sems["dr21"]]
        dxa = [sems["dxa0"], sems["dxa1"]]
        dou = [sems["dou0"], sems["dou1"]]
        dsw = sems["dsw"]
        for s_ in sems.values():
            SY.sem_clear(s_.h)
        nc.all_engine_barrier()

        def bar():
            nc.all_engine_barrier()

        def ln_stats(src_ap, so):
            V.bn_stats(bnbuf[:, 0:6], src_ap[:, 0:512])
            sv.inc(V.bn_stats(bnbuf[:, 6:12], src_ap[:, 512:1024]))
            sv.wait(V)
            sv.inc(V.bn_aggr(stat[:, so:so + 2], bnbuf[:, 0:12]))
            sv.wait(V)
            sv.inc(V.tensor_scalar(stat[:, so + 2:so + 3], stat[:, so + 1:so + 2],
                                   EPS, None, op0=OP.add))
            sv.wait(V)
            return sv.inc(V.reciprocal(stat[:, so + 3:so + 4],
                                       stat[:, so + 2:so + 3]))

        def ln_fin(so, v2):
            sv.wait(S, v2)
            s1 = ss.inc(S.activation(stat[:, so + 2:so + 3], stat[:, so + 3:so + 4],
                                     AF.Sqrt))
            ss.wait(V, s1)
            sv.wait(V)
            mbv = sv.inc(V.tensor_scalar(stat[:, so + 3:so + 4], stat[:, so:so + 1],
                                         stat[:, so + 2:so + 3], -1.0,
                                         op0=OP.mult, op1=OP.mult))
            return stat[:, so + 2:so + 3], stat[:, so + 3:so + 4], mbv

        def ln_chain(src_ap, so):
            return ln_fin(so, ln_stats(src_ap, so))

        _ph = ExitStack()
        it_f = _ph.enter_context(nc.sbuf_tensor(_un("it_f"), [128, 128], I32))
        it_p = _ph.enter_context(nc.sbuf_tensor(_un("it_p"), [128, 1], F32))
        trigA = _ph.enter_context(nc.sbuf_tensor(_un("trigA"), [64, 1024], F32))
        trigB = _ph.enter_context(nc.sbuf_tensor(_un("trigB"), [64, 1024], F32))
        borowf = _ph.enter_context(nc.sbuf_tensor(_un("borowf"), [1, H], F32))
        wstg0 = _ph.enter_context(nc.sbuf_tensor(_un("wstg0"), [128, DH], F32))
        wstg1 = _ph.enter_context(nc.sbuf_tensor(_un("wstg1"), [128, DH], F32))
        qh16 = _ph.enter_context(nc.sbuf_tensor(_un("qh16"), [128, DH], F16))
        wqa = _ph.enter_context(nc.sbuf_tensor(_un("wqa"), [128, DH], BF16))
        wqb = _ph.enter_context(nc.sbuf_tensor(_un("wqb"), [128, DH], BF16))
        pst = _ph.enter_context(nc.psum_tensor(_un("pst"), [128, 128], F32))
        with _ph:
            stg = [wstg0, wstg1]
            wqx = [wqa, wqb]
            G.iota(it_f[:], pattern=[[1, 128]], base=0, channel_multiplier=0)
            sg.inc(G.iota(it_p[:], pattern=[[0, 1]], base=0, channel_multiplier=1,
                   allow_small_or_imprecise_dtypes=True))
            sg.wait(V)
            V.tensor_scalar(ident[:], it_f[:], it_p[:], None, op0=OP.is_equal)
            V.tensor_scalar(identf[:], it_f[:], it_p[:], None, op0=OP.is_equal)
            V.memset(ones_row[:], 1.0)
            V.memset(ones1f[:], 1.0)
            V.memset(negRM[:], -RM)
            V.memset(posRM[:], RM)
            sv.inc(V.memset(ones2[:], 0.0))
            sv.wait(V)
            V.memset(ones2[0:64, 0:1], 1.0)
            sv.inc(V.memset(ones2[64:128, 1:2], 1.0))

            dc.inc(SY.dma_start(stat[:, 0:OC], bq_h), 16)
            dc.inc(SY.dma_start(bv_pre[:], bv_h), 16)
            dc.inc(SY.dma_start(borowf[:], bo_h[None, :]), 16)
            dc.inc(SY.dma_start(convw_s[:], cw_h), 16)
            dc.inc(SY.dma_start(convb_s[:], cb_h), 16)
            dc.inc(SY.dma_start(trigA[:], cos_h), 16)
            dc.inc(SY.dma_start(trigB[:], sin_h), 16)
            dc.wait(V)
            sv.inc(V.tensor_scalar(bq_pre[:], stat[:, 0:OC], 0.125, None, op0=OP.mult))
            sv.inc(V.tensor_copy(bo_row[:], borowf[:]))
            V.tensor_copy(cosfm[0:64, :], trigA[:])
            V.tensor_copy(cosfm[64:128, :], trigA[:])
            V.tensor_copy(sinfm[0:64, :], trigB[:])
            sv.inc(V.tensor_copy(sinfm[64:128, :], trigB[:]))

            wsrc = [wq_h, wk_h, wv_h]
            castv = []
            wspill = [0, 0]
            for i, (wh, b) in enumerate([(w, k) for w in range(3) for k in range(2)]):
                eng = SY if i % 2 == 0 else G
                if i >= 2:
                    sv.wait(eng, castv[i - 2])
                lv = dstg[i % 2].inc(eng.dma_start(
                    stg[i % 2][:, 0:4 * H].rearrange("p (c n) -> p c n", n=H),
                    wsrc[wh][b * 512:(b + 1) * 512, :].rearrange(
                        "(c p) n -> p c n", p=128)), 16)
                dstg[i % 2].wait(V, lv)
                if i >= 2:
                    dsq[i % 2].wait(V, wspill[i % 2])
                castv.append(sv.inc(V.tensor_copy(
                    wqx[i % 2][:, 0:4 * H], stg[i % 2][:, 0:4 * H])))
                sv.wait(SY, castv[-1])
                for k4 in range(4):
                    tix = wh * KC + b * 4 + k4
                    wspill[i % 2] = dsq[i % 2].inc(SY.dma_start(
                        wqkv_sp[:, :, tix * 128:(tix + 1) * 128].rearrange(
                            "o p n -> p o n"),
                        wqx[i % 2][:, k4 * H:(k4 + 1) * H].rearrange(
                            "p (o n) -> p o n", n=128)), 16)
            for b in range(2):
                sl = b % 2
                eng = SY if sl == 0 else G
                sv.wait(eng, castv[-2])
                lv = dstg[sl].inc(eng.dma_start(
                    stg[sl][:, 0:4 * H].rearrange("p (c n) -> p c n", n=H),
                    wo_h[b * 512:(b + 1) * 512, :].rearrange(
                        "(c p) n -> p c n", p=128)), 16)
                dstg[sl].wait(V, lv)
                castv.append(sv.inc(V.tensor_copy(
                    wo3[:, b * 4 * H:(b + 1) * 4 * H], stg[sl][:, 0:4 * H])))

            def quant_weight(src_h, nchunk, width, sw_col, dst_sp, mwcol=None):
                nb = DH // width
                nbatch = nchunk // nb
                bw = nb * width
                redv = []
                sv.wait(SY)
                sv.wait(G)
                for c in range(nbatch):
                    if c >= 2:
                        sv.wait(SY, redv[c - 2])
                        sv.wait(G, redv[c - 2])
                    eng = SY if c % 2 == 0 else G
                    lv = dstg[c % 2].inc(eng.dma_start(
                        stg[c % 2][:, 0:bw].rearrange("p (c n) -> p c n", n=width),
                        src_h[c * nb * 128:(c + 1) * nb * 128, :].rearrange(
                            "(c p) n -> p c n", p=128)), 16)
                    dstg[c % 2].wait(V, lv)
                    redv.append(sv.inc(V.tensor_reduce(
                        nrm[:, c % 16:c % 16 + 1], stg[c % 2][:, 0:bw],
                        axis=AX.X, op=OP.add, apply_absolute_value=True)))
                sv.wait(V)
                sv.inc(V.tensor_reduce(nrm[:, 20:21], nrm[:, 0:nbatch],
                                       axis=AX.X, op=OP.add))
                sv.wait(TE)
                tq1 = st.inc(TE.matmul(pst[0:1, 0:128], nrm[:, 20:21], identf[:],
                                       start=True, stop=True))
                st.wait(V, tq1)
                sv.inc(V.tensor_reduce(nrm[0:1, 21:22], pst[0:1, 0:128],
                                       axis=AX.X, op=OP.add))
                sv.wait(TE)
                tq2 = st.inc(TE.matmul(pst[:, 0:1], ones1f[:], nrm[0:1, 21:22],
                                       start=True, stop=True))
                st.wait(V, tq2)
                sv.inc(V.tensor_copy(nrm[:, 21:22], pst[:, 0:1]))
                sv.wait(V)
                sv.inc(V.reciprocal(nrm[:, 22:23], nrm[:, 21:22]))
                sv.wait(V)
                nel = float(nchunk * 128 * width)
                sv.inc(V.tensor_scalar(wsc[:, sw_col:sw_col + 1], nrm[:, 22:23], nel,
                                       None, op0=OP.mult))
                if mwcol is not None:
                    sv.inc(V.tensor_scalar(wsc[:, mwcol:mwcol + 1], nrm[:, 21:22],
                                           1.0 / (nel * 127.0), None, op0=OP.mult))
                sv.wait(V)
                qv = [0, 0]
                qst = [0, 0]
                sv.wait(SY)
                sv.wait(G)
                dss[0].wait(V)
                dss[1].wait(V)
                dsq[0].wait(V)
                dsq[1].wait(V)
                for c in range(nbatch):
                    sl = c % 2
                    if c >= 2:
                        sv.wait(SY, qv[sl])
                        sv.wait(G, qv[sl])
                        dss[sl].wait(V, qst[sl])
                    eng = SY if c % 2 == 0 else G
                    lv = dstg[sl].inc(eng.dma_start(
                        stg[sl][:, 0:bw].rearrange("p (c n) -> p c n", n=width),
                        src_h[c * nb * 128:(c + 1) * nb * 128, :].rearrange(
                            "(c p) n -> p c n", p=128)), 16)
                    dstg[sl].wait(V, lv)
                    sv.inc(V.tensor_scalar(qh16[:, 0:bw], stg[sl][:, 0:bw],
                                           wsc[:, sw_col:sw_col + 1], RM,
                                           op0=OP.mult, op1=OP.add))
                    sv.wait(V)
                    sv.inc(V.tensor_scalar(wqx[sl][:, 0:bw], qh16[:, 0:bw],
                                           -RM, -1.0, op0=OP.add, op1=OP.max))
                    sv.wait(V)
                    qv[sl] = sv.inc(V.tensor_scalar(wqx[sl][:, 0:bw], wqx[sl][:, 0:bw],
                                                    1.0, None, op0=OP.min))
                    sv.wait(G, qv[sl])
                    qst[sl] = dss[sl].inc(G.dma_start(
                        dst_sp[c * nb:(c + 1) * nb].rearrange("c p n -> p c n"),
                        wqx[sl][:, 0:bw].rearrange("p (c n) -> p c n", n=width)), 16)

            quant_weight(w1t_h, KC, DH, 0, w1qt_sp, mwcol=1)
            quant_weight(w2t_h, HC, H, 2, w2qt_sp)
            for s_ in (dstg[0], dstg[1], dss[0], dss[1], dsq[0], dsq[1], dc):
                s_.wait(SY)
        bar()

        _ph = ExitStack()
        attA = _ph.enter_context(nc.sbuf_tensor(_un("attA"), [128, OC * T], BF16))
        with _ph:
            A3 = attA[:, :].rearrange("p (c n) -> p c n", n=T)

            for s in range(NSEQ):
                _ph = ExitStack()
                xr = [_ph.enter_context(nc.sbuf_tensor(_un("xra"), [128, H], F32))
                      for _ in range(3)]
                ln1b0 = _ph.enter_context(nc.sbuf_tensor(_un("ln1b0"), [128, H], BF16))
                ln1b1 = _ph.enter_context(nc.sbuf_tensor(_un("ln1b1"), [128, H], BF16))
                ptrA = _ph.enter_context(nc.psum_tensor(_un("ptrA"), [128, 512], F32))
                ptrB = _ph.enter_context(nc.psum_tensor(_un("ptrB"), [128, 512], F32))
                with _ph:
                    ln1b = [ln1b0, ln1b1]
                    ptr = [ptrA, ptrB]
                    xload = [dxr[i].inc(SY.dma_start(
                        xr[i][:], x_h[s, i * 128:(i + 1) * 128, :]), 16) for i in range(3)]
                    tcv = [0, 0]
                    lnuse = [0, 0]
                    for i in range(NT128):
                        dxr[i % 3].wait(V, xload[i])
                        r_ap, mb_ap, mbv_ = ln_chain(xr[i % 3][:], (i % 4) * 4)
                        sv.wait(S, mbv_)
                        if i >= 2:
                            st.wait(S, lnuse[i % 2])
                        sev = ss.inc(S.activation(ln1b[i % 2][:], xr[i % 3][:], AF.Identity,
                                                  scale=r_ap, bias=mb_ap))
                        if i + 3 < NT128:
                            ss.wait(SY, sev)
                            xload.append(dxr[(i + 3) % 3].inc(SY.dma_start(
                                xr[(i + 3) % 3][:], x_h[s, (i + 3) * 128:(i + 4) * 128, :]), 16))
                        ss.wait(TE, sev)
                        for half in range(2):
                            sv.wait(TE, tcv[half])
                            for k4 in range(4):
                                kc = half * 4 + k4
                                _ins = TE.matmul(ptr[half][:, k4 * 128:(k4 + 1) * 128],
                                          ln1b[i % 2][:, kc * 128:(kc + 1) * 128],
                                          ident[:], start=True, stop=True)
                            tev = st.inc(_ins)
                            lnuse[i % 2] = tev
                            st.wait(V, tev)
                            tcv[half] = sv.inc(V.tensor_copy(
                                A3[:, half * 4:half * 4 + 4, i * 128:(i + 1) * 128],
                                ptr[half][:, :].rearrange("p (c n) -> p c n", n=128)))
                bar()

                _ph = ExitStack()
                wsb0 = _ph.enter_context(
                    nc.sbuf_tensor(_un("wsb0"), [128, 3 * KC * 128], BF16))
                wsb1 = _ph.enter_context(
                    nc.sbuf_tensor(_un("wsb1"), [128, 3 * KC * 128], BF16))
                vtmp = _ph.enter_context(nc.sbuf_tensor(_un("vtmp"), [128, 512], BF16))
                rot2 = [_ph.enter_context(nc.sbuf_tensor(_un("rot2"), [128, 512], BF16))
                        for _ in range(2)]
                rt1 = _ph.enter_context(nc.sbuf_tensor(_un("rt1"), [128, 512], BF16))
                rt2 = _ph.enter_context(nc.sbuf_tensor(_un("rt2"), [128, 512], BF16))
                sqb = _ph.enter_context(nc.sbuf_tensor(_un("sqb"), [128, 512], BF16))
                prodb = _ph.enter_context(nc.sbuf_tensor(_un("prodb"), [128, 8], BF16))
                pq = _ph.enter_context(nc.psum_tensor(_un("pq"), [128, 512], F32))
                pk = _ph.enter_context(nc.psum_tensor(_un("pk"), [128, 512], F32))
                pv = _ph.enter_context(nc.psum_tensor(_un("pv"), [128, 512], F32))
                ptrV = _ph.enter_context(nc.psum_tensor(_un("ptrV"), [128, 512], F32))
                pnb = _ph.enter_context(nc.psum_tensor(_un("pnb"), [2, 512], F32))
                pnb2 = _ph.enter_context(nc.psum_tensor(_un("pnb2"), [1, 8], F32))
                pbc = _ph.enter_context(nc.psum_tensor(_un("pbc"), [128, 8], F32))
                with _ph:
                    wsb = [wsb0, wsb1]
                    if s == 0:
                        VR = vres[:, :].rearrange("p (t c) -> p t c", c=130)
                        V.memset(VR[:, :, 64:65], 1.0)
                        onesv = sv.inc(V.memset(VR[:, :, 129:130], 1.0))
                    else:
                        onesv = 0
                    evq = evk = evv = 0
                    vtuse = 0
                    trv = 0
                    nbred = 0
                    nbmm = 0
                    pend_nb = [None]

                    def flush_nb():
                        nonlocal nbred, nbmm
                        if pend_nb[0] is None:
                            return
                        oc_, sqk = pend_nb[0]
                        pend_nb[0] = None
                        sv.wait(TE, sqk)
                        sv.wait(TE, nbred)
                        TE.matmul(pnb[:, 0:256], ones2[:], sqb[:, 0:256],
                                  start=True, stop=True)
                        nbmm = st.inc(TE.matmul(pnb[:, 256:512], ones2[:],
                                                sqb[:, 256:512],
                                                start=True, stop=True))
                        st.wait(V, nbmm)
                        V.tensor_reduce(nrm[0:2, oc_:oc_ + 1], pnb[:, 0:256],
                                        axis=AX.X, op=OP.max)
                        nbred = sv.inc(V.tensor_reduce(
                            nrm[0:2, 8 + oc_:9 + oc_], pnb[:, 256:512],
                            axis=AX.X, op=OP.max))
                    wldv = [0, 0]
                    wsuse = [0, 0]
                    wldv[0] = dstg[0].inc(SY.dma_start(wsb[0][:], wqkv_sp[0]), 16)
                    for oc in range(OC):
                        par = oc % 2
                        qview = qres[:, oc * T:(oc + 1) * T]
                        kview = kres[:, oc * T:(oc + 1) * T]
                        VD4 = vres[:, oc * NKT * 130:(oc + 1) * NKT * 130].rearrange(
                            "p (t g c) -> p t g c", g=2, c=65)
                        if oc + 1 < OC:
                            sl2 = (oc + 1) % 2
                            eng = SY if sl2 == 0 else G
                            st.wait(eng, wsuse[sl2])
                            wldv[sl2] = dstg[sl2].inc(
                                eng.dma_start(wsb[sl2][:], wqkv_sp[oc + 1]), 16)
                        dstg[par].wait(TE, wldv[par])
                        for j in range(NT512):
                            ss.wait(TE, evq)
                            for kc in range(KC):
                                _ins = TE.matmul(pq[:],
                                          wsb[par][:, (0 * KC + kc) * 128:
                                                   (0 * KC + kc) * 128 + 128],
                                          A3[:, kc, j * 512:(j + 1) * 512],
                                          start=(kc == 0), stop=(kc == KC - 1))
                            mmq = st.inc(_ins)
                            ss.wait(TE, evk)
                            for kc in range(KC):
                                _ins = TE.matmul(pk[:],
                                          wsb[par][:, (1 * KC + kc) * 128:
                                                   (1 * KC + kc) * 128 + 128],
                                          A3[:, kc, j * 512:(j + 1) * 512],
                                          start=(kc == 0), stop=(kc == KC - 1))
                            mmk = st.inc(_ins)
                            ss.wait(TE, evv)
                            st.wait(TE, vtuse)
                            for kc in range(KC):
                                _ins = TE.matmul(pv[:],
                                          wsb[par][:, (2 * KC + kc) * 128:
                                                   (2 * KC + kc) * 128 + 128],
                                          A3[:, kc, j * 512:(j + 1) * 512],
                                          start=(kc == 0), stop=(kc == KC - 1))
                            mmv = st.inc(_ins)
                            wsuse[par] = mmv
                            st.wait(S, mmq)
                            evq = ss.inc(S.activation(qview[:, j * 512:(j + 1) * 512], pq[:],
                                                      AF.Identity, scale=0.125,
                                                      bias=bq_pre[:, oc:oc + 1]))
                            st.wait(S, mmk)
                            evk = ss.inc(S.activation(kview[:, j * 512:(j + 1) * 512], pk[:],
                                                      AF.Identity, scale=1.0, bias=0.0))
                            st.wait(S, mmv)
                            evv = ss.inc(S.activation(vtmp[:], pv[:], AF.Identity,
                                                      scale=1.0, bias=bv_pre[:, oc:oc + 1]))
                            ss.wait(TE, evv)
                            ss.wait(TE, trv)
                            if s == 0:
                                sv.wait(TE, onesv)
                            for i4 in range(4):
                                _ins = TE.matmul(ptrV[:, i4 * 128:(i4 + 1) * 128],
                                          vtmp[:, i4 * 128:(i4 + 1) * 128], ident[:],
                                          start=True, stop=True)
                            vtuse = st.inc(_ins)
                            st.wait(S, vtuse)
                            trv = ss.inc(S.copy(
                                VD4[:, j * 4:j * 4 + 4, :, 0:64],
                                ptrV[:, :].rearrange("p (t g c) -> p t g c", g=2, c=64)))
                        flush_nb()
                        ss.wait(V)
                        rolast = 0
                        rits = [(qview, 0, 2), (qview, 0, 3),
                                (kview, 2, 2), (kview, 2, 3)]
                        for it in range(4):
                            buf, dstoff, jj = rits[it]
                            src = buf[:, jj * 512:(jj + 1) * 512]
                            ccols = cosfm[:, (jj - 2) * 512:(jj - 1) * 512]
                            scols = sinfm[:, (jj - 2) * 512:(jj - 1) * 512]
                            sv.wait(V)
                            for hh in range(2):
                                b0 = hh * 64
                                V.tensor_scalar(rot2[0][b0:b0 + 32, :],
                                                src[b0 + 32:b0 + 64, :],
                                                -1.0, None, op0=OP.mult)
                                V.tensor_scalar(rot2[0][b0 + 32:b0 + 64, :],
                                                src[b0:b0 + 32, :],
                                                1.0, None, op0=OP.mult)
                            sv.inc(V.tensor_tensor(rt1[:], src, ccols, op=OP.mult))
                            sv.wait(V)
                            sv.inc(V.tensor_tensor(rt2[:], rot2[0][:], scols,
                                                   op=OP.mult))
                            sv.wait(V)
                            rolast = sv.inc(V.tensor_tensor(
                                src, rt1[:], rt2[:], op=OP.add))
                        st.wait(V, nbmm)
                        sv.wait(V)
                        V.tensor_tensor(sqb[:, 0:128], qview[:, 0:1024:8],
                                        qview[:, 0:1024:8], op=OP.mult)
                        V.tensor_tensor(sqb[:, 128:192], qview[:, 1024:1536:8],
                                        qview[:, 1024:1536:8], op=OP.mult)
                        V.tensor_tensor(sqb[:, 192:256], qview[:, 1536:2048:8],
                                        qview[:, 1536:2048:8], op=OP.mult)
                        V.tensor_tensor(sqb[:, 256:384], kview[:, 0:1024:8],
                                        kview[:, 0:1024:8], op=OP.mult)
                        V.tensor_tensor(sqb[:, 384:448], kview[:, 1024:1536:8],
                                        kview[:, 1024:1536:8], op=OP.mult)
                        sq_k = sv.inc(V.tensor_tensor(
                            sqb[:, 448:512], kview[:, 1536:2048:8],
                            kview[:, 1536:2048:8], op=OP.mult))
                        pend_nb[0] = (oc, sq_k)
                    flush_nb()
                    sv.wait(S)
                    ss.inc(S.activation(nrm[0:2, 16:24], nrm[0:2, 0:8], AF.Sqrt))
                    ss.wait(S)
                    ssq2 = ss.inc(S.activation(nrm[0:2, 0:8], nrm[0:2, 8:16], AF.Sqrt))
                    ss.wait(V, ssq2)
                    sv.inc(V.tensor_tensor(nrm[0:2, 8:16], nrm[0:2, 16:24], nrm[0:2, 0:8],
                                           op=OP.mult))
                    sv.wait(V)
                    pv_ = sv.inc(V.tensor_copy(prodb[0:2, 0:8], nrm[0:2, 8:16]))
                    sv.wait(TE, pv_)
                    tb = st.inc(TE.matmul(pnb2[:], ones2[0:2, 0:1], prodb[0:2, 0:8],
                                          start=True, stop=True))
                    st.wait(V, tb)
                    ngv = sv.inc(V.tensor_scalar(nrm[0:1, 0:8], pnb2[:], -1.0, None, op0=OP.mult))
                    sv.wait(TE, ngv)
                    tb2 = st.inc(TE.matmul(pbc[:], ones1f[:], nrm[0:1, 0:8],
                                           start=True, stop=True))
                    st.wait(V, tb2)
                    sv.inc(V.tensor_copy(negM[:, 0:8], pbc[:]))
                bar()

                _ph = ExitStack()
                ptil0 = _ph.enter_context(nc.sbuf_tensor(_un("ptil0"), [128, 1024], BF16))
                ptil1 = _ph.enter_context(nc.sbuf_tensor(_un("ptil1"), [128, 1024], BF16))
                ptil2 = _ph.enter_context(nc.sbuf_tensor(_un("ptil2"), [128, 1024], BF16))
                tmpB0 = _ph.enter_context(nc.sbuf_tensor(_un("tmpB0"), [64, T], BF16))
                tmpB1 = _ph.enter_context(nc.sbuf_tensor(_un("tmpB1"), [64, T], BF16))
                invrA = _ph.enter_context(nc.sbuf_tensor(_un("invrA"), [1, T], F32))
                invrB = _ph.enter_context(nc.sbuf_tensor(_un("invrB"), [1, T], F32))
                lbufA = _ph.enter_context(nc.sbuf_tensor(_un("lbufA"), [1, T], F32))
                lbufB = _ph.enter_context(nc.sbuf_tensor(_un("lbufB"), [1, T], F32))
                invbA2 = _ph.enter_context(nc.sbuf_tensor(_un("invbA2"), [64, T], BF16))
                invbB2 = _ph.enter_context(nc.sbuf_tensor(_un("invbB2"), [64, T], BF16))
                sc0 = _ph.enter_context(nc.psum_tensor(_un("sc0"), [128, 1024], F32))
                sc1 = _ph.enter_context(nc.psum_tensor(_un("sc1"), [128, 1024], F32))
                accA = _ph.enter_context(nc.psum_tensor(_un("accA"), [65, 512], F32))
                accA2 = _ph.enter_context(nc.psum_tensor(_un("accA2"), [65, 512], F32))
                accB = _ph.enter_context(nc.psum_tensor(_un("accB"), [65, 512], F32))
                pib = _ph.enter_context(nc.psum_tensor(_un("pib"), [128, 512], F32))
                with _ph:
                    ptil = [ptil0, ptil1, ptil2]
                    tmpB = [tmpB0, tmpB1]
                    sc = [sc0, sc1]
                    expv = [0, 0, 0]
                    avv3 = [0, 0, 0]
                    epiv = 0
                    epivA = [0, 0]
                    epivB = 0
                    bcv = 0
                    ibcp = 0
                    tailv = 0
                    erv_ = 0
                    dmaBv = [0, 0]
                    smmv = {}
                    NSTEP = 4 * NKT
                    pend = [None]

                    def qkd(p):
                        return (qres[:, p * T:(p + 1) * T],
                                kres[:, p * T:(p + 1) * T],
                                vres[:, p * NKT * 130:(p + 1) * NKT * 130]
                                .rearrange("p (t n) -> p t n", n=130))

                    def issue_smm(g):
                        p, i = divmod(g, NSTEP)
                        qc, kt = divmod(i, NKT)
                        sl = g % 2
                        qc_, kc_, _ = qkd(p)
                        TE.matmul(sc[sl][:, 0:512], kc_[0:64, kt * 128:(kt + 1) * 128],
                                  qc_[0:64, qc * 512:(qc + 1) * 512],
                                  start=True, stop=True, tile_position=(0, 0))
                        smmv[g] = st.inc(TE.matmul(
                            sc[sl][:, 512:1024], kc_[64:128, kt * 128:(kt + 1) * 128],
                            qc_[64:128, qc * 512:(qc + 1) * 512],
                            start=True, stop=True, tile_position=(64, 0)))

                    def flush_tailA(r):
                        nonlocal erv_
                        if r == 0:
                            st.wait(V, bcv)
                            sv.wait(V)
                        V.reciprocal(invrA[0:1, r * 512:(r + 1) * 512],
                                     lbufA[0:1, r * 512:(r + 1) * 512])
                        erv_ = sv.inc(V.reciprocal(
                            invrB[0:1, r * 512:(r + 1) * 512],
                            lbufB[0:1, r * 512:(r + 1) * 512]))

                    def flush_tailB():
                        nonlocal bcv, ibcp, tailv
                        ep, erv = pend[0]
                        pend[0] = None
                        sv.wait(TE, erv_)
                        for r in range(4):
                            sv.wait(TE, ibcp)
                            TE.matmul(pib[0:64, :], ones1f[0:1, 0:64],
                                      invrA[0:1, r * 512:(r + 1) * 512],
                                      start=True, stop=True, tile_position=(0, 0))
                            bcv = st.inc(TE.matmul(
                                pib[64:128, :], ones1f[0:1, 0:64],
                                invrB[0:1, r * 512:(r + 1) * 512],
                                start=True, stop=True, tile_position=(0, 64)))
                            st.wait(V, bcv)
                            V.tensor_copy(invbA2[:, r * 512:(r + 1) * 512],
                                          pib[0:64, :])
                            ibcp = sv.inc(V.tensor_copy(
                                invbB2[:, r * 512:(r + 1) * 512], pib[64:128, :]))
                        sv.wait(V)
                        V.tensor_tensor(
                            attA[0:64, ep * T:(ep + 1) * T],
                            attA[0:64, ep * T:(ep + 1) * T],
                            invbA2[:, :], op=OP.mult)
                        tailv = sv.inc(V.tensor_tensor(
                            tmpB[ep % 2][:, :], tmpB[ep % 2][:, :],
                            invbB2[:, :], op=OP.mult))
                        sv.wait(SY, tailv)
                        dmaBv[ep % 2] = dtb[ep % 2].inc(SY.dma_start(
                            attA[64:128, ep * T:(ep + 1) * T], tmpB[ep % 2][:]), 16)

                    issue_smm(0)
                    issue_smm(1)
                    for g in range(NPAIR * NSTEP):
                        p, i = divmod(g, NSTEP)
                        qc, kt = divmod(i, NKT)
                        sl = g % 2
                        if pend[0] is not None and i in (2, 10, 18, 26):
                            flush_tailA((2, 10, 18, 26).index(i))
                        if i == 2 * NKT + 4 and pend[0] is not None:
                            flush_tailB()
                        _, _, dc4 = qkd(p)
                        pl = g % 3
                        st.wait(S, smmv[g])
                        st.wait(S, avv3[pl])
                        expv[pl] = ss.inc(S.activation(
                            ptil[pl][:], sc[sl][:], AF.Exp,
                            bias=negM[:, p:p + 1], scale=1.0))
                        ss.wait(TE, expv[pl])
                        if g + 2 < NPAIR * NSTEP:
                            issue_smm(g + 2)
                        accAd = accA if qc % 2 == 0 else accA2
                        if kt == 0:
                            sv.wait(TE, epivA[qc % 2])
                        TE.matmul(accAd[:], dc4[:, kt, 0:65], ptil[pl][:, 0:512],
                                  start=(kt == 0), stop=(kt == NKT - 1))
                        if kt == 0:
                            sv.wait(TE, epivB)
                        avv3[pl] = st.inc(TE.matmul(
                            accB[:], dc4[:, kt, 65:130], ptil[pl][:, 512:1024],
                            start=(kt == 0), stop=(kt == NKT - 1)))
                        if kt == NKT - 1:
                            st.wait(V, avv3[pl])
                            sv.wait(V)
                            if qc == 0 and p >= 2:
                                dtb[p % 2].wait(V, dmaBv[p % 2])
                            V.tensor_copy(lbufB[0:1, qc * 512:(qc + 1) * 512],
                                          accB[64:65, :])
                            epivB = sv.inc(V.tensor_copy(
                                tmpB[p % 2][:, qc * 512:(qc + 1) * 512],
                                accB[0:64, :]))
                            V.tensor_copy(lbufA[0:1, qc * 512:(qc + 1) * 512],
                                          accAd[64:65, :])
                            epivA[qc % 2] = epiv = sv.inc(V.tensor_copy(
                                attA[0:64, p * T + qc * 512:p * T + (qc + 1) * 512],
                                accAd[0:64, :]))
                            if qc == 3:
                                pend[0] = (p, epiv)
                    for r_ in range(4):
                        flush_tailA(r_)
                    flush_tailB()
                    dtb[0].wait(SY)
                    dtb[1].wait(SY)
                bar()

                _ph = ExitStack()
                xr = [_ph.enter_context(nc.sbuf_tensor(_un("xrb"), [128, H], F32))
                      for _ in range(4)]
                ln3t = _ph.enter_context(nc.sbuf_tensor(_un("ln3t"), [128, H], F32))
                x2t = [_ph.enter_context(nc.sbuf_tensor(_un("x2t"), [128, H], F32))
                       for _ in range(4)]
                ln2t = _ph.enter_context(nc.sbuf_tensor(_un("ln2t"), [128, H], F32))
                qx16 = _ph.enter_context(nc.sbuf_tensor(_un("qx16"), [128, H], F16))
                xqtm = _ph.enter_context(nc.sbuf_tensor(_un("xqtm"), [128, H], BF16))
                xqs0 = _ph.enter_context(nc.sbuf_tensor(_un("xqs0"), [128, 512], BF16))
                xqs1 = _ph.enter_context(nc.sbuf_tensor(_un("xqs1"), [128, 512], BF16))
                rowA = _ph.enter_context(nc.sbuf_tensor(_un("rowA"), [1, T], F32))
                pw0 = _ph.enter_context(nc.psum_tensor(_un("pw0"), [128, 1024], F32))
                pw1 = _ph.enter_context(nc.psum_tensor(_un("pw1"), [128, 1024], F32))
                pw2 = _ph.enter_context(nc.psum_tensor(_un("pw2"), [128, 1024], F32))
                pw3 = _ph.enter_context(nc.psum_tensor(_un("pw3"), [128, 1024], F32))
                with _ph:
                    xqs = [xqs0, xqs1]
                    pw = [pw0, pw1, pw2, pw3]
                    wo_use = [0, 0]
                    evacv = [0] * 4
                    xq_dma = [0, 0]
                    ln3use_s = 0
                    ln2use_s = 0
                    xqtmuse_s = 0
                    x2dma = [0] * 4
                    x2rds = [0] * 4
                    xruse = [0] * 4
                    for g in range(8):
                        gp = g % 2
                        pwp = [pw[gp * 2], pw[gp * 2 + 1]]
                        for kc in range(KC):
                            if kc == 0:
                                for i in range(2):
                                    ss.wait(TE, evacv[gp * 2 + i])
                            for i in range(2):
                                ti = g * 2 + i
                                for hh in range(2):
                                    _ins = TE.matmul(pwp[i][:, hh * 512:(hh + 1) * 512],
                                              A3[:, kc, ti * 128:(ti + 1) * 128],
                                              wo3[:, kc * H + hh * 512:
                                                  kc * H + (hh + 1) * 512],
                                              start=(kc == 0), stop=False)
                            st.inc(_ins)
                        for i in range(2):
                            for hh in range(2):
                                _ins = TE.matmul(pwp[i][:, hh * 512:(hh + 1) * 512],
                                          ones_row[0:1, 0:128],
                                          bo_row[0:1, hh * 512:(hh + 1) * 512],
                                          start=False, stop=True)
                        womm = st.inc(_ins)
                        xvv = [0] * 2
                        v2s = [0] * 2
                        for i in range(2):
                            ti = g * 2 + i
                            io = ti % 4
                            sv.wait(SY, xruse[io])
                            xvv[i] = dxr[io].inc(SY.dma_start(
                                xr[io][:], x_h[s, ti * 128:(ti + 1) * 128, :]), 16)
                            st.wait(V, womm)
                            v2s[i] = ln_stats(pwp[i][:], 16 + io * 4)
                        fin3 = [ln_fin(16 + ((g * 2 + i) % 4) * 4, v2s[i])
                                for i in range(2)]
                        for i in range(2):
                            ti = g * 2 + i
                            io = ti % 4
                            r3, mb3, mbv3 = fin3[i]
                            sv.wait(S, mbv3)
                            sv.wait(S, ln3use_s)
                            l3v = ss.inc(S.activation(ln3t[:], pwp[i][:],
                                                      AF.Identity,
                                                      scale=r3, bias=mb3))
                            dxr[io].wait(V, xvv[i])
                            ss.wait(V, l3v)
                            dx2[io % 2].wait(V)
                            ss.wait(V, x2rds[io])
                            addv = sv.inc(V.tensor_tensor(
                                x2t[io][:], xr[io][:], ln3t[:], op=OP.add))
                            ln3use_s = addv
                            xruse[io] = addv
                            sv.wait(SY, addv)
                            x2dma[io] = dx2[io % 2].inc(SY.dma_start(
                                x2_sp[s, ti * 128:(ti + 1) * 128, :], x2t[io][:]), 16)
                        sv.wait(V)
                        for i in range(2):
                            io = (g * 2 + i) % 4
                            v2s[i] = ln_stats(x2t[io][:], 32 + io * 4)
                        fin2 = [ln_fin(32 + ((g * 2 + i) % 4) * 4, v2s[i])
                                for i in range(2)]
                        for i in range(2):
                            ti = g * 2 + i
                            io = ti % 4
                            qo = 48 + io * 4
                            r2, mb2, mbv2 = fin2[i]
                            sv.wait(S, mbv2)
                            sv.wait(S, ln2use_s)
                            l2v = ss.inc(S.activation(ln2t[:], x2t[io][:],
                                                      AF.Identity,
                                                      scale=r2, bias=mb2))
                            x2rds[io] = l2v
                            ss.wait(V, l2v)
                            sv.inc(V.tensor_reduce(
                                stat[:, qo:qo + 1], ln2t[:], axis=AX.X,
                                op=OP.max, apply_absolute_value=True))
                            sv.wait(V)
                            sv.inc(V.tensor_scalar(stat[:, qo + 1:qo + 2],
                                                   stat[:, qo:qo + 1], 1e-5,
                                                   None, op0=OP.max))
                            sv.wait(V)
                            sv.inc(V.reciprocal(stat[:, qo + 2:qo + 3],
                                                stat[:, qo + 1:qo + 2]))
                            sv.wait(V)
                            sv.inc(V.tensor_scalar(
                                stat[:, qo + 3:qo + 4], stat[:, qo + 2:qo + 3],
                                127.0, None, op0=OP.mult))
                            sv.inc(V.tensor_scalar(c1sb[:, ti:ti + 1],
                                                   stat[:, qo + 1:qo + 2],
                                                   wsc[:, 1:2], None, op0=OP.mult))
                            sv.wait(V)
                            st.wait(V, xqtmuse_s)
                            sv.inc(V.tensor_scalar(qx16[:], ln2t[:],
                                                   stat[:, qo + 3:qo + 4], RM,
                                                   op0=OP.mult, op1=OP.add))
                            sv.wait(V)
                            xqv = sv.inc(V.tensor_scalar(
                                xqtm[:], qx16[:], -RM, None,
                                op0=OP.add))
                            ln2use_s = xqv
                            sv.wait(TE, xqv)
                            for half in range(2):
                                dqs[half].wait(TE, xq_dma[half])
                                for k4 in range(4):
                                    kc = half * 4 + k4
                                    _ins = TE.matmul(pwp[i][:, half * 512 + k4 * 128:
                                                    half * 512 + (k4 + 1) * 128],
                                              xqtm[:, kc * 128:(kc + 1) * 128],
                                              ident[:], start=True, stop=True)
                                thv = xqtmuse_s = st.inc(_ins)
                                st.wait(S, thv)
                                cpv = ss.inc(S.copy(
                                    xqs[half][:], pwp[i][:, half * 512:(half + 1) * 512]))
                                ss.wait(SY, cpv)
                                xq_dma[half] = dqs[half].inc(SY.dma_start(
                                    xqd_sp[s][half * 4:half * 4 + 4, :,
                                              ti * 128:(ti + 1) * 128].rearrange(
                                                  "c p n -> p c n"),
                                    xqs[half][:, :].rearrange("p (c n) -> p c n", n=128)), 16)
                            evacv[gp * 2 + i] = ss.v
                    sv.wait(TE)
                    dqs[0].wait(TE, xq_dma[0])
                    dqs[1].wait(TE, xq_dma[1])
                    for ti in range(NT128):
                        dst = pw0 if ti < 8 else pw1
                        _ins = TE.matmul(dst[0:1, (ti % 8) * 128:(ti % 8 + 1) * 128],
                                  c1sb[:, ti:ti + 1], identf[:], start=True, stop=True)
                    tcv2 = st.inc(_ins)
                    st.wait(V, tcv2)
                    V.tensor_copy(rowA[0:1, 0:1024], pw0[0:1, :])
                    rv2 = sv.inc(V.tensor_copy(rowA[0:1, 1024:2048], pw1[0:1, :]))
                    sv.wait(SY, rv2)
                    dcr.inc(SY.dma_start(c1_sp[s][None, :], rowA[:]), 16)
                    for s_ in (dcr, dx2[0], dx2[1], dqs[0], dqs[1]):
                        s_.wait(SY)
                bar()

        w3_es.close()
        bar()
        _ph = ExitStack()
        w1q = _ph.enter_context(nc.sbuf_tensor(_un("w1q"), [128, KC * DH], BF16))
        xqd = _ph.enter_context(nc.sbuf_tensor(_un("xqd"), [128, KC * T], BF16))
        diag = _ph.enter_context(nc.sbuf_tensor(_un("diag"), [128, 3 * HC * 128], BF16))
        c1f = _ph.enter_context(nc.sbuf_tensor(_un("c1f"), [128, T], F32))
        crow = _ph.enter_context(nc.sbuf_tensor(_un("crow"), [1, T], F32))
        hsc2 = [_ph.enter_context(nc.sbuf_tensor(_un("hsc2"), [128, T + 2], BF16))
                for _ in range(2)]
        hsb = _ph.enter_context(nc.sbuf_tensor(_un("hsb"), [128, T], BF16))
        macc = _ph.enter_context(nc.sbuf_tensor(_un("macc"), [128, T], BF16))
        mincc = _ph.enter_context(nc.sbuf_tensor(_un("mincc"), [128, T], BF16))
        s2row = _ph.enter_context(nc.sbuf_tensor(_un("s2row"), [1, T], F32))
        pwA = _ph.enter_context(nc.psum_tensor(_un("pwA"), [128, 512], F32))
        pwB = _ph.enter_context(nc.psum_tensor(_un("pwB"), [128, 512], F32))
        pcA = _ph.enter_context(nc.psum_tensor(_un("pcA"), [128, 512], F32))
        pcB = _ph.enter_context(nc.psum_tensor(_un("pcB"), [128, 512], F32))
        pam = _ph.enter_context(nc.psum_tensor(_un("pam"), [128, 256], F32))
        prow = _ph.enter_context(nc.psum_tensor(_un("prow"), [1, 512], F32))
        pbc1 = _ph.enter_context(nc.psum_tensor(_un("pbc1"), [128, 512], F32))
        with _ph:
            pwx = [pwA, pwB]
            pcx = [pcA, pcB]
            w1ld = 0
            for kc in range(KC):
                w1ld = drl.inc(G.dma_start(w1q[:, kc * DH:(kc + 1) * DH], w1qt_sp[kc]), 16)
            dgv = 0
            for c in range(HC):
                for j3 in range(3):
                    dgv = sv.inc(V.tensor_scalar(
                        diag[:, (c * 3 + j3) * 128:(c * 3 + j3 + 1) * 128],
                        ident[:], convw_s[:, c * 3 + j3:c * 3 + j3 + 1], None, op0=OP.mult))
            V.memset(hsc2[0][:, 0:1], 0.0)
            V.memset(hsc2[0][:, T + 1:T + 2], 0.0)
            V.memset(hsc2[1][:, 0:1], 0.0)
            hscz = sv.inc(V.memset(hsc2[1][:, T + 1:T + 2], 0.0))
            spillv = 0
            mm_prev = 0
            hscuse = [0, 0]
            for s in range(NSEQ):
                st.wait(G)
                sv.wait(G)
                st.wait(V)
                for kc in range(KC):
                    drl.inc(G.dma_start(xqd[:, kc * T:(kc + 1) * T], xqd_sp[s, kc]), 16)
                drl.inc(G.dma_start(crow[:], c1_sp[s][None, :]), 16)
                drl.wait(TE)
                sv.wait(S)
                c1bv = 0
                for jbc in range(NT512):
                    if jbc > 0:
                        ss.wait(TE, c1bv)
                    tcb = st.inc(TE.matmul(pbc1[:], ones1f[:],
                                           crow[0:1, jbc * 512:(jbc + 1) * 512],
                                           start=True, stop=True))
                    st.wait(S, tcb)
                    c1bv = ss.inc(S.copy(c1f[:, jbc * 512:(jbc + 1) * 512], pbc1[:]))
                V.memset(macc[:], -3e38)
                mmi = sv.inc(V.memset(mincc[:], 3e38))
                drl.wait(TE)
                pwv = [0, 0]
                silv = [0, 0]
                hval = {}

                def w1_block(c):
                    nonlocal mmi
                    hs_ = hsc2[c % 2]
                    for j in range(NT512):
                        slw = j % 2
                        sv.wait(TE, pwv[slw])
                        for kc in range(KC):
                            _ins = TE.matmul(pwx[slw][:],
                                      w1q[:, kc * DH + c * 128:kc * DH + (c + 1) * 128],
                                      xqd[:, kc * T + j * 512:kc * T + (j + 1) * 512],
                                      start=(kc == 0), stop=(kc == KC - 1))
                        wmm = st.inc(_ins)
                        st.wait(V, wmm)
                        ss.wait(V, c1bv)
                        sv.wait(V, hscz)
                        st.wait(V, hscuse[c % 2])
                        pwv[slw] = sv.inc(V.tensor_tensor(
                            hs_[:, 1 + j * 512:1 + (j + 1) * 512], pwx[slw][:],
                            c1f[:, j * 512:(j + 1) * 512], op=OP.mult))
                    hval[c] = sv.v

                def conv_block(c):
                    nonlocal mmi, mm_prev, spillv
                    hs_ = hsc2[c % 2]
                    for j in range(NT512):
                        slc = j % 2
                        sv.wait(TE, hval[c])
                        if c == 0:
                            sv.wait(TE, dgv)
                        ss.wait(TE, silv[slc])
                        for tap in range(3):
                            _ins = TE.matmul(pcx[slc][:],
                                      diag[:, (c * 3 + tap) * 128:(c * 3 + tap + 1) * 128],
                                      hs_[:, tap + j * 512:tap + j * 512 + 512],
                                      start=(tap == 0), stop=(tap == 2))
                        cmm = st.inc(_ins)
                        hscuse[c % 2] = cmm
                        st.wait(S, cmm)
                        dhs.wait(S, spillv)
                        sv.wait(S, mm_prev)
                        silv[slc] = ss.inc(S.activation(hsb[:, j * 512:(j + 1) * 512],
                                                        pcx[slc][:], AF.Silu,
                                                        bias=convb_s[:, c:c + 1], scale=1.0))
                    ss.wait(V, silv[(NT512 - 1) % 2])
                    sv.wait(V, mmi)
                    V.tensor_tensor(macc[:], macc[:], hsb[:], op=OP.max)
                    mmi = mm_prev = sv.inc(V.tensor_tensor(mincc[:], mincc[:], hsb[:], op=OP.min))
                    ss.wait(SY, silv[(NT512 - 1) % 2])
                    spillv = dhs.inc(SY.dma_start(hs_sp[s, c], hsb[:]), 16)

                w1_block(0)
                for c in range(HC):
                    if c + 1 < HC:
                        w1_block(c + 1)
                    conv_block(c)
                sv.wait(TE, mmi)
                pamv = 0
                for ti in range(NT128):
                    sv.wait(TE, pamv)
                    TE.matmul(pam[:, 0:128], macc[:, ti * 128:(ti + 1) * 128], ident[:],
                              start=True, stop=True)
                    tav = st.inc(TE.matmul(pam[:, 128:256], mincc[:, ti * 128:(ti + 1) * 128],
                                           ident[:], start=True, stop=True))
                    st.wait(V, tav)
                    V.tensor_reduce(stat[:, 40:41], pam[:, 0:128], axis=AX.X, op=OP.max)
                    pamv = sv.inc(V.tensor_reduce(stat[:, 41:42], pam[:, 128:256], axis=AX.X,
                                                  op=OP.min, negate=True))
                    sv.wait(V)
                    sv.inc(V.tensor_tensor(stat[:, 42:43], stat[:, 40:41], stat[:, 41:42],
                                           op=OP.max))
                    sv.wait(V)
                    sv.inc(V.tensor_scalar(stat[:, 43:44], stat[:, 42:43], 1e-5, None,
                                           op0=OP.max))
                    sv.wait(V)
                    sv.inc(V.reciprocal(stat[:, 44:45], stat[:, 43:44]))
                    sv.wait(V)
                    sv.inc(V.tensor_scalar(s2s[:, ti:ti + 1], stat[:, 44:45], 127.0,
                                           None, op0=OP.mult))
                sv.wait(TE)
                dsw.wait(V)
                srv = 0
                for tg in range(4):
                    if tg > 0:
                        sv.wait(TE, srv)
                    for t4 in range(4):
                        ti = tg * 4 + t4
                        _ins = TE.matmul(prow[0:1, t4 * 128:(t4 + 1) * 128], s2s[:, ti:ti + 1],
                                  identf[:], start=True, stop=True)
                    trv2 = st.inc(_ins)
                    st.wait(V, trv2)
                    srv = sv.inc(V.tensor_copy(s2row[0:1, tg * 512:(tg + 1) * 512], prow[0:1, :]))
                sv.wait(SY, srv)
                dsw.inc(SY.dma_start(s2_sp[s][None, :], s2row[:]), 16)
            dhs.wait(SY)
            dsw.wait(SY)
            drl.wait(SY)
        bar()

        _ph = ExitStack()
        w2q = _ph.enter_context(nc.sbuf_tensor(_un("w2q"), [128, HC * H], BF16))
        ring0 = _ph.enter_context(nc.sbuf_tensor(_un("ring0"), [128, HC * 512], BF16))
        ring1 = _ph.enter_context(nc.sbuf_tensor(_un("ring1"), [128, HC * 512], BF16))
        srow0 = _ph.enter_context(nc.sbuf_tensor(_un("srow0"), [1, T], F32))
        srow1 = _ph.enter_context(nc.sbuf_tensor(_un("srow1"), [1, T], F32))
        s2b0 = _ph.enter_context(nc.sbuf_tensor(_un("s2b0"), [128, 512], BF16))
        s2b1 = _ph.enter_context(nc.sbuf_tensor(_un("s2b1"), [128, 512], BF16))
        m216a = _ph.enter_context(nc.sbuf_tensor(_un("m216a"), [128, 512], F16))
        m216b = _ph.enter_context(nc.sbuf_tensor(_un("m216b"), [128, 512], F16))
        q216a = _ph.enter_context(nc.sbuf_tensor(_un("q216a"), [128, 512], F16))
        q216b = _ph.enter_context(nc.sbuf_tensor(_un("q216b"), [128, 512], F16))
        ln4t = _ph.enter_context(nc.sbuf_tensor(_un("ln4t"), [128, H], F32))
        x2r0 = _ph.enter_context(nc.sbuf_tensor(_un("x2r0"), [128, H], F32))
        x2r1 = _ph.enter_context(nc.sbuf_tensor(_un("x2r1"), [128, H], F32))
        outt0 = _ph.enter_context(nc.sbuf_tensor(_un("outt0"), [128, H], F32))
        outt1 = _ph.enter_context(nc.sbuf_tensor(_un("outt1"), [128, H], F32))
        p2a = _ph.enter_context(nc.psum_tensor(_un("p2a"), [128, 1024], F32))
        p2b = _ph.enter_context(nc.psum_tensor(_un("p2b"), [128, 1024], F32))
        pbs = _ph.enter_context(nc.psum_tensor(_un("pbs"), [128, 512], F32))
        with _ph:
            ring = [ring0, ring1]
            s2b = [s2b0, s2b1]
            x2r = [x2r0, x2r1]
            outt = [outt0, outt1]
            p2 = [p2a, p2b]
            for c in range(HC):
                drl.inc(G.dma_start(w2q[:, c * H:(c + 1) * H], w2qt_sp[c]), 16)
            ringuse = [0, 0]
            s2use = [0, 0]
            addv2 = [0, 0]
            rsubv = [0, 0]
            outdma = [0, 0]
            x2v_use = [0, 0]
            evv_ = [0, 0]
            pbsuse = 0
            srow = [srow0, srow1]
            srldv = [0, 0]
            sj = [(s, j) for s in range(NSEQ) for j in range(NT512)]
            NSJ = len(sj)
            ringldv = {}
            rndall_v = {}

            def load_ring(idx):
                s_, j_ = sj[idx]
                slr_ = idx % 2
                if j_ == 0:
                    drl.wait(G)
                    srldv[s_] = drl.inc(
                        G.dma_start(srow[s_][:], s2_sp[s_][None, :]), 16)
                st.wait(SY, ringuse[slr_])
                ringldv[idx] = dr2[slr_].inc(SY.dma_start(
                    ring[slr_][:, :].rearrange("p (c n) -> p c n", n=512),
                    hs_sp[s_][:, :, j_ * 512:(j_ + 1) * 512].rearrange(
                        "c p n -> p c n")), 16)

            def quant(idx):
                nonlocal pbsuse
                s_, j_ = sj[idx]
                slr_ = idx % 2
                drl.wait(TE, srldv[s_])
                ss.wait(TE, pbsuse)
                tb4 = st.inc(TE.matmul(pbs[:], ones1f[:],
                                       srow[s_][0:1, j_ * 512:(j_ + 1) * 512],
                                       start=True, stop=True))
                st.wait(S, tb4)
                sv.wait(S, s2use[slr_])
                s2bv = pbsuse = ss.inc(S.copy(s2b[slr_][:], pbs[:]))
                ss.wait(V, s2bv)
                dr2[slr_].wait(V, ringldv[idx])
                q216 = [q216a, q216b]
                m216 = [m216a, m216b]
                mvs = {}

                def qmult(c_):
                    slt_ = c_ % 2
                    ss.wait(V, addv2[slt_])
                    mvs[c_] = sv.inc(V.tensor_tensor(
                        m216[slt_][:], ring[slr_][:, c_ * 512:(c_ + 1) * 512],
                        s2b[slr_][:], op=OP.mult))

                qmult(0)
                for c in range(HC):
                    slt = c % 2
                    if c + 1 < HC:
                        qmult(c + 1)
                    sv.wait(S, mvs[c])
                    sv.wait(S, rsubv[slt])
                    addv2[slt] = ss.inc(S.activation(
                        q216[slt][:], m216[slt][:], AF.Identity,
                        bias=posRM[:, 0:1], scale=1.0))
                    ss.wait(V, addv2[slt])
                    rsubv[slt] = sv.inc(V.tensor_scalar(
                        ring[slr_][:, c * 512:(c + 1) * 512], q216[slt][:],
                        -RM, None, op0=OP.add))
                s2use[slr_] = sv.v
                rndall_v[idx] = sv.v

            load_ring(0)
            quant(0)
            for idx in range(NSJ):
                s, j = sj[idx]
                slr = idx % 2
                if idx + 1 < NSJ:
                    load_ring(idx + 1)
                    quant(idx + 1)
                rndall = rndall_v[idx]
                for i2 in range(2):
                    for half in range(2):
                            ti = j * 4 + i2 * 2 + half
                            pp = p2[half]
                            sv.wait(TE, rndall)
                            sv.wait(TE, evv_[half])
                            for c in range(HC):
                                base = c * 512 + (i2 * 2 + half) * 128
                                for hh in range(2):
                                    _ins = TE.matmul(pp[:, hh * 512:(hh + 1) * 512],
                                              ring[slr][:, base:base + 128],
                                              w2q[:, c * H + hh * 512:c * H + (hh + 1) * 512],
                                              start=(c == 0), stop=(c == HC - 1))
                            mmv_ = st.inc(_ins)
                            ringuse[slr] = mmv_
                            sv.wait(SY, x2v_use[half])
                            xv = dxa[half].inc(SY.dma_start(
                                x2r[half][:], x2_sp[s, ti * 128:(ti + 1) * 128, :]), 16)
                            st.wait(V, mmv_)
                            r4, mb4, mbv4 = ln_chain(pp[:], 48 + half * 4)
                            sv.wait(S, mbv4)
                            l4v = ss.inc(S.activation(ln4t[:], pp[:], AF.Identity,
                                                      scale=r4, bias=mb4))
                            dxa[half].wait(V, xv)
                            ss.wait(V, l4v)
                            dou[half].wait(V, outdma[half])
                            ov = sv.inc(V.tensor_tensor(outt[half][:], x2r[half][:], ln4t[:],
                                                        op=OP.add))
                            evv_[half] = ov
                            x2v_use[half] = ov
                            sv.wait(SY, ov)
                            outdma[half] = dou[half].inc(SY.dma_start(
                                out_h[s, ti * 128:(ti + 1) * 128, :], outt[half][:]), 16)
            dou[0].wait(SY)
            dou[1].wait(SY)
        bar()

    return nc, dbg_outs


def host_inputs(inputs):
    x = np.ascontiguousarray(inputs["L0_attn_hidden_states"])
    common = dict(
        wq=np.ascontiguousarray(inputs["L0_attn_wq"]),
        wk=np.ascontiguousarray(inputs["L0_attn_wk"]),
        wv=np.ascontiguousarray(inputs["L0_attn_wv"]),
        wo=np.ascontiguousarray(inputs["L0_attn_wo"]),
        bq=np.ascontiguousarray(np.asarray(inputs["L0_attn_bq"]).reshape(OC, 128).T),
        bv=np.ascontiguousarray(np.asarray(inputs["L0_attn_bv"]).reshape(OC, 128).T),
        bo=np.ascontiguousarray(inputs["L0_attn_bo"]),
        cosT=np.ascontiguousarray(np.asarray(inputs["L0_attn_cos"]).T),
        sinT=np.ascontiguousarray(np.asarray(inputs["L0_attn_sin"]).T),
        w1t=np.ascontiguousarray(np.asarray(inputs["L0_ffn_w1"]).T),
        cw=np.ascontiguousarray(np.asarray(inputs["L0_ffn_conv_w"]).reshape(HC, 128, 3)
                                .transpose(1, 0, 2).reshape(128, HC * 3)),
        cb=np.ascontiguousarray(np.asarray(inputs["L0_ffn_conv_b"]).reshape(HC, 128).T),
        w2t=np.ascontiguousarray(np.asarray(inputs["L0_ffn_w2"]).T),
    )
    maps = []
    for c in range(8):
        m = dict(common)
        m["x"] = np.ascontiguousarray(x[c * NSEQ:(c + 1) * NSEQ])
        maps.append(m)
    return maps


def kernel(**inputs):
    maps = host_inputs(inputs)
    nc, _ = build_nc()
    res = run_bass_kernel_spmd(nc, maps, core_ids=list(range(8)))
    out = np.concatenate([np.asarray(r["out"]) for r in res.results], axis=0)
    return out.astype(np.float32)


# version 21 (final)
# speedup vs baseline: 1.1155x; 1.0195x over previous
import numpy as np
import concourse.bass as bass
import concourse.mybir as mybir
from concourse import library_config as _lc
_LIB = {l.name: l for l in _lc.all_libraries}
from concourse.bass_utils import run_bass_kernel_spmd
from contextlib import ExitStack

F32 = mybir.dt.float32
F16 = mybir.dt.float16
BF16 = mybir.dt.bfloat16
I32 = mybir.dt.int32
AF = mybir.ActivationFunctionType
AX = mybir.AxisListType
OP = mybir.AluOpType

H, NH, HD, T, NSEQ = 1024, 16, 64, 2048, 2
DH = 4096
NT128, NT512 = T // 128, T // 512
OC, HC, KC = H // 128, DH // 128, H // 128
NPAIR, NKT = NH // 2, T // 128
EPS = 1e-5
RM = 1536.0


class Sem:
    def __init__(self, es, nc, name):
        self.h = es.enter_context(nc.semaphore(name))
        self.v = 0

    def inc(self, ins, n=1):
        ins.then_inc(self.h, n)
        self.v += n
        return self.v

    def wait(self, eng, v=None):
        eng.wait_ge(self.h, self.v if v is None else v)


def build_nc(dbg=()):
    nc = bass.Bass("TRN2", target_bir_lowering=False)
    _nctr = [0]

    def _un(n):
        _nctr[0] += 1
        return f"{n}_{_nctr[0]}"
    x_h = nc.dram_tensor("x", [NSEQ, T, H], F32, kind="ExternalInput").ap()
    wq_h = nc.dram_tensor("wq", [H, H], F32, kind="ExternalInput").ap()
    wk_h = nc.dram_tensor("wk", [H, H], F32, kind="ExternalInput").ap()
    wv_h = nc.dram_tensor("wv", [H, H], F32, kind="ExternalInput").ap()
    wo_h = nc.dram_tensor("wo", [H, H], F32, kind="ExternalInput").ap()
    bq_h = nc.dram_tensor("bq", [128, OC], F32, kind="ExternalInput").ap()
    bv_h = nc.dram_tensor("bv", [128, OC], F32, kind="ExternalInput").ap()
    bo_h = nc.dram_tensor("bo", [H], F32, kind="ExternalInput").ap()
    cos_h = nc.dram_tensor("cosT", [HD, 1024], F32, kind="ExternalInput").ap()
    sin_h = nc.dram_tensor("sinT", [HD, 1024], F32, kind="ExternalInput").ap()
    w1t_h = nc.dram_tensor("w1t", [H, DH], F32, kind="ExternalInput").ap()
    cw_h = nc.dram_tensor("cw", [128, 3 * HC], F32, kind="ExternalInput").ap()
    cb_h = nc.dram_tensor("cb", [128, HC], F32, kind="ExternalInput").ap()
    w2t_h = nc.dram_tensor("w2t", [DH, H], F32, kind="ExternalInput").ap()
    out_h = nc.dram_tensor("out", [NSEQ, T, H], F32, kind="ExternalOutput").ap()

    wo_sp = nc.dram_tensor("wo_sp", [KC, 128, H], BF16, kind="Internal").ap()
    w1qt_sp = nc.dram_tensor("w1qt_sp", [KC, 128, DH], BF16, kind="Internal").ap()
    w2qt_sp = nc.dram_tensor("w2qt_sp", [HC, 128, H], BF16, kind="Internal").ap()
    wqkv_sp = nc.dram_tensor("wqkv_sp", [OC, 128, 3 * KC * 128], BF16,
                             kind="Internal").ap()
    xqd_sp = nc.dram_tensor("xqd_sp", [NSEQ, KC, 128, T], BF16, kind="Internal").ap()
    x2_sp = nc.dram_tensor("x2_sp", [NSEQ, T, H], F32, kind="Internal").ap()
    hs_sp = nc.dram_tensor("hs_sp", [NSEQ, HC, 128, T], BF16, kind="Internal").ap()
    c1_sp = nc.dram_tensor("c1_sp", [NSEQ, T], F32, kind="Internal").ap()
    s2_sp = nc.dram_tensor("s2_sp", [NSEQ, T], F32, kind="Internal").ap()

    V, S, TE, G, SY = nc.vector, nc.scalar, nc.tensor, nc.gpsimd, nc.sync

    es = ExitStack()
    sbuf = lambda n, shape, d=F32: es.enter_context(nc.sbuf_tensor(n, shape, d))

    dbg_outs = {}
    for name, shape, d in dbg:
        dbg_outs[name] = nc.dram_tensor(name, shape, d, kind="ExternalOutput").ap()

    with es:
        ident = sbuf("ident", [128, 128], BF16)
        identf = sbuf("identf", [128, 128], F32)
        ones1f = sbuf("ones1f", [1, 128], F32)
        cosfm = sbuf("cosfm", [128, 1024], BF16)
        sinfm = sbuf("sinfm", [128, 1024], BF16)
        bq_pre = sbuf("bq_pre", [128, OC], F32)
        bv_pre = sbuf("bv_pre", [128, OC], F32)
        bo_row = sbuf("bo_row", [1, H], BF16)
        ones_row = sbuf("ones_row", [1, 128], BF16)
        ones2 = sbuf("ones2", [128, 2], BF16)
        convw_s = sbuf("convw_s", [128, 3 * HC], F32)
        convb_s = sbuf("convb_s", [128, HC], F32)
        negM = sbuf("negM", [128, NPAIR], F32)
        negRM = sbuf("negRM", [128, 1], F32)
        posRM = sbuf("posRM", [128, 1], F32)
        wsc = sbuf("wsc", [128, 4], F32)
        stat = sbuf("stat", [128, 64], F32)
        bnbuf = sbuf("bnbuf", [128, 12], F32)
        nrm = sbuf("nrm", [128, 24], F32)
        s2s = sbuf("s2s", [128, NT128], F32)
        c1sb = sbuf("c1sb", [128, NT128], F32)
        w3_es = ExitStack()
        wo3 = w3_es.enter_context(nc.sbuf_tensor(_un("wo3"), [128, KC * H], BF16))
        qres = w3_es.enter_context(nc.sbuf_tensor(_un("qres"), [128, OC * T], BF16))
        kres = w3_es.enter_context(nc.sbuf_tensor(_un("kres"), [128, OC * T], BF16))
        vres = w3_es.enter_context(
            nc.sbuf_tensor(_un("vres"), [128, NPAIR * NKT * 130], BF16))

        sems = {}
        for name in ["dc", "drl", "tv", "ts", "tt", "tg",
                     "dxr0", "dxr1", "dxr2", "dxr3", "dst0", "dst1", "dss0", "dss1",
                     "dsq0", "dsq1", "dsk0", "dsk1", "dsd0", "dsd1",
                     "datt0", "datt1", "dtb0", "dtb1", "dwo0", "dwo1",
                     "dx20", "dx21", "dqs0", "dqs1", "dcr", "dhs",
                     "dr20", "dr21", "dxa0", "dxa1", "dou0", "dou1", "dsw"]:
            sems[name] = Sem(es, nc, name)
        dc, drl = sems["dc"], sems["drl"]
        sv, ss, st, sg = (sems[k] for k in ["tv", "ts", "tt", "tg"])
        dxr = [sems["dxr0"], sems["dxr1"], sems["dxr2"], sems["dxr3"]]
        dstg = [sems["dst0"], sems["dst1"]]
        dss = [sems["dss0"], sems["dss1"]]
        dsq = [sems["dsq0"], sems["dsq1"]]
        dsk = [sems["dsk0"], sems["dsk1"]]
        dsd = [sems["dsd0"], sems["dsd1"]]
        datt = [sems["datt0"], sems["datt1"]]
        dtb = [sems["dtb0"], sems["dtb1"]]
        dwo = [sems["dwo0"], sems["dwo1"]]
        dx2 = [sems["dx20"], sems["dx21"]]
        dqs = [sems["dqs0"], sems["dqs1"]]
        dcr, dhs = sems["dcr"], sems["dhs"]
        dr2 = [sems["dr20"], sems["dr21"]]
        dxa = [sems["dxa0"], sems["dxa1"]]
        dou = [sems["dou0"], sems["dou1"]]
        dsw = sems["dsw"]
        for s_ in sems.values():
            SY.sem_clear(s_.h)
        nc.all_engine_barrier()

        def bar():
            nc.all_engine_barrier()

        def ln_stats(src_ap, so):
            V.bn_stats(bnbuf[:, 0:6], src_ap[:, 0:512])
            sv.inc(V.bn_stats(bnbuf[:, 6:12], src_ap[:, 512:1024]))
            sv.wait(V)
            sv.inc(V.bn_aggr(stat[:, so:so + 2], bnbuf[:, 0:12]))
            sv.wait(V)
            sv.inc(V.tensor_scalar(stat[:, so + 2:so + 3], stat[:, so + 1:so + 2],
                                   EPS, None, op0=OP.add))
            sv.wait(V)
            return sv.inc(V.reciprocal(stat[:, so + 3:so + 4],
                                       stat[:, so + 2:so + 3]))

        def ln_fin(so, v2):
            sv.wait(S, v2)
            s1 = ss.inc(S.activation(stat[:, so + 2:so + 3], stat[:, so + 3:so + 4],
                                     AF.Sqrt))
            ss.wait(V, s1)
            sv.wait(V)
            mbv = sv.inc(V.tensor_scalar(stat[:, so + 3:so + 4], stat[:, so:so + 1],
                                         stat[:, so + 2:so + 3], -1.0,
                                         op0=OP.mult, op1=OP.mult))
            return stat[:, so + 2:so + 3], stat[:, so + 3:so + 4], mbv

        def ln_chain(src_ap, so):
            return ln_fin(so, ln_stats(src_ap, so))

        _ph = ExitStack()
        it_f = _ph.enter_context(nc.sbuf_tensor(_un("it_f"), [128, 128], I32))
        it_p = _ph.enter_context(nc.sbuf_tensor(_un("it_p"), [128, 1], F32))
        trigA = _ph.enter_context(nc.sbuf_tensor(_un("trigA"), [64, 1024], F32))
        trigB = _ph.enter_context(nc.sbuf_tensor(_un("trigB"), [64, 1024], F32))
        borowf = _ph.enter_context(nc.sbuf_tensor(_un("borowf"), [1, H], F32))
        wstg0 = _ph.enter_context(nc.sbuf_tensor(_un("wstg0"), [128, DH], F32))
        wstg1 = _ph.enter_context(nc.sbuf_tensor(_un("wstg1"), [128, DH], F32))
        qh16 = _ph.enter_context(nc.sbuf_tensor(_un("qh16"), [128, DH], F16))
        wqa = _ph.enter_context(nc.sbuf_tensor(_un("wqa"), [128, DH], BF16))
        wqb = _ph.enter_context(nc.sbuf_tensor(_un("wqb"), [128, DH], BF16))
        pst = _ph.enter_context(nc.psum_tensor(_un("pst"), [128, 128], F32))
        with _ph:
            stg = [wstg0, wstg1]
            wqx = [wqa, wqb]
            G.iota(it_f[:], pattern=[[1, 128]], base=0, channel_multiplier=0)
            sg.inc(G.iota(it_p[:], pattern=[[0, 1]], base=0, channel_multiplier=1,
                   allow_small_or_imprecise_dtypes=True))
            sg.wait(V)
            V.tensor_scalar(ident[:], it_f[:], it_p[:], None, op0=OP.is_equal)
            V.tensor_scalar(identf[:], it_f[:], it_p[:], None, op0=OP.is_equal)
            V.memset(ones_row[:], 1.0)
            V.memset(ones1f[:], 1.0)
            V.memset(negRM[:], -RM)
            V.memset(posRM[:], RM)
            sv.inc(V.memset(ones2[:], 0.0))
            sv.wait(V)
            V.memset(ones2[0:64, 0:1], 1.0)
            sv.inc(V.memset(ones2[64:128, 1:2], 1.0))

            dc.inc(SY.dma_start(stat[:, 0:OC], bq_h), 16)
            dc.inc(SY.dma_start(bv_pre[:], bv_h), 16)
            dc.inc(SY.dma_start(borowf[:], bo_h[None, :]), 16)
            dc.inc(SY.dma_start(convw_s[:], cw_h), 16)
            dc.inc(SY.dma_start(convb_s[:], cb_h), 16)
            dc.inc(SY.dma_start(trigA[:], cos_h), 16)
            dc.inc(SY.dma_start(trigB[:], sin_h), 16)
            dc.wait(V)
            sv.inc(V.tensor_scalar(bq_pre[:], stat[:, 0:OC], 0.125, None, op0=OP.mult))
            sv.inc(V.tensor_copy(bo_row[:], borowf[:]))
            V.tensor_copy(cosfm[0:64, :], trigA[:])
            V.tensor_copy(cosfm[64:128, :], trigA[:])
            V.tensor_copy(sinfm[0:64, :], trigB[:])
            sv.inc(V.tensor_copy(sinfm[64:128, :], trigB[:]))

            wsrc = [wq_h, wk_h, wv_h]
            castv = []
            wspill = [0, 0]
            for i, (wh, b) in enumerate([(w, k) for w in range(3) for k in range(2)]):
                eng = SY if i % 2 == 0 else G
                if i >= 2:
                    sv.wait(eng, castv[i - 2])
                lv = dstg[i % 2].inc(eng.dma_start(
                    stg[i % 2][:, 0:4 * H].rearrange("p (c n) -> p c n", n=H),
                    wsrc[wh][b * 512:(b + 1) * 512, :].rearrange(
                        "(c p) n -> p c n", p=128)), 16)
                dstg[i % 2].wait(V, lv)
                if i >= 2:
                    dsq[i % 2].wait(V, wspill[i % 2])
                castv.append(sv.inc(V.tensor_copy(
                    wqx[i % 2][:, 0:4 * H], stg[i % 2][:, 0:4 * H])))
                sv.wait(SY, castv[-1])
                for k4 in range(4):
                    tix = wh * KC + b * 4 + k4
                    wspill[i % 2] = dsq[i % 2].inc(SY.dma_start(
                        wqkv_sp[:, :, tix * 128:(tix + 1) * 128].rearrange(
                            "o p n -> p o n"),
                        wqx[i % 2][:, k4 * H:(k4 + 1) * H].rearrange(
                            "p (o n) -> p o n", n=128)), 16)
            for b in range(2):
                sl = b % 2
                eng = SY if sl == 0 else G
                sv.wait(eng, castv[-2])
                lv = dstg[sl].inc(eng.dma_start(
                    stg[sl][:, 0:4 * H].rearrange("p (c n) -> p c n", n=H),
                    wo_h[b * 512:(b + 1) * 512, :].rearrange(
                        "(c p) n -> p c n", p=128)), 16)
                dstg[sl].wait(V, lv)
                castv.append(sv.inc(V.tensor_copy(
                    wo3[:, b * 4 * H:(b + 1) * 4 * H], stg[sl][:, 0:4 * H])))

            def quant_weight(src_h, nchunk, width, sw_col, dst_sp, mwcol=None):
                nb = DH // width
                nbatch = nchunk // nb
                bw = nb * width
                redv = []
                sv.wait(SY)
                sv.wait(G)
                for c in range(nbatch):
                    if c >= 2:
                        sv.wait(SY, redv[c - 2])
                        sv.wait(G, redv[c - 2])
                    eng = SY if c % 2 == 0 else G
                    lv = dstg[c % 2].inc(eng.dma_start(
                        stg[c % 2][:, 0:bw].rearrange("p (c n) -> p c n", n=width),
                        src_h[c * nb * 128:(c + 1) * nb * 128, :].rearrange(
                            "(c p) n -> p c n", p=128)), 16)
                    dstg[c % 2].wait(V, lv)
                    redv.append(sv.inc(V.tensor_reduce(
                        nrm[:, c % 16:c % 16 + 1], stg[c % 2][:, 0:bw],
                        axis=AX.X, op=OP.add, apply_absolute_value=True)))
                sv.wait(V)
                sv.inc(V.tensor_reduce(nrm[:, 20:21], nrm[:, 0:nbatch],
                                       axis=AX.X, op=OP.add))
                sv.wait(TE)
                tq1 = st.inc(TE.matmul(pst[0:1, 0:128], nrm[:, 20:21], identf[:],
                                       start=True, stop=True))
                st.wait(V, tq1)
                sv.inc(V.tensor_reduce(nrm[0:1, 21:22], pst[0:1, 0:128],
                                       axis=AX.X, op=OP.add))
                sv.wait(TE)
                tq2 = st.inc(TE.matmul(pst[:, 0:1], ones1f[:], nrm[0:1, 21:22],
                                       start=True, stop=True))
                st.wait(V, tq2)
                sv.inc(V.tensor_copy(nrm[:, 21:22], pst[:, 0:1]))
                sv.wait(V)
                sv.inc(V.reciprocal(nrm[:, 22:23], nrm[:, 21:22]))
                sv.wait(V)
                nel = float(nchunk * 128 * width)
                sv.inc(V.tensor_scalar(wsc[:, sw_col:sw_col + 1], nrm[:, 22:23], nel,
                                       None, op0=OP.mult))
                if mwcol is not None:
                    sv.inc(V.tensor_scalar(wsc[:, mwcol:mwcol + 1], nrm[:, 21:22],
                                           1.0 / (nel * 127.0), None, op0=OP.mult))
                sv.wait(V)
                qv = [0, 0]
                qst = [0, 0]
                roundv = [0, 0]
                sv.wait(SY)
                sv.wait(G)
                dss[0].wait(V)
                dss[1].wait(V)
                dsq[0].wait(V)
                dsq[1].wait(V)
                lvs = {}

                def qload(c_):
                    sl_ = c_ % 2
                    if c_ >= 2:
                        sv.wait(SY, qv[sl_])
                        sv.wait(G, qv[sl_])
                        ss.wait(SY, roundv[sl_])
                        ss.wait(G, roundv[sl_])
                    eng_ = SY if c_ % 2 == 0 else G
                    lvs[c_] = dstg[sl_].inc(eng_.dma_start(
                        stg[sl_][:, 0:bw].rearrange("p (c n) -> p c n", n=width),
                        src_h[c_ * nb * 128:(c_ + 1) * nb * 128, :].rearrange(
                            "(c p) n -> p c n", p=128)), 16)

                qload(0)
                if nbatch > 1:
                    qload(1)
                for c in range(nbatch):
                    sl = c % 2
                    if c >= 2:
                        dss[sl].wait(V, qst[sl])
                    dstg[sl].wait(S, lvs[c])
                    sv.wait(S)
                    roundv[sl] = ss.inc(S.activation(
                        qh16[:, 0:bw], stg[sl][:, 0:bw], AF.Identity,
                        scale=wsc[:, sw_col:sw_col + 1], bias=posRM[:, 0:1]))
                    if c + 2 < nbatch:
                        qload(c + 2)
                    ss.wait(V)
                    sv.inc(V.tensor_scalar(wqx[sl][:, 0:bw], qh16[:, 0:bw],
                                           -RM, -1.0, op0=OP.add, op1=OP.max))
                    sv.wait(V)
                    qv[sl] = sv.inc(V.tensor_scalar(wqx[sl][:, 0:bw], wqx[sl][:, 0:bw],
                                                    1.0, None, op0=OP.min))
                    sv.wait(SY, qv[sl])
                    qst[sl] = dss[sl].inc(SY.dma_start(
                        dst_sp[c * nb:(c + 1) * nb].rearrange("c p n -> p c n"),
                        wqx[sl][:, 0:bw].rearrange("p (c n) -> p c n", n=width)), 16)

            quant_weight(w1t_h, KC, DH, 0, w1qt_sp, mwcol=1)
            quant_weight(w2t_h, HC, H, 2, w2qt_sp)
            for s_ in (dstg[0], dstg[1], dss[0], dss[1], dsq[0], dsq[1], dc):
                s_.wait(SY)
        bar()

        _ph = ExitStack()
        attA = _ph.enter_context(nc.sbuf_tensor(_un("attA"), [128, OC * T], BF16))
        with _ph:
            A3 = attA[:, :].rearrange("p (c n) -> p c n", n=T)

            for s in range(NSEQ):
                _ph = ExitStack()
                xr = [_ph.enter_context(nc.sbuf_tensor(_un("xra"), [128, H], F32))
                      for _ in range(3)]
                ln1b0 = _ph.enter_context(nc.sbuf_tensor(_un("ln1b0"), [128, H], BF16))
                ln1b1 = _ph.enter_context(nc.sbuf_tensor(_un("ln1b1"), [128, H], BF16))
                ptrA = _ph.enter_context(nc.psum_tensor(_un("ptrA"), [128, 512], F32))
                ptrB = _ph.enter_context(nc.psum_tensor(_un("ptrB"), [128, 512], F32))
                with _ph:
                    ln1b = [ln1b0, ln1b1]
                    ptr = [ptrA, ptrB]
                    xload = [dxr[i].inc(SY.dma_start(
                        xr[i][:], x_h[s, i * 128:(i + 1) * 128, :]), 16) for i in range(3)]
                    tcv = [0, 0]
                    lnuse = [0, 0]
                    for i in range(NT128):
                        dxr[i % 3].wait(V, xload[i])
                        r_ap, mb_ap, mbv_ = ln_chain(xr[i % 3][:], (i % 4) * 4)
                        sv.wait(S, mbv_)
                        if i >= 2:
                            st.wait(S, lnuse[i % 2])
                        sev = ss.inc(S.activation(ln1b[i % 2][:], xr[i % 3][:], AF.Identity,
                                                  scale=r_ap, bias=mb_ap))
                        if i + 3 < NT128:
                            ss.wait(SY, sev)
                            xload.append(dxr[(i + 3) % 3].inc(SY.dma_start(
                                xr[(i + 3) % 3][:], x_h[s, (i + 3) * 128:(i + 4) * 128, :]), 16))
                        ss.wait(TE, sev)
                        for half in range(2):
                            sv.wait(TE, tcv[half])
                            for k4 in range(4):
                                kc = half * 4 + k4
                                _ins = TE.matmul(ptr[half][:, k4 * 128:(k4 + 1) * 128],
                                          ln1b[i % 2][:, kc * 128:(kc + 1) * 128],
                                          ident[:], start=True, stop=True)
                            tev = st.inc(_ins)
                            lnuse[i % 2] = tev
                            st.wait(V, tev)
                            tcv[half] = sv.inc(V.tensor_copy(
                                A3[:, half * 4:half * 4 + 4, i * 128:(i + 1) * 128],
                                ptr[half][:, :].rearrange("p (c n) -> p c n", n=128)))
                bar()

                _ph = ExitStack()
                wsb0 = _ph.enter_context(
                    nc.sbuf_tensor(_un("wsb0"), [128, 3 * KC * 128], BF16))
                wsb1 = _ph.enter_context(
                    nc.sbuf_tensor(_un("wsb1"), [128, 3 * KC * 128], BF16))
                vtmp = _ph.enter_context(nc.sbuf_tensor(_un("vtmp"), [128, 512], BF16))
                rot2 = [_ph.enter_context(nc.sbuf_tensor(_un("rot2"), [128, 512], BF16))
                        for _ in range(2)]
                rt1 = _ph.enter_context(nc.sbuf_tensor(_un("rt1"), [128, 512], BF16))
                rt2 = _ph.enter_context(nc.sbuf_tensor(_un("rt2"), [128, 512], BF16))
                sqb = _ph.enter_context(nc.sbuf_tensor(_un("sqb"), [128, 512], BF16))
                prodb = _ph.enter_context(nc.sbuf_tensor(_un("prodb"), [128, 8], BF16))
                pq = _ph.enter_context(nc.psum_tensor(_un("pq"), [128, 512], F32))
                pk = _ph.enter_context(nc.psum_tensor(_un("pk"), [128, 512], F32))
                pv = _ph.enter_context(nc.psum_tensor(_un("pv"), [128, 512], F32))
                ptrV = _ph.enter_context(nc.psum_tensor(_un("ptrV"), [128, 512], F32))
                pnb = _ph.enter_context(nc.psum_tensor(_un("pnb"), [2, 512], F32))
                pnb2 = _ph.enter_context(nc.psum_tensor(_un("pnb2"), [1, 8], F32))
                pbc = _ph.enter_context(nc.psum_tensor(_un("pbc"), [128, 8], F32))
                with _ph:
                    wsb = [wsb0, wsb1]
                    if s == 0:
                        VR = vres[:, :].rearrange("p (t c) -> p t c", c=130)
                        V.memset(VR[:, :, 64:65], 1.0)
                        onesv = sv.inc(V.memset(VR[:, :, 129:130], 1.0))
                    else:
                        onesv = 0
                    evq = evk = evv = 0
                    vtuse = 0
                    trv = 0
                    nbred = 0
                    nbmm = 0
                    pend_nb = [None]

                    def flush_nb():
                        nonlocal nbred, nbmm
                        if pend_nb[0] is None:
                            return
                        oc_, sqk = pend_nb[0]
                        pend_nb[0] = None
                        sv.wait(TE, sqk)
                        sv.wait(TE, nbred)
                        TE.matmul(pnb[:, 0:256], ones2[:], sqb[:, 0:256],
                                  start=True, stop=True)
                        nbmm = st.inc(TE.matmul(pnb[:, 256:512], ones2[:],
                                                sqb[:, 256:512],
                                                start=True, stop=True))
                        st.wait(V, nbmm)
                        V.tensor_reduce(nrm[0:2, oc_:oc_ + 1], pnb[:, 0:256],
                                        axis=AX.X, op=OP.max)
                        nbred = sv.inc(V.tensor_reduce(
                            nrm[0:2, 8 + oc_:9 + oc_], pnb[:, 256:512],
                            axis=AX.X, op=OP.max))
                    wldv = [0, 0]
                    wsuse = [0, 0]
                    wldv[0] = dstg[0].inc(SY.dma_start(wsb[0][:], wqkv_sp[0]), 16)
                    for oc in range(OC):
                        par = oc % 2
                        qview = qres[:, oc * T:(oc + 1) * T]
                        kview = kres[:, oc * T:(oc + 1) * T]
                        VD4 = vres[:, oc * NKT * 130:(oc + 1) * NKT * 130].rearrange(
                            "p (t g c) -> p t g c", g=2, c=65)
                        if oc + 1 < OC:
                            sl2 = (oc + 1) % 2
                            eng = SY if sl2 == 0 else G
                            st.wait(eng, wsuse[sl2])
                            wldv[sl2] = dstg[sl2].inc(
                                eng.dma_start(wsb[sl2][:], wqkv_sp[oc + 1]), 16)
                        dstg[par].wait(TE, wldv[par])
                        for j in range(NT512):
                            ss.wait(TE, evq)
                            for kc in range(KC):
                                _ins = TE.matmul(pq[:],
                                          wsb[par][:, (0 * KC + kc) * 128:
                                                   (0 * KC + kc) * 128 + 128],
                                          A3[:, kc, j * 512:(j + 1) * 512],
                                          start=(kc == 0), stop=(kc == KC - 1))
                            mmq = st.inc(_ins)
                            ss.wait(TE, evk)
                            for kc in range(KC):
                                _ins = TE.matmul(pk[:],
                                          wsb[par][:, (1 * KC + kc) * 128:
                                                   (1 * KC + kc) * 128 + 128],
                                          A3[:, kc, j * 512:(j + 1) * 512],
                                          start=(kc == 0), stop=(kc == KC - 1))
                            mmk = st.inc(_ins)
                            ss.wait(TE, evv)
                            st.wait(TE, vtuse)
                            for kc in range(KC):
                                _ins = TE.matmul(pv[:],
                                          wsb[par][:, (2 * KC + kc) * 128:
                                                   (2 * KC + kc) * 128 + 128],
                                          A3[:, kc, j * 512:(j + 1) * 512],
                                          start=(kc == 0), stop=(kc == KC - 1))
                            mmv = st.inc(_ins)
                            wsuse[par] = mmv
                            st.wait(S, mmq)
                            evq = ss.inc(S.activation(qview[:, j * 512:(j + 1) * 512], pq[:],
                                                      AF.Identity, scale=0.125,
                                                      bias=bq_pre[:, oc:oc + 1]))
                            st.wait(S, mmk)
                            evk = ss.inc(S.activation(kview[:, j * 512:(j + 1) * 512], pk[:],
                                                      AF.Identity, scale=1.0, bias=0.0))
                            st.wait(S, mmv)
                            evv = ss.inc(S.activation(vtmp[:], pv[:], AF.Identity,
                                                      scale=1.0, bias=bv_pre[:, oc:oc + 1]))
                            ss.wait(TE, evv)
                            ss.wait(TE, trv)
                            if s == 0:
                                sv.wait(TE, onesv)
                            for i4 in range(4):
                                _ins = TE.matmul(ptrV[:, i4 * 128:(i4 + 1) * 128],
                                          vtmp[:, i4 * 128:(i4 + 1) * 128], ident[:],
                                          start=True, stop=True)
                            vtuse = st.inc(_ins)
                            st.wait(S, vtuse)
                            trv = ss.inc(S.copy(
                                VD4[:, j * 4:j * 4 + 4, :, 0:64],
                                ptrV[:, :].rearrange("p (t g c) -> p t g c", g=2, c=64)))
                        flush_nb()
                        ss.wait(V)
                        rolast = 0
                        rits = [(qview, 0, 2), (qview, 0, 3),
                                (kview, 2, 2), (kview, 2, 3)]
                        for it in range(4):
                            buf, dstoff, jj = rits[it]
                            src = buf[:, jj * 512:(jj + 1) * 512]
                            ccols = cosfm[:, (jj - 2) * 512:(jj - 1) * 512]
                            scols = sinfm[:, (jj - 2) * 512:(jj - 1) * 512]
                            sv.wait(V)
                            for hh in range(2):
                                b0 = hh * 64
                                V.tensor_scalar(rot2[0][b0:b0 + 32, :],
                                                src[b0 + 32:b0 + 64, :],
                                                -1.0, None, op0=OP.mult)
                                V.tensor_scalar(rot2[0][b0 + 32:b0 + 64, :],
                                                src[b0:b0 + 32, :],
                                                1.0, None, op0=OP.mult)
                            sv.inc(V.tensor_tensor(rt1[:], src, ccols, op=OP.mult))
                            sv.wait(V)
                            sv.inc(V.tensor_tensor(rt2[:], rot2[0][:], scols,
                                                   op=OP.mult))
                            sv.wait(V)
                            rolast = sv.inc(V.tensor_tensor(
                                src, rt1[:], rt2[:], op=OP.add))
                        st.wait(V, nbmm)
                        sv.wait(V)
                        V.tensor_tensor(sqb[:, 0:128], qview[:, 0:1024:8],
                                        qview[:, 0:1024:8], op=OP.mult)
                        V.tensor_tensor(sqb[:, 128:192], qview[:, 1024:1536:8],
                                        qview[:, 1024:1536:8], op=OP.mult)
                        V.tensor_tensor(sqb[:, 192:256], qview[:, 1536:2048:8],
                                        qview[:, 1536:2048:8], op=OP.mult)
                        V.tensor_tensor(sqb[:, 256:384], kview[:, 0:1024:8],
                                        kview[:, 0:1024:8], op=OP.mult)
                        V.tensor_tensor(sqb[:, 384:448], kview[:, 1024:1536:8],
                                        kview[:, 1024:1536:8], op=OP.mult)
                        sq_k = sv.inc(V.tensor_tensor(
                            sqb[:, 448:512], kview[:, 1536:2048:8],
                            kview[:, 1536:2048:8], op=OP.mult))
                        pend_nb[0] = (oc, sq_k)
                    flush_nb()
                    sv.wait(S)
                    ss.inc(S.activation(nrm[0:2, 16:24], nrm[0:2, 0:8], AF.Sqrt))
                    ss.wait(S)
                    ssq2 = ss.inc(S.activation(nrm[0:2, 0:8], nrm[0:2, 8:16], AF.Sqrt))
                    ss.wait(V, ssq2)
                    sv.inc(V.tensor_tensor(nrm[0:2, 8:16], nrm[0:2, 16:24], nrm[0:2, 0:8],
                                           op=OP.mult))
                    sv.wait(V)
                    pv_ = sv.inc(V.tensor_copy(prodb[0:2, 0:8], nrm[0:2, 8:16]))
                    sv.wait(TE, pv_)
                    tb = st.inc(TE.matmul(pnb2[:], ones2[0:2, 0:1], prodb[0:2, 0:8],
                                          start=True, stop=True))
                    st.wait(V, tb)
                    ngv = sv.inc(V.tensor_scalar(nrm[0:1, 0:8], pnb2[:], -1.0, None, op0=OP.mult))
                    sv.wait(TE, ngv)
                    tb2 = st.inc(TE.matmul(pbc[:], ones1f[:], nrm[0:1, 0:8],
                                           start=True, stop=True))
                    st.wait(V, tb2)
                    sv.inc(V.tensor_copy(negM[:, 0:8], pbc[:]))
                bar()

                _ph = ExitStack()
                ptil0 = _ph.enter_context(nc.sbuf_tensor(_un("ptil0"), [128, 1024], BF16))
                ptil1 = _ph.enter_context(nc.sbuf_tensor(_un("ptil1"), [128, 1024], BF16))
                ptil2 = _ph.enter_context(nc.sbuf_tensor(_un("ptil2"), [128, 1024], BF16))
                tmpB0 = _ph.enter_context(nc.sbuf_tensor(_un("tmpB0"), [64, T], BF16))
                tmpB1 = _ph.enter_context(nc.sbuf_tensor(_un("tmpB1"), [64, T], BF16))
                invrA = _ph.enter_context(nc.sbuf_tensor(_un("invrA"), [1, T], F32))
                invrB = _ph.enter_context(nc.sbuf_tensor(_un("invrB"), [1, T], F32))
                lbufA = _ph.enter_context(nc.sbuf_tensor(_un("lbufA"), [1, T], F32))
                lbufB = _ph.enter_context(nc.sbuf_tensor(_un("lbufB"), [1, T], F32))
                invbA2 = _ph.enter_context(nc.sbuf_tensor(_un("invbA2"), [64, T], BF16))
                invbB2 = _ph.enter_context(nc.sbuf_tensor(_un("invbB2"), [64, T], BF16))
                sc0 = _ph.enter_context(nc.psum_tensor(_un("sc0"), [128, 1024], F32))
                sc1 = _ph.enter_context(nc.psum_tensor(_un("sc1"), [128, 1024], F32))
                accA = _ph.enter_context(nc.psum_tensor(_un("accA"), [65, 512], F32))
                accA2 = _ph.enter_context(nc.psum_tensor(_un("accA2"), [65, 512], F32))
                accB = _ph.enter_context(nc.psum_tensor(_un("accB"), [65, 512], F32))
                pib = _ph.enter_context(nc.psum_tensor(_un("pib"), [128, 512], F32))
                with _ph:
                    ptil = [ptil0, ptil1, ptil2]
                    tmpB = [tmpB0, tmpB1]
                    sc = [sc0, sc1]
                    expv = [0, 0, 0]
                    avv3 = [0, 0, 0]
                    epiv = 0
                    epivA = [0, 0]
                    epivB = 0
                    bcv = 0
                    ibcp = 0
                    tailv = 0
                    erv_ = 0
                    dmaBv = [0, 0]
                    smmv = {}
                    NSTEP = 4 * NKT
                    pend = [None]

                    def qkd(p):
                        return (qres[:, p * T:(p + 1) * T],
                                kres[:, p * T:(p + 1) * T],
                                vres[:, p * NKT * 130:(p + 1) * NKT * 130]
                                .rearrange("p (t n) -> p t n", n=130))

                    def issue_smm(g):
                        p, i = divmod(g, NSTEP)
                        qc, kt = divmod(i, NKT)
                        sl = g % 2
                        qc_, kc_, _ = qkd(p)
                        TE.matmul(sc[sl][:, 0:512], kc_[0:64, kt * 128:(kt + 1) * 128],
                                  qc_[0:64, qc * 512:(qc + 1) * 512],
                                  start=True, stop=True, tile_position=(0, 0))
                        smmv[g] = st.inc(TE.matmul(
                            sc[sl][:, 512:1024], kc_[64:128, kt * 128:(kt + 1) * 128],
                            qc_[64:128, qc * 512:(qc + 1) * 512],
                            start=True, stop=True, tile_position=(64, 0)))

                    def flush_tailA(r):
                        nonlocal erv_
                        if r == 0:
                            st.wait(V, bcv)
                            sv.wait(V)
                        V.reciprocal(invrA[0:1, r * 512:(r + 1) * 512],
                                     lbufA[0:1, r * 512:(r + 1) * 512])
                        erv_ = sv.inc(V.reciprocal(
                            invrB[0:1, r * 512:(r + 1) * 512],
                            lbufB[0:1, r * 512:(r + 1) * 512]))

                    def flush_tailB():
                        nonlocal bcv, ibcp, tailv
                        ep, erv = pend[0]
                        pend[0] = None
                        sv.wait(TE, erv_)
                        for r in range(4):
                            sv.wait(TE, ibcp)
                            TE.matmul(pib[0:64, :], ones1f[0:1, 0:64],
                                      invrA[0:1, r * 512:(r + 1) * 512],
                                      start=True, stop=True, tile_position=(0, 0))
                            bcv = st.inc(TE.matmul(
                                pib[64:128, :], ones1f[0:1, 0:64],
                                invrB[0:1, r * 512:(r + 1) * 512],
                                start=True, stop=True, tile_position=(0, 64)))
                            st.wait(V, bcv)
                            V.tensor_copy(invbA2[:, r * 512:(r + 1) * 512],
                                          pib[0:64, :])
                            ibcp = sv.inc(V.tensor_copy(
                                invbB2[:, r * 512:(r + 1) * 512], pib[64:128, :]))
                        sv.wait(V)
                        V.tensor_tensor(
                            attA[0:64, ep * T:(ep + 1) * T],
                            attA[0:64, ep * T:(ep + 1) * T],
                            invbA2[:, :], op=OP.mult)
                        tailv = sv.inc(V.tensor_tensor(
                            tmpB[ep % 2][:, :], tmpB[ep % 2][:, :],
                            invbB2[:, :], op=OP.mult))
                        sv.wait(SY, tailv)
                        dmaBv[ep % 2] = dtb[ep % 2].inc(SY.dma_start(
                            attA[64:128, ep * T:(ep + 1) * T], tmpB[ep % 2][:]), 16)

                    issue_smm(0)
                    issue_smm(1)
                    for g in range(NPAIR * NSTEP):
                        p, i = divmod(g, NSTEP)
                        qc, kt = divmod(i, NKT)
                        sl = g % 2
                        if pend[0] is not None and i in (2, 10, 18, 26):
                            flush_tailA((2, 10, 18, 26).index(i))
                        if i == 2 * NKT + 4 and pend[0] is not None:
                            flush_tailB()
                        _, _, dc4 = qkd(p)
                        pl = g % 3
                        st.wait(S, smmv[g])
                        st.wait(S, avv3[pl])
                        expv[pl] = ss.inc(S.activation(
                            ptil[pl][:], sc[sl][:], AF.Exp,
                            bias=negM[:, p:p + 1], scale=1.0))
                        ss.wait(TE, expv[pl])
                        if g + 2 < NPAIR * NSTEP:
                            issue_smm(g + 2)
                        accAd = accA if qc % 2 == 0 else accA2
                        if kt == 0:
                            sv.wait(TE, epivA[qc % 2])
                        TE.matmul(accAd[:], dc4[:, kt, 0:65], ptil[pl][:, 0:512],
                                  start=(kt == 0), stop=(kt == NKT - 1))
                        if kt == 0:
                            sv.wait(TE, epivB)
                        avv3[pl] = st.inc(TE.matmul(
                            accB[:], dc4[:, kt, 65:130], ptil[pl][:, 512:1024],
                            start=(kt == 0), stop=(kt == NKT - 1)))
                        if kt == NKT - 1:
                            st.wait(V, avv3[pl])
                            sv.wait(V)
                            if qc == 0 and p >= 2:
                                dtb[p % 2].wait(V, dmaBv[p % 2])
                            V.tensor_copy(lbufB[0:1, qc * 512:(qc + 1) * 512],
                                          accB[64:65, :])
                            epivB = sv.inc(V.tensor_copy(
                                tmpB[p % 2][:, qc * 512:(qc + 1) * 512],
                                accB[0:64, :]))
                            V.tensor_copy(lbufA[0:1, qc * 512:(qc + 1) * 512],
                                          accAd[64:65, :])
                            epivA[qc % 2] = epiv = sv.inc(V.tensor_copy(
                                attA[0:64, p * T + qc * 512:p * T + (qc + 1) * 512],
                                accAd[0:64, :]))
                            if qc == 3:
                                pend[0] = (p, epiv)
                    for r_ in range(4):
                        flush_tailA(r_)
                    flush_tailB()
                    dtb[0].wait(SY)
                    dtb[1].wait(SY)
                bar()

                _ph = ExitStack()
                xr = [_ph.enter_context(nc.sbuf_tensor(_un("xrb"), [128, H], F32))
                      for _ in range(4)]
                ln3t = _ph.enter_context(nc.sbuf_tensor(_un("ln3t"), [128, H], F32))
                x2t = [_ph.enter_context(nc.sbuf_tensor(_un("x2t"), [128, H], F32))
                       for _ in range(4)]
                ln2t = _ph.enter_context(nc.sbuf_tensor(_un("ln2t"), [128, H], F32))
                qx16 = _ph.enter_context(nc.sbuf_tensor(_un("qx16"), [128, H], F16))
                xqtm = _ph.enter_context(nc.sbuf_tensor(_un("xqtm"), [128, H], BF16))
                xqs0 = _ph.enter_context(nc.sbuf_tensor(_un("xqs0"), [128, 512], BF16))
                xqs1 = _ph.enter_context(nc.sbuf_tensor(_un("xqs1"), [128, 512], BF16))
                rowA = _ph.enter_context(nc.sbuf_tensor(_un("rowA"), [1, T], F32))
                pw0 = _ph.enter_context(nc.psum_tensor(_un("pw0"), [128, 1024], F32))
                pw1 = _ph.enter_context(nc.psum_tensor(_un("pw1"), [128, 1024], F32))
                pw2 = _ph.enter_context(nc.psum_tensor(_un("pw2"), [128, 1024], F32))
                pw3 = _ph.enter_context(nc.psum_tensor(_un("pw3"), [128, 1024], F32))
                with _ph:
                    xqs = [xqs0, xqs1]
                    pw = [pw0, pw1, pw2, pw3]
                    wo_use = [0, 0]
                    evacv = [0] * 4
                    xq_dma = [0, 0]
                    ln3use_s = 0
                    ln2use_s = 0
                    xqtmuse_s = 0
                    x2dma = [0] * 4
                    x2rds = [0] * 4
                    xruse = [0] * 4
                    for g in range(8):
                        gp = g % 2
                        pwp = [pw[gp * 2], pw[gp * 2 + 1]]
                        for kc in range(KC):
                            if kc == 0:
                                for i in range(2):
                                    ss.wait(TE, evacv[gp * 2 + i])
                            for i in range(2):
                                ti = g * 2 + i
                                for hh in range(2):
                                    _ins = TE.matmul(pwp[i][:, hh * 512:(hh + 1) * 512],
                                              A3[:, kc, ti * 128:(ti + 1) * 128],
                                              wo3[:, kc * H + hh * 512:
                                                  kc * H + (hh + 1) * 512],
                                              start=(kc == 0), stop=False)
                            st.inc(_ins)
                        for i in range(2):
                            for hh in range(2):
                                _ins = TE.matmul(pwp[i][:, hh * 512:(hh + 1) * 512],
                                          ones_row[0:1, 0:128],
                                          bo_row[0:1, hh * 512:(hh + 1) * 512],
                                          start=False, stop=True)
                        womm = st.inc(_ins)
                        xvv = [0] * 2
                        v2s = [0] * 2
                        for i in range(2):
                            ti = g * 2 + i
                            io = ti % 4
                            sv.wait(SY, xruse[io])
                            xvv[i] = dxr[io].inc(SY.dma_start(
                                xr[io][:], x_h[s, ti * 128:(ti + 1) * 128, :]), 16)
                            st.wait(V, womm)
                            v2s[i] = ln_stats(pwp[i][:], 16 + io * 4)
                        fin3 = [ln_fin(16 + ((g * 2 + i) % 4) * 4, v2s[i])
                                for i in range(2)]
                        for i in range(2):
                            ti = g * 2 + i
                            io = ti % 4
                            r3, mb3, mbv3 = fin3[i]
                            sv.wait(S, mbv3)
                            sv.wait(S, ln3use_s)
                            l3v = ss.inc(S.activation(ln3t[:], pwp[i][:],
                                                      AF.Identity,
                                                      scale=r3, bias=mb3))
                            dxr[io].wait(V, xvv[i])
                            ss.wait(V, l3v)
                            dx2[io % 2].wait(V)
                            ss.wait(V, x2rds[io])
                            addv = sv.inc(V.tensor_tensor(
                                x2t[io][:], xr[io][:], ln3t[:], op=OP.add))
                            ln3use_s = addv
                            xruse[io] = addv
                            sv.wait(SY, addv)
                            x2dma[io] = dx2[io % 2].inc(SY.dma_start(
                                x2_sp[s, ti * 128:(ti + 1) * 128, :], x2t[io][:]), 16)
                        sv.wait(V)
                        for i in range(2):
                            io = (g * 2 + i) % 4
                            v2s[i] = ln_stats(x2t[io][:], 32 + io * 4)
                        fin2 = [ln_fin(32 + ((g * 2 + i) % 4) * 4, v2s[i])
                                for i in range(2)]
                        for i in range(2):
                            ti = g * 2 + i
                            io = ti % 4
                            qo = 48 + io * 4
                            r2, mb2, mbv2 = fin2[i]
                            sv.wait(S, mbv2)
                            sv.wait(S, ln2use_s)
                            l2v = ss.inc(S.activation(ln2t[:], x2t[io][:],
                                                      AF.Identity,
                                                      scale=r2, bias=mb2))
                            x2rds[io] = l2v
                            ss.wait(V, l2v)
                            sv.inc(V.tensor_reduce(
                                stat[:, qo:qo + 1], ln2t[:], axis=AX.X,
                                op=OP.max, apply_absolute_value=True))
                            sv.wait(V)
                            sv.inc(V.tensor_scalar(stat[:, qo + 1:qo + 2],
                                                   stat[:, qo:qo + 1], 1e-5,
                                                   None, op0=OP.max))
                            sv.wait(V)
                            sv.inc(V.reciprocal(stat[:, qo + 2:qo + 3],
                                                stat[:, qo + 1:qo + 2]))
                            sv.wait(V)
                            sv.inc(V.tensor_scalar(
                                stat[:, qo + 3:qo + 4], stat[:, qo + 2:qo + 3],
                                127.0, None, op0=OP.mult))
                            sv.inc(V.tensor_scalar(c1sb[:, ti:ti + 1],
                                                   stat[:, qo + 1:qo + 2],
                                                   wsc[:, 1:2], None, op0=OP.mult))
                            sv.wait(V)
                            st.wait(V, xqtmuse_s)
                            sv.inc(V.tensor_scalar(qx16[:], ln2t[:],
                                                   stat[:, qo + 3:qo + 4], RM,
                                                   op0=OP.mult, op1=OP.add))
                            sv.wait(V)
                            xqv = sv.inc(V.tensor_scalar(
                                xqtm[:], qx16[:], -RM, None,
                                op0=OP.add))
                            ln2use_s = xqv
                            sv.wait(TE, xqv)
                            for half in range(2):
                                dqs[half].wait(TE, xq_dma[half])
                                for k4 in range(4):
                                    kc = half * 4 + k4
                                    _ins = TE.matmul(pwp[i][:, half * 512 + k4 * 128:
                                                    half * 512 + (k4 + 1) * 128],
                                              xqtm[:, kc * 128:(kc + 1) * 128],
                                              ident[:], start=True, stop=True)
                                thv = xqtmuse_s = st.inc(_ins)
                                st.wait(S, thv)
                                cpv = ss.inc(S.copy(
                                    xqs[half][:], pwp[i][:, half * 512:(half + 1) * 512]))
                                ss.wait(SY, cpv)
                                xq_dma[half] = dqs[half].inc(SY.dma_start(
                                    xqd_sp[s][half * 4:half * 4 + 4, :,
                                              ti * 128:(ti + 1) * 128].rearrange(
                                                  "c p n -> p c n"),
                                    xqs[half][:, :].rearrange("p (c n) -> p c n", n=128)), 16)
                            evacv[gp * 2 + i] = ss.v
                    sv.wait(TE)
                    dqs[0].wait(TE, xq_dma[0])
                    dqs[1].wait(TE, xq_dma[1])
                    for ti in range(NT128):
                        dst = pw0 if ti < 8 else pw1
                        _ins = TE.matmul(dst[0:1, (ti % 8) * 128:(ti % 8 + 1) * 128],
                                  c1sb[:, ti:ti + 1], identf[:], start=True, stop=True)
                    tcv2 = st.inc(_ins)
                    st.wait(V, tcv2)
                    V.tensor_copy(rowA[0:1, 0:1024], pw0[0:1, :])
                    rv2 = sv.inc(V.tensor_copy(rowA[0:1, 1024:2048], pw1[0:1, :]))
                    sv.wait(SY, rv2)
                    dcr.inc(SY.dma_start(c1_sp[s][None, :], rowA[:]), 16)
                    for s_ in (dcr, dx2[0], dx2[1], dqs[0], dqs[1]):
                        s_.wait(SY)
                bar()

        w3_es.close()
        bar()
        _ph = ExitStack()
        w1q = _ph.enter_context(nc.sbuf_tensor(_un("w1q"), [128, KC * DH], BF16))
        xqd = _ph.enter_context(nc.sbuf_tensor(_un("xqd"), [128, KC * T], BF16))
        diag = _ph.enter_context(nc.sbuf_tensor(_un("diag"), [128, 3 * HC * 128], BF16))
        c1f = _ph.enter_context(nc.sbuf_tensor(_un("c1f"), [128, T], F32))
        crow = _ph.enter_context(nc.sbuf_tensor(_un("crow"), [1, T], F32))
        hsc2 = [_ph.enter_context(nc.sbuf_tensor(_un("hsc2"), [128, T + 2], BF16))
                for _ in range(2)]
        hsb = _ph.enter_context(nc.sbuf_tensor(_un("hsb"), [128, T], BF16))
        macc = _ph.enter_context(nc.sbuf_tensor(_un("macc"), [128, T], BF16))
        mincc = _ph.enter_context(nc.sbuf_tensor(_un("mincc"), [128, T], BF16))
        s2row = _ph.enter_context(nc.sbuf_tensor(_un("s2row"), [1, T], F32))
        pwA = _ph.enter_context(nc.psum_tensor(_un("pwA"), [128, 512], F32))
        pwB = _ph.enter_context(nc.psum_tensor(_un("pwB"), [128, 512], F32))
        pcA = _ph.enter_context(nc.psum_tensor(_un("pcA"), [128, 512], F32))
        pcB = _ph.enter_context(nc.psum_tensor(_un("pcB"), [128, 512], F32))
        pam = _ph.enter_context(nc.psum_tensor(_un("pam"), [128, 256], F32))
        prow = _ph.enter_context(nc.psum_tensor(_un("prow"), [1, 512], F32))
        pbc1 = _ph.enter_context(nc.psum_tensor(_un("pbc1"), [128, 512], F32))
        with _ph:
            pwx = [pwA, pwB]
            pcx = [pcA, pcB]
            w1ld = 0
            for kc in range(KC):
                w1ld = drl.inc(G.dma_start(w1q[:, kc * DH:(kc + 1) * DH], w1qt_sp[kc]), 16)
            dgv = 0
            for c in range(HC):
                for j3 in range(3):
                    dgv = sv.inc(V.tensor_scalar(
                        diag[:, (c * 3 + j3) * 128:(c * 3 + j3 + 1) * 128],
                        ident[:], convw_s[:, c * 3 + j3:c * 3 + j3 + 1], None, op0=OP.mult))
            V.memset(hsc2[0][:, 0:1], 0.0)
            V.memset(hsc2[0][:, T + 1:T + 2], 0.0)
            V.memset(hsc2[1][:, 0:1], 0.0)
            hscz = sv.inc(V.memset(hsc2[1][:, T + 1:T + 2], 0.0))
            spillv = 0
            mm_prev = 0
            hscuse = [0, 0]
            for s in range(NSEQ):
                st.wait(G)
                sv.wait(G)
                st.wait(V)
                for kc in range(KC):
                    drl.inc(G.dma_start(xqd[:, kc * T:(kc + 1) * T], xqd_sp[s, kc]), 16)
                drl.inc(G.dma_start(crow[:], c1_sp[s][None, :]), 16)
                drl.wait(TE)
                sv.wait(S)
                c1bv = 0
                for jbc in range(NT512):
                    if jbc > 0:
                        ss.wait(TE, c1bv)
                    tcb = st.inc(TE.matmul(pbc1[:], ones1f[:],
                                           crow[0:1, jbc * 512:(jbc + 1) * 512],
                                           start=True, stop=True))
                    st.wait(S, tcb)
                    c1bv = ss.inc(S.copy(c1f[:, jbc * 512:(jbc + 1) * 512], pbc1[:]))
                V.memset(macc[:], -3e38)
                mmi = sv.inc(V.memset(mincc[:], 3e38))
                drl.wait(TE)
                pwv = [0, 0]
                silv = [0, 0]
                hval = {}

                def w1_block(c):
                    nonlocal mmi
                    hs_ = hsc2[c % 2]
                    for j in range(NT512):
                        slw = j % 2
                        sv.wait(TE, pwv[slw])
                        for kc in range(KC):
                            _ins = TE.matmul(pwx[slw][:],
                                      w1q[:, kc * DH + c * 128:kc * DH + (c + 1) * 128],
                                      xqd[:, kc * T + j * 512:kc * T + (j + 1) * 512],
                                      start=(kc == 0), stop=(kc == KC - 1))
                        wmm = st.inc(_ins)
                        st.wait(V, wmm)
                        ss.wait(V, c1bv)
                        sv.wait(V, hscz)
                        st.wait(V, hscuse[c % 2])
                        pwv[slw] = sv.inc(V.tensor_tensor(
                            hs_[:, 1 + j * 512:1 + (j + 1) * 512], pwx[slw][:],
                            c1f[:, j * 512:(j + 1) * 512], op=OP.mult))
                    hval[c] = sv.v

                def conv_block(c):
                    nonlocal mmi, mm_prev, spillv
                    hs_ = hsc2[c % 2]
                    for j in range(NT512):
                        slc = j % 2
                        sv.wait(TE, hval[c])
                        if c == 0:
                            sv.wait(TE, dgv)
                        ss.wait(TE, silv[slc])
                        for tap in range(3):
                            _ins = TE.matmul(pcx[slc][:],
                                      diag[:, (c * 3 + tap) * 128:(c * 3 + tap + 1) * 128],
                                      hs_[:, tap + j * 512:tap + j * 512 + 512],
                                      start=(tap == 0), stop=(tap == 2))
                        cmm = st.inc(_ins)
                        hscuse[c % 2] = cmm
                        st.wait(S, cmm)
                        dhs.wait(S, spillv)
                        sv.wait(S, mm_prev)
                        silv[slc] = ss.inc(S.activation(hsb[:, j * 512:(j + 1) * 512],
                                                        pcx[slc][:], AF.Silu,
                                                        bias=convb_s[:, c:c + 1], scale=1.0))
                    ss.wait(V, silv[(NT512 - 1) % 2])
                    sv.wait(V, mmi)
                    V.tensor_tensor(macc[:], macc[:], hsb[:], op=OP.max)
                    mmi = mm_prev = sv.inc(V.tensor_tensor(mincc[:], mincc[:], hsb[:], op=OP.min))
                    ss.wait(SY, silv[(NT512 - 1) % 2])
                    spillv = dhs.inc(SY.dma_start(hs_sp[s, c], hsb[:]), 16)

                w1_block(0)
                for c in range(HC):
                    if c + 1 < HC:
                        w1_block(c + 1)
                    conv_block(c)
                sv.wait(TE, mmi)
                pamv = 0
                for ti in range(NT128):
                    sv.wait(TE, pamv)
                    TE.matmul(pam[:, 0:128], macc[:, ti * 128:(ti + 1) * 128], ident[:],
                              start=True, stop=True)
                    tav = st.inc(TE.matmul(pam[:, 128:256], mincc[:, ti * 128:(ti + 1) * 128],
                                           ident[:], start=True, stop=True))
                    st.wait(V, tav)
                    V.tensor_reduce(stat[:, 40:41], pam[:, 0:128], axis=AX.X, op=OP.max)
                    pamv = sv.inc(V.tensor_reduce(stat[:, 41:42], pam[:, 128:256], axis=AX.X,
                                                  op=OP.min, negate=True))
                    sv.wait(V)
                    sv.inc(V.tensor_tensor(stat[:, 42:43], stat[:, 40:41], stat[:, 41:42],
                                           op=OP.max))
                    sv.wait(V)
                    sv.inc(V.tensor_scalar(stat[:, 43:44], stat[:, 42:43], 1e-5, None,
                                           op0=OP.max))
                    sv.wait(V)
                    sv.inc(V.reciprocal(stat[:, 44:45], stat[:, 43:44]))
                    sv.wait(V)
                    sv.inc(V.tensor_scalar(s2s[:, ti:ti + 1], stat[:, 44:45], 127.0,
                                           None, op0=OP.mult))
                sv.wait(TE)
                dsw.wait(V)
                srv = 0
                for tg in range(4):
                    if tg > 0:
                        sv.wait(TE, srv)
                    for t4 in range(4):
                        ti = tg * 4 + t4
                        _ins = TE.matmul(prow[0:1, t4 * 128:(t4 + 1) * 128], s2s[:, ti:ti + 1],
                                  identf[:], start=True, stop=True)
                    trv2 = st.inc(_ins)
                    st.wait(V, trv2)
                    srv = sv.inc(V.tensor_copy(s2row[0:1, tg * 512:(tg + 1) * 512], prow[0:1, :]))
                sv.wait(SY, srv)
                dsw.inc(SY.dma_start(s2_sp[s][None, :], s2row[:]), 16)
            dhs.wait(SY)
            dsw.wait(SY)
            drl.wait(SY)
        bar()

        _ph = ExitStack()
        w2q = _ph.enter_context(nc.sbuf_tensor(_un("w2q"), [128, HC * H], BF16))
        ring0 = _ph.enter_context(nc.sbuf_tensor(_un("ring0"), [128, HC * 512], BF16))
        ring1 = _ph.enter_context(nc.sbuf_tensor(_un("ring1"), [128, HC * 512], BF16))
        srow0 = _ph.enter_context(nc.sbuf_tensor(_un("srow0"), [1, T], F32))
        srow1 = _ph.enter_context(nc.sbuf_tensor(_un("srow1"), [1, T], F32))
        s2b0 = _ph.enter_context(nc.sbuf_tensor(_un("s2b0"), [128, 512], BF16))
        s2b1 = _ph.enter_context(nc.sbuf_tensor(_un("s2b1"), [128, 512], BF16))
        m216a = _ph.enter_context(nc.sbuf_tensor(_un("m216a"), [128, 512], F16))
        m216b = _ph.enter_context(nc.sbuf_tensor(_un("m216b"), [128, 512], F16))
        q216a = _ph.enter_context(nc.sbuf_tensor(_un("q216a"), [128, 512], F16))
        q216b = _ph.enter_context(nc.sbuf_tensor(_un("q216b"), [128, 512], F16))
        ln4t = _ph.enter_context(nc.sbuf_tensor(_un("ln4t"), [128, H], F32))
        x2r0 = _ph.enter_context(nc.sbuf_tensor(_un("x2r0"), [128, H], F32))
        x2r1 = _ph.enter_context(nc.sbuf_tensor(_un("x2r1"), [128, H], F32))
        outt0 = _ph.enter_context(nc.sbuf_tensor(_un("outt0"), [128, H], F32))
        outt1 = _ph.enter_context(nc.sbuf_tensor(_un("outt1"), [128, H], F32))
        p2a = _ph.enter_context(nc.psum_tensor(_un("p2a"), [128, 1024], F32))
        p2b = _ph.enter_context(nc.psum_tensor(_un("p2b"), [128, 1024], F32))
        pbs = _ph.enter_context(nc.psum_tensor(_un("pbs"), [128, 512], F32))
        with _ph:
            ring = [ring0, ring1]
            s2b = [s2b0, s2b1]
            x2r = [x2r0, x2r1]
            outt = [outt0, outt1]
            p2 = [p2a, p2b]
            for c in range(HC):
                drl.inc(G.dma_start(w2q[:, c * H:(c + 1) * H], w2qt_sp[c]), 16)
            ringuse = [0, 0]
            s2use = [0, 0]
            addv2 = [0, 0]
            rsubv = [0, 0]
            outdma = [0, 0]
            x2v_use = [0, 0]
            evv_ = [0, 0]
            pbsuse = 0
            srow = [srow0, srow1]
            srldv = [0, 0]
            sj = [(s, j) for s in range(NSEQ) for j in range(NT512)]
            NSJ = len(sj)
            ringldv = {}
            rndall_v = {}

            def load_ring(idx):
                s_, j_ = sj[idx]
                slr_ = idx % 2
                if j_ == 0:
                    drl.wait(G)
                    srldv[s_] = drl.inc(
                        G.dma_start(srow[s_][:], s2_sp[s_][None, :]), 16)
                st.wait(SY, ringuse[slr_])
                ringldv[idx] = dr2[slr_].inc(SY.dma_start(
                    ring[slr_][:, :].rearrange("p (c n) -> p c n", n=512),
                    hs_sp[s_][:, :, j_ * 512:(j_ + 1) * 512].rearrange(
                        "c p n -> p c n")), 16)

            def quant(idx):
                nonlocal pbsuse
                s_, j_ = sj[idx]
                slr_ = idx % 2
                drl.wait(TE, srldv[s_])
                ss.wait(TE, pbsuse)
                tb4 = st.inc(TE.matmul(pbs[:], ones1f[:],
                                       srow[s_][0:1, j_ * 512:(j_ + 1) * 512],
                                       start=True, stop=True))
                st.wait(S, tb4)
                sv.wait(S, s2use[slr_])
                s2bv = pbsuse = ss.inc(S.copy(s2b[slr_][:], pbs[:]))
                ss.wait(V, s2bv)
                dr2[slr_].wait(V, ringldv[idx])
                q216 = [q216a, q216b]
                m216 = [m216a, m216b]
                mvs = {}

                def qmult(c_):
                    slt_ = c_ % 2
                    ss.wait(V, addv2[slt_])
                    mvs[c_] = sv.inc(V.tensor_tensor(
                        m216[slt_][:], ring[slr_][:, c_ * 512:(c_ + 1) * 512],
                        s2b[slr_][:], op=OP.mult))

                qmult(0)
                for c in range(HC):
                    slt = c % 2
                    if c + 1 < HC:
                        qmult(c + 1)
                    sv.wait(S, mvs[c])
                    sv.wait(S, rsubv[slt])
                    addv2[slt] = ss.inc(S.activation(
                        q216[slt][:], m216[slt][:], AF.Identity,
                        bias=posRM[:, 0:1], scale=1.0))
                    ss.wait(V, addv2[slt])
                    rsubv[slt] = sv.inc(V.tensor_scalar(
                        ring[slr_][:, c * 512:(c + 1) * 512], q216[slt][:],
                        -RM, None, op0=OP.add))
                s2use[slr_] = sv.v
                rndall_v[idx] = sv.v

            load_ring(0)
            quant(0)
            for idx in range(NSJ):
                s, j = sj[idx]
                slr = idx % 2
                if idx + 1 < NSJ:
                    load_ring(idx + 1)
                    quant(idx + 1)
                rndall = rndall_v[idx]
                for i2 in range(2):
                    for half in range(2):
                            ti = j * 4 + i2 * 2 + half
                            pp = p2[half]
                            sv.wait(TE, rndall)
                            sv.wait(TE, evv_[half])
                            for c in range(HC):
                                base = c * 512 + (i2 * 2 + half) * 128
                                for hh in range(2):
                                    _ins = TE.matmul(pp[:, hh * 512:(hh + 1) * 512],
                                              ring[slr][:, base:base + 128],
                                              w2q[:, c * H + hh * 512:c * H + (hh + 1) * 512],
                                              start=(c == 0), stop=(c == HC - 1))
                            mmv_ = st.inc(_ins)
                            ringuse[slr] = mmv_
                            sv.wait(SY, x2v_use[half])
                            xv = dxa[half].inc(SY.dma_start(
                                x2r[half][:], x2_sp[s, ti * 128:(ti + 1) * 128, :]), 16)
                            st.wait(V, mmv_)
                            r4, mb4, mbv4 = ln_chain(pp[:], 48 + half * 4)
                            sv.wait(S, mbv4)
                            l4v = ss.inc(S.activation(ln4t[:], pp[:], AF.Identity,
                                                      scale=r4, bias=mb4))
                            dxa[half].wait(V, xv)
                            ss.wait(V, l4v)
                            dou[half].wait(V, outdma[half])
                            ov = sv.inc(V.tensor_tensor(outt[half][:], x2r[half][:], ln4t[:],
                                                        op=OP.add))
                            evv_[half] = ov
                            x2v_use[half] = ov
                            sv.wait(SY, ov)
                            outdma[half] = dou[half].inc(SY.dma_start(
                                out_h[s, ti * 128:(ti + 1) * 128, :], outt[half][:]), 16)
            dou[0].wait(SY)
            dou[1].wait(SY)
        bar()

    return nc, dbg_outs


def host_inputs(inputs):
    x = np.ascontiguousarray(inputs["L0_attn_hidden_states"])
    common = dict(
        wq=np.ascontiguousarray(inputs["L0_attn_wq"]),
        wk=np.ascontiguousarray(inputs["L0_attn_wk"]),
        wv=np.ascontiguousarray(inputs["L0_attn_wv"]),
        wo=np.ascontiguousarray(inputs["L0_attn_wo"]),
        bq=np.ascontiguousarray(np.asarray(inputs["L0_attn_bq"]).reshape(OC, 128).T),
        bv=np.ascontiguousarray(np.asarray(inputs["L0_attn_bv"]).reshape(OC, 128).T),
        bo=np.ascontiguousarray(inputs["L0_attn_bo"]),
        cosT=np.ascontiguousarray(np.asarray(inputs["L0_attn_cos"]).T),
        sinT=np.ascontiguousarray(np.asarray(inputs["L0_attn_sin"]).T),
        w1t=np.ascontiguousarray(np.asarray(inputs["L0_ffn_w1"]).T),
        cw=np.ascontiguousarray(np.asarray(inputs["L0_ffn_conv_w"]).reshape(HC, 128, 3)
                                .transpose(1, 0, 2).reshape(128, HC * 3)),
        cb=np.ascontiguousarray(np.asarray(inputs["L0_ffn_conv_b"]).reshape(HC, 128).T),
        w2t=np.ascontiguousarray(np.asarray(inputs["L0_ffn_w2"]).T),
    )
    maps = []
    for c in range(8):
        m = dict(common)
        m["x"] = np.ascontiguousarray(x[c * NSEQ:(c + 1) * NSEQ])
        maps.append(m)
    return maps


def kernel(**inputs):
    maps = host_inputs(inputs)
    nc, _ = build_nc()
    res = run_bass_kernel_spmd(nc, maps, core_ids=list(range(8)))
    out = np.concatenate([np.asarray(r["out"]) for r in res.results], axis=0)
    return out.astype(np.float32)
